# version 09
# baseline (speedup 1.0000x reference)
import contextlib
import numpy as np
import ml_dtypes
import concourse.bass as bass
import concourse.mybir as mybir
from concourse.bass_utils import run_bass_kernel_spmd
from concourse.bass_core_rust import IndirectOffsetOnAxis

F32 = mybir.dt.float32
BF16 = mybir.dt.bfloat16
I32 = mybir.dt.int32
U32 = mybir.dt.uint32
AF = mybir.ActivationFunctionType
ALU = mybir.AluOpType
AX = mybir.AxisListType
bfdt = ml_dtypes.bfloat16

L, D, NB, NT, TL = 2048, 1024, 2, 16, 128
NH, DH, DR = 8, 64, 32
NIH = 4
DFF, NE, RLO = 4096, 8, 16
ROW = 512 + 512 + 32
GSL = 16
NSW = 6
SCALE = (DH + DR) ** -0.5
NEG = -3.0e38
EPS = 1e-5
_PB = 196608


def _host_consts():
    th = 10000.0 ** (-np.arange(16, dtype=np.float64) / 16.0)

    def pack(ang):
        c = np.cos(ang).astype(np.float32)
        s = np.sin(ang).astype(np.float32)
        return (c.reshape(NT, TL, 16).transpose(1, 0, 2).reshape(TL, NT * 16).copy(),
                s.reshape(NT, TL, 16).transpose(1, 0, 2).reshape(TL, NT * 16).copy())
    CT, ST = pack(np.arange(L, dtype=np.float64)[:, None] * th[None, :])
    CT2, ST2 = pack((2.0 * np.arange(L, dtype=np.float64)[:, None] - 63.0) * th[None, :])
    ang_k = np.arange(96, dtype=np.float64)[:, None] * th[None, :]
    CK = np.repeat(np.cos(ang_k).astype(np.float32).reshape(1, 96 * 16), TL, axis=0).copy()
    SK = np.repeat(np.sin(ang_k).astype(np.float32).reshape(1, 96 * 16), TL, axis=0).copy()
    return CT, ST, CT2, ST2, CK, SK



def _mask_consts():
    p = np.arange(TL)[:, None]
    s = np.arange(TL)[None, :]
    K128 = (p - s >= 64).astype(np.float32)
    u = np.arange(TL)[None, :]
    K128 = (u <= p).astype(np.float32)
    NEGM = np.where(u <= p, 0.0, NEG).astype(np.float32)
    d = (p - s).astype(np.float64)
    K0 = (d >= 64).astype(np.float32)
    SURR = np.where((d >= 0) & (d <= 63), 1e30 + d * 1e24,
                    np.where(d < 0, -1e30 + d * 1e24, 0.0)).astype(np.float32)
    u2 = np.arange(256)[None, :]
    BANDM = np.where((u2 >= p + 65) & (u2 <= p + 128), 0.0, NEG).astype(np.float32)
    BANDM4 = np.tile(BANDM, (1, 4)).copy()
    IDENT = np.eye(TL, dtype=np.float32).astype(bfdt)
    return K128, NEGM, K0, SURR, BANDM4, IDENT


def build_nc(dbg=False):
    nc = bass.Bass("TRN2", target_bir_lowering=False)

    x_d = nc.dram_tensor("x", [NB, L, D], F32, kind="ExternalInput").ap()
    out_d = nc.dram_tensor("out", [NB, L, D], F32, kind="ExternalOutput").ap()
    win = {}
    for name, shape in [
        ("wdkv", [D, 256]), ("wuk", [256, 512]), ("wuv", [256, 512]),
        ("wdq", [D, 512]), ("wuq", [512, 512]), ("wqr", [512, 256]),
        ("wkr", [D, DR]), ("wout", [512, D]), ("iwq", [D, 256]), ("iwk", [D, 64]),
        ("iw", [TL, NIH]),
        ("wiT", [D, DFF]), ("woT", [DFF, D]), ("gwT", [D, TL]),
        ("AT", [D, NE * TL]), ("BT", [NE, RLO, DFF]),
        ("bi", [1, DFF]), ("bo", [1, D]), ("gb", [1, TL]),
        ("CT", [TL, NT * 16]), ("ST", [TL, NT * 16]),
        ("CT2", [TL, NT * 16]), ("ST2", [TL, NT * 16]),
        ("CK", [TL, 96 * 16]), ("SK", [TL, 96 * 16]),
        ("K128", [TL, TL]), ("NEGM", [TL, TL]),
        ("K0", [TL, TL]), ("SURR0", [TL, TL]), ("BANDM", [TL, 1024]),
    ]:
        win[name] = nc.dram_tensor(name, shape, F32, kind="ExternalInput").ap()
    ident_d = nc.dram_tensor("identw", [TL, TL], BF16, kind="ExternalInput").ap()
    kvtab_d = nc.dram_tensor("kvtab", [L, ROW], BF16, kind="Internal").ap()
    xnt_d = nc.dram_tensor("xnt", [NB, NT, TL, 8 * TL], BF16, kind="Internal").ap()
    part_d = nc.dram_tensor("part", [NB, L, D], F32, kind="Internal").ap()

    cur = [16448]

    def alloc(name, shape, dt):
        per = int(np.prod(shape[1:])) * mybir.dt.size(dt)
        per = (per + 31) // 32 * 32
        off = cur[0]
        cur[0] = off + per
        assert cur[0] <= _PB - 64, f"sbuf overflow at {name}"
        return nc.alloc_sbuf_tensor_at(name, shape, dt, offset=off)

    ident = alloc("ident", [TL, TL], BF16)
    ones_r = alloc("ones_r", [1, TL], BF16)
    cstc = alloc("cstc", [TL, 8], F32)
    spc = alloc("spc", [TL, 8], F32)
    spc2 = alloc("spc2", [TL, 8], F32)
    spc3 = alloc("spc3", [TL, 8], F32)

    a0 = cur[0]
    CT_s = alloc("CT_s", [TL, NT * 16], F32)
    ST_s = alloc("ST_s", [TL, NT * 16], F32)
    CT2_s = alloc("CT2_s", [TL, NT * 16], F32)
    ST2_s = alloc("ST2_s", [TL, NT * 16], F32)
    CK_s = alloc("CK_s", [TL, 96 * 16], BF16)
    SK_s = alloc("SK_s", [TL, 96 * 16], BF16)
    BANDM_s = alloc("BANDM_s", [TL, 1024], F32)
    iw_s = alloc("iw_s", [TL, NIH], F32)
    K128s = alloc("K128s", [TL, TL], F32)
    NEGMs = alloc("NEGMs", [TL, TL], F32)
    K0s = alloc("K0s", [TL, TL], F32)
    SURR0s = alloc("SURR0s", [TL, TL], F32)
    wA = {}
    for name, shape in [
        ("wdkv", [TL, 8, 256]), ("wuk", [TL, 2, 512]), ("wuv", [TL, 2, 512]),
        ("wdq", [TL, 8, 512]), ("wuq", [TL, 4, 512]), ("wqr", [TL, 4, 256]),
        ("wkr", [TL, 8, DR]), ("wout", [TL, 4, D]), ("iwq", [TL, 8, 256]),
        ("iwk", [TL, 8, 64]),
    ]:
        wA[name] = alloc("A_" + name, shape, BF16)
    kiT = alloc("kiT", [64, L], BF16)
    G = alloc("G", [TL, GSL, ROW], BF16)
    prods = alloc("prods", [TL, NH * GSL * DH], BF16)
    I_a = alloc("I_a", [TL, L], F32)
    I_b = alloc("I_b", [TL, L], F32)
    bscore = I_b
    kcb = alloc("kcb", [TL, 2, 512], BF16)
    VSb = alloc("VSb", [TL, 2, 512], BF16)
    kmub = alloc("kmub", [TL, 2, DR], BF16)
    rotb = alloc("rotb", [TL, TL], BF16)
    kcTb = alloc("kcTb", [TL, 4, 256], BF16)
    krTt = alloc("krTt", [96, 2, TL], BF16)
    qcT = alloc("qcT", [TL, 4, TL], BF16)
    qropeT = alloc("qropeT", [96, NH, TL], BF16)
    qropeb = alloc("qropeb", [TL, NH, TL], BF16)
    battn = alloc("battn", [TL, L], BF16)
    batT = alloc("batT", [TL, 2, TL], BF16)
    x_s = alloc("x_s", [TL, D], F32)
    xn_s = I_a[:, 0:D]
    xnbf = alloc("xnbf", [TL, D], BF16)
    xnT = alloc("xnT", [TL, 8, TL], BF16)
    tab = alloc("tab", [TL, ROW], BF16)
    ckvT = alloc("ckvT", [TL, 2, TL], BF16)
    cqT = alloc("cqT", [TL, 4, TL], BF16)
    qiT = alloc("qiT", [64, NIH, TL], BF16)
    qc = alloc("qc", [TL, 512], BF16)
    qr = alloc("qr", [TL, 256], F32)
    mu = alloc("mu", [TL, 256], F32)
    qt1 = alloc("qt1", [TL, 128], BF16)
    qt2 = alloc("qt2", [TL, 128], BF16)
    kt1 = alloc("kt1", [TL, GSL * 16], BF16)
    kt2 = alloc("kt2", [TL, GSL * 16], BF16)
    ktt = alloc("ktt", [TL, GSL * 16], BF16)
    rp = battn
    rp2 = alloc("rp2", [TL, NH * GSL * 16], BF16)
    Ssc = alloc("Ssc", [TL, NH * GSL], F32)
    Ss2 = alloc("Ss2", [TL, NH * GSL], F32)
    ebuf = alloc("ebuf", [TL, NH * GSL], BF16)
    mrun = alloc("mrun", [TL, NH], F32)
    mnew = alloc("mnew", [TL, NH], F32)
    alph = alloc("alph", [TL, NH], F32)
    dnm = alloc("dnm", [TL, NH], F32)
    rpart = alloc("rpart", [TL, NH], F32)
    Aacc = alloc("Aacc", [TL, 512], F32)
    avp = alloc("avp", [TL, 512], F32)
    attnbf = alloc("attnbf", [TL, 512], BF16)
    attnT = alloc("attnT", [TL, 4, TL], BF16)
    part_s = alloc("part_s", [TL, D], F32)
    idx96 = alloc("idx96", [TL, 96], U32)
    v8 = alloc("v8", [TL, 8], F32)
    col1 = alloc("col1", [TL, 1], F32)
    col2 = alloc("col2", [TL, 1], F32)
    w1t = alloc("w1t", [TL, 512], F32)
    assert cur[0] <= _PB - 64

    cur[0] = a0
    wiT_s = alloc("wiT_s", [TL, 8, DFF // 2], BF16)
    woT_s = alloc("woT_s", [TL, 16, D], BF16)
    BT_s = alloc("BT_s", [16, NE, DFF // 2], BF16)
    AT_s = alloc("AT_s", [TL, 8, NE * TL], BF16)
    gwT_s = alloc("gwT_s", [TL, 8, TL], BF16)
    bi_s = alloc("bi_s", [1, DFF // 2], BF16)
    bo_s = alloc("bo_s", [1, D], BF16)
    gbr = alloc("gbr", [1, TL], BF16)
    Hacc = alloc("Hacc", [TL, DFF // 2], F32)
    stg = Hacc
    base_s = alloc("base_s", [TL, DFF // 2], BF16)
    lorab = alloc("lorab", [TL, 1024], BF16)
    sadd = alloc("sadd", [TL, 1024], BF16)
    Hbf = alloc("Hbf", [TL, DFF // 2], BF16)
    HT = alloc("HT", [TL, 16, TL], BF16)
    xnT_b = alloc("xnT_b", [TL, 8, TL], BF16)
    partb = alloc("partb", [TL, D], F32)
    outb = alloc("outb", [TL, D], F32)
    la_s = alloc("la_s", [16, NE * TL], BF16)
    probs = alloc("probs", [TL, NE], F32)
    gates = alloc("gates", [TL, NE], F32)
    dcol = alloc("dcol", [TL, NE], F32)
    c1 = alloc("c1", [TL, 1], F32)
    c2 = alloc("c2", [TL, 1], F32)
    c3 = alloc("c3", [TL, 1], F32)
    gpk = alloc("gpk", [TL, 16], BF16)
    gsr = alloc("gsr", [1, TL], BF16)
    gates_all = alloc("gates_all", [TL, NB * NT * NE], F32)
    c3_all = alloc("c3_all", [TL, NB * NT], F32)
    stg2 = alloc("stg2", [TL, DFF // 2], F32)
    assert cur[0] <= _PB - 64

    prog = []

    class Ctr:
        __slots__ = ("sem", "n")

        def __init__(self, sem):
            self.sem, self.n = sem, 0

    def E(ek, fn):
        prog.append((ek, fn))

    def EI(ek, ctr, fn, k=1):
        selfw = ek in ("ve", "sc", "gp")
        n_after = ctr.n + k

        def go(eng, fn=fn, s=ctr.sem, k=k, selfw=selfw, n_after=n_after):
            fn(eng).then_inc(s, k)
            if selfw:
                eng.wait_ge(s, n_after)
        prog.append((ek, go))
        ctr.n += k

    def W(ek, ctr, val=None):
        v = ctr.n if val is None else val
        prog.append((ek, lambda eng, s=ctr.sem, v=v: eng.wait_ge(s, v)))

    def Wsp(ek, ctr):
        W(ek, ctr)
        if ek == 've':
            EI('ve', sv, lambda e: e.memset(spc[:, 0:8], 0.0))
        elif ek == 'sc':
            EI('sc', sc, lambda e: e.memzero(spc2[:, 0:8]))

    stack = contextlib.ExitStack()
    d = Ctr(stack.enter_context(nc.semaphore("d")))
    g = Ctr(stack.enter_context(nc.semaphore("g")))
    sv = Ctr(stack.enter_context(nc.semaphore("sv")))
    sc = Ctr(stack.enter_context(nc.semaphore("sc")))
    stc = Ctr(stack.enter_context(nc.semaphore("stc")))
    sg = Ctr(stack.enter_context(nc.semaphore("sg")))

    class Bank:
        def __init__(self, name, shape, dt):
            self.t = nc.alloc_psum_tensor(name, shape, dt)
            self.rel = None

        def guard(self):
            if self.rel is not None:
                W('te', self.rel[0], self.rel[1])

        def done(self, ctr):
            self.rel = (ctr, ctr.n)

    pa0 = Bank("pa0", [TL, 512], F32)
    pa1 = Bank("pa1", [TL, 512], F32)
    pc0 = Bank("pc0", [TL, 1024], F32)
    pc1 = Bank("pc1", [TL, 1024], F32)
    ptr0 = Bank("ptr0", [TL, TL], BF16)
    ptr1 = Bank("ptr1", [TL, TL], BF16)

    def mmgroup(bank, sl, mms, evacs):
        bank.guard()
        n = len(mms)
        for i, (lh, rh) in enumerate(mms):
            fn = (lambda e, lh=lh, rh=rh, s=(i == 0), p=(i == n - 1), b=bank, sl=sl:
                  e.matmul(b.t[sl], lh, rh, start=s, stop=p))
            if i == n - 1:
                EI('te', stc, fn)
            else:
                E('te', fn)
        last = None
        for ek, ctr, fn in evacs:
            W(ek, stc)
            EI(ek, ctr, fn)
            last = ctr
        if last is not None:
            bank.done(last)

    tr_i = [0]

    def transpose_to(dst, src, r0=0, r1=TL):
        bank = (ptr0, ptr1)[tr_i[0] % 2]
        tr_i[0] += 1
        bank.guard()
        EI('te', stc, lambda e, b=bank, s=src: e.transpose(b.t[:], s, ident[:]))
        W('ve', stc)
        EI('ve', sv, lambda e, b=bank, dd=dst, r0=r0, r1=r1:
           e.tensor_copy(out=dd, in_=b.t[r0:r1, :]))
        bank.done(sv)

    EI('ve', sv, lambda e: e.memset(ones_r[:], 1.0))
    _consts = [EPS, 1.0 / D, -1.0, 1e24, 1e30]
    for ci, cval in enumerate(_consts):
        EI('ve', sv, lambda e, ci=ci, cval=cval: e.memset(cstc[:, ci:ci + 1], cval))
        nc.const_aps.register(F32, cval, cstc[:, ci:ci + 1])
    EI('sy', d, lambda e: e.dma_start(out=ident[:], in_=ident_d), 16)
    for nm, t in [("CT", CT_s), ("ST", ST_s), ("CT2", CT2_s), ("ST2", ST2_s),
                  ("iw", iw_s), ("BANDM", BANDM_s),
                  ("K128", K128s), ("NEGM", NEGMs), ("K0", K0s), ("SURR0", SURR0s)]:
        EI('sy', d, lambda e, a=win[nm], t=t: e.dma_start(out=t[:], in_=a), 16)
    for nm, t in [("CK", CK_s), ("SK", SK_s)]:
        EI('sy', d, lambda e, a=win[nm]: e.dma_start(out=I_a[:, 0:1536], in_=a), 16)
        W('ve', d)
        EI('ve', sv, lambda e, t=t: e.tensor_copy(out=t[:], in_=I_a[:, 0:1536]))
        W('sy', sv)

    for nm in ["wdkv", "wuk", "wuv", "wdq", "wuq", "wqr", "wkr", "wout", "iwq", "iwk"]:
        K, C = wA[nm].shape[1], wA[nm].shape[2]
        src3 = win[nm].rearrange("(k p) c -> k p c", p=TL)
        for k in range(K):
            EI('sy', d, lambda e, s=src3, k=k, C=C:
               e.dma_start(out=I_a[:, 0:C], in_=s[k]), 16)
            W('ve', d)
            EI('ve', sv, lambda e, nm=nm, k=k, C=C:
               e.tensor_copy(out=wA[nm][:, k, :], in_=I_a[:, 0:C]))
            W('sy', sv)

    def emit_ln():
        EI('ve', sv, lambda e: e.tensor_reduce(out=col1[:], in_=x_s[:], axis=AX.X, op=ALU.add))
        W('ve', sv)
        EI('ve', sv, lambda e: e.tensor_scalar(out=col1[:], in0=col1[:], scalar1=1.0 / D,
                                               scalar2=None, op0=ALU.mult))
        W('ve', sv)
        EI('ve', sv, lambda e: e.tensor_scalar(out=xn_s[:], in0=x_s[:], scalar1=col1[:],
                                               scalar2=None, op0=ALU.subtract))
        Wsp('sc', sv)
        EI('sc', sc, lambda e: e.activation(out=I_b[:, 0:D], in_=xn_s[:], func=AF.Square))
        Wsp('ve', sc)
        EI('ve', sv, lambda e: e.tensor_reduce(out=col2[:], in_=I_b[:, 0:D], axis=AX.X, op=ALU.add))
        Wsp('sc', sv)
        EI('sc', sc, lambda e: e.activation(out=col2[:], in_=col2[:], func=AF.Sqrt,
                                            scale=1.0 / D, bias=EPS))
        Wsp('ve', sc)
        EI('ve', sv, lambda e: e.reciprocal(out=col2[:], in_=col2[:]))
        W('ve', sv)
        EI('ve', sv, lambda e: e.tensor_scalar(out=xn_s[:], in0=xn_s[:], scalar1=col2[:],
                                               scalar2=None, op0=ALU.mult))
        EI('ve', sv, lambda e: e.tensor_copy(out=xnbf[:], in_=xn_s[:]))
        W('te', sv)
        for c in range(8):
            transpose_to(xnT[:, c, :], xnbf[:, c * TL:(c + 1) * TL])

    def keys_phase(b):
        for T in range(NT):
            T0 = T * TL
            W('sy', g)
            EI('sy', d, lambda e, b=b, T0=T0: e.dma_start(
                out=x_s[:], in_=x_d[b, T0:T0 + TL, :]), 16)
            W('ve', d)
            W('sc', sv)
            emit_ln()
            W('sy', sv)
            EI('sy', d, lambda e, b=b, T=T: e.dma_start(
                out=xnt_d[b, T], in_=xnT[:].rearrange("p c t -> p (c t)")), 16)
            for cc in range(2):
                mmgroup((pa0, pa1)[cc], np.s_[:, 0:TL],
                        [(wA["wdkv"][:, k, cc * TL:(cc + 1) * TL], xnT[:, k, :])
                         for k in range(8)],
                        [('ve', sv, (lambda e, cc=cc, bk=(pa0, pa1)[cc]: e.tensor_copy(
                            out=ckvT[:, cc, :], in_=bk.t[:, 0:TL])))])
            W('te', sv)
            mmgroup(pc0, np.s_[:, 0:512],
                    [(ckvT[:, c, :], wA["wuk"][:, c, :]) for c in range(2)],
                    [('ve', sv, lambda e: e.tensor_copy(out=tab[:, 0:512],
                                                        in_=pc0.t[:, 0:512]))])
            mmgroup(pc1, np.s_[:, 0:512],
                    [(ckvT[:, c, :], wA["wuv"][:, c, :]) for c in range(2)],
                    [('ve', sv, lambda e: e.tensor_copy(out=tab[:, 512:1024],
                                                        in_=pc1.t[:, 0:512]))])
            mmgroup(pa0, np.s_[:, 0:DR],
                    [(xnT[:, k, :], wA["wkr"][:, k, :]) for k in range(8)],
                    [('ve', sv, lambda e: e.tensor_copy(out=w1t[:, 0:DR],
                                                        in_=pa0.t[:, 0:DR]))])
            Wsp('sc', sv)
            EI('sc', sc, lambda e: e.activation(out=w1t[:, 64:64 + DR], in_=w1t[:, 0:DR],
                                                func=AF.Abs))
            EI('sc', sc, lambda e: e.activation(out=w1t[:, 64:64 + DR], in_=w1t[:, 64:64 + DR],
                                                func=AF.Exp, scale=-1.0))
            EI('sc', sc, lambda e: e.activation(out=w1t[:, 64:64 + DR], in_=w1t[:, 64:64 + DR],
                                                func=AF.Ln, bias=1.0))
            EI('sc', sc, lambda e: e.activation(out=w1t[:, 128:128 + DR], in_=w1t[:, 0:DR],
                                                func=AF.Relu))
            Wsp('ve', sc)
            EI('ve', sv, lambda e: e.tensor_tensor(out=tab[:, 1024:1056],
                                                   in0=w1t[:, 64:64 + DR],
                                                   in1=w1t[:, 128:128 + DR], op=ALU.add))
            mmgroup(pa1, np.s_[0:64, 0:TL],
                    [(wA["iwk"][:, k, :], xnT[:, k, :]) for k in range(8)],
                    [('ve', sv, (lambda e, T0=T0: e.tensor_copy(
                        out=kiT[0:64, T0:T0 + TL], in_=pa1.t[0:64, 0:TL])))])
            W('sy', sv)
            EI('sy', d, lambda e, T0=T0: e.dma_start(
                out=kvtab_d[T0:T0 + TL, :], in_=tab[:]), 16)

    def emit_gathers(k0):
        W('gp', sv)
        W('gp', d)
        for j in range(GSL):
            EI('gp', g, (lambda e, j=j, k0=k0: e.indirect_dma_start(
                out=G[:, j, :], out_offset=None, in_=kvtab_d,
                in_offset=IndirectOffsetOnAxis(idx96[:, k0 + j:k0 + j + 1], 0))), 16)

    def attn_phase(b):
        for T in range(NT):
            T0 = T * TL
            EI('sy', d, lambda e, b=b, T=T: e.dma_start(
                out=xnT[:].rearrange("p c t -> p (c t)"), in_=xnt_d[b, T]), 16)
            EI('sy', d, lambda e, b=b, T0=T0: e.dma_start(
                out=x_s[:], in_=x_d[b, T0:T0 + TL, :]), 16)
            if T > 0:
                W('sy', stc)
                EI('sy', d, (lambda e, T0=T0: e.dma_start(
                    out=kcb[:],
                    in_=kvtab_d[T0 - TL:T0 + TL, 0:512].rearrange(
                        "(j p) c -> p j c", p=TL))), 16)
                EI('sy', d, (lambda e, T0=T0: e.dma_start(
                    out=VSb[:],
                    in_=kvtab_d[T0 - TL:T0 + TL, 512:1024].rearrange(
                        "(j p) c -> p j c", p=TL))), 16)
                EI('sy', d, (lambda e, T0=T0: e.dma_start(
                    out=kmub[:],
                    in_=kvtab_d[T0 - TL:T0 + TL, 1024:1056].rearrange(
                        "(j p) c -> p j c", p=TL))), 16)
            W('te', d)
            W('ve', d)
            W('ve', sg)
            W('sc', sv)
            W('gp', sv)
            for cc in range(4):
                mmgroup((pa0, pa1)[cc % 2], np.s_[:, 0:TL],
                        [(wA["wdq"][:, k, cc * TL:(cc + 1) * TL], xnT[:, k, :])
                         for k in range(8)],
                        [('ve', sv, (lambda e, cc=cc, bk=(pa0, pa1)[cc % 2]: e.tensor_copy(
                            out=cqT[:, cc, :], in_=bk.t[:, 0:TL])))])
            for h in range(NIH):
                mmgroup((pa0, pa1)[h % 2], np.s_[0:64, 0:TL],
                        [(wA["iwq"][:, k, h * 64:(h + 1) * 64], xnT[:, k, :])
                         for k in range(8)],
                        [('ve', sv, (lambda e, h=h, bk=(pa0, pa1)[h % 2]: e.tensor_copy(
                            out=qiT[0:64, h, :], in_=bk.t[0:64, 0:TL])))])
            W('te', sv)
            mmgroup(pc0, np.s_[:, 0:512],
                    [(cqT[:, c, :], wA["wuq"][:, c, :]) for c in range(4)],
                    [('ve', sv, lambda e: e.tensor_copy(out=qc[:], in_=pc0.t[:, 0:512]))])
            mmgroup(pc1, np.s_[:, 0:256],
                    [(cqT[:, c, :], wA["wqr"][:, c, :]) for c in range(4)],
                    [('ve', sv, lambda e: e.tensor_copy(out=qr[:], in_=pc1.t[:, 0:256]))])
            Wsp('sc', sv)
            EI('sc', sc, lambda e: e.activation(out=mu[:], in_=qr[:], func=AF.Abs))
            EI('sc', sc, lambda e: e.activation(out=mu[:], in_=mu[:], func=AF.Exp, scale=-1.0))
            EI('sc', sc, lambda e: e.activation(out=mu[:], in_=mu[:], func=AF.Ln, bias=1.0))
            EI('sc', sc, lambda e: e.activation(out=qr[:], in_=qr[:], func=AF.Relu))
            Wsp('ve', sc)
            EI('ve', sv, lambda e: e.tensor_tensor(out=mu[:], in0=mu[:], in1=qr[:], op=ALU.add))
            mu_v = mu[:].rearrange("p (h two i) -> p h two i", h=NH, two=2)
            ct_v = CT_s[:, T * 16:(T + 1) * 16].rearrange("p (o i) -> p o i", o=1).to_broadcast([TL, NH, 16])
            st_v = ST_s[:, T * 16:(T + 1) * 16].rearrange("p (o i) -> p o i", o=1).to_broadcast([TL, NH, 16])
            q1v = qt1[:].rearrange("p (h i) -> p h i", h=NH)
            q2v = qt2[:].rearrange("p (h i) -> p h i", h=NH)
            tmpv = rp[:, 0:NH * 16].rearrange("p (h i) -> p h i", h=NH)
            EI('ve', sv, lambda e, a=mu_v, b=ct_v, o=q1v: e.tensor_tensor(out=o, in0=a[:, :, 0, :], in1=b, op=ALU.mult))
            EI('ve', sv, lambda e, a=mu_v, b=st_v, o=tmpv: e.tensor_tensor(out=o, in0=a[:, :, 1, :], in1=b, op=ALU.mult))
            EI('ve', sv, lambda e, o=q1v, b=tmpv: e.tensor_tensor(out=o, in0=o, in1=b, op=ALU.subtract))
            EI('ve', sv, lambda e, a=mu_v, b=st_v, o=q2v: e.tensor_tensor(out=o, in0=a[:, :, 0, :], in1=b, op=ALU.mult))
            EI('ve', sv, lambda e, a=mu_v, b=ct_v, o=tmpv: e.tensor_tensor(out=o, in0=a[:, :, 1, :], in1=b, op=ALU.mult))
            EI('ve', sv, lambda e, o=q2v, b=tmpv: e.tensor_tensor(out=o, in0=o, in1=b, op=ALU.add))
            if T > 0:
                W('te', sv)
                for c in range(4):
                    transpose_to(qcT[:, c, :], qc[:, c * TL:(c + 1) * TL])
                ct2_v = CT2_s[:, T * 16:(T + 1) * 16].rearrange("p (o i) -> p o i", o=1).to_broadcast([TL, NH, 16])
                st2_v = ST2_s[:, T * 16:(T + 1) * 16].rearrange("p (o i) -> p o i", o=1).to_broadcast([TL, NH, 16])
                s1v = Ssc[:].rearrange("p (h i) -> p h i", h=NH)
                s2v = Ss2[:].rearrange("p (h i) -> p h i", h=NH)
                qb1 = qropeb[:, :, 64:80]
                qb2 = qropeb[:, :, 80:96]
                EI('ve', sv, lambda e, a=mu_v, b=ct2_v, o=s1v: e.tensor_tensor(out=o, in0=a[:, :, 0, :], in1=b, op=ALU.mult))
                EI('ve', sv, lambda e, a=mu_v, b=st2_v, o=s2v: e.tensor_tensor(out=o, in0=a[:, :, 1, :], in1=b, op=ALU.mult))
                EI('ve', sv, lambda e, o=qb1, x=s1v, y=s2v: e.tensor_tensor(out=o, in0=x, in1=y, op=ALU.subtract))
                EI('ve', sv, lambda e, a=mu_v, b=st2_v, o=s1v: e.tensor_tensor(out=o, in0=a[:, :, 0, :], in1=b, op=ALU.mult))
                EI('ve', sv, lambda e, a=mu_v, b=ct2_v, o=s2v: e.tensor_tensor(out=o, in0=a[:, :, 1, :], in1=b, op=ALU.mult))
                EI('ve', sv, lambda e, o=qb2, x=s1v, y=s2v: e.tensor_tensor(out=o, in0=x, in1=y, op=ALU.add))
                W('te', sv)
                for h in range(NH):
                    transpose_to(qropeT[64:96, h, :], qropeb[:, h, :], r0=64, r1=96)
            Wt = T0 + 64 if T > 0 else TL
            nch = (Wt + 511) // 512
            for j in range(nch):
                w = min(512, Wt - j * 512)
                for h in range(NIH):
                    bk = (pa0, pa1)[(j * NIH + h) % 2]
                    lh = qiT[0:64, h, :]
                    rh = kiT[0:64, j * 512:j * 512 + w]
                    if h == 0:
                        mmgroup(bk, np.s_[:, 0:w], [(lh, rh)],
                                [('ve', sv, (lambda e, bk=bk, w=w, j=j: e.tensor_scalar(
                                    out=I_a[:, j * 512:j * 512 + w], in0=bk.t[:, 0:w],
                                    scalar1=0.0, scalar2=iw_s[:, 0:1],
                                    op0=ALU.max, op1=ALU.mult)))])
                    else:
                        W('sc', sv)
                        mmgroup(bk, np.s_[:, 0:w], [(lh, rh)],
                                [('sc', sc, (lambda e, bk=bk, w=w: e.activation(
                                    out=I_b[:, 0:w], in_=bk.t[:, 0:w], func=AF.Relu)))])
                        Wsp('ve', sc)
                        EI('ve', sv, (lambda e, w=w, j=j, h=h: e.scalar_tensor_tensor(
                            out=I_a[:, j * 512:j * 512 + w], in0=I_b[:, 0:w],
                            scalar=iw_s[:, h:h + 1], in1=I_a[:, j * 512:j * 512 + w],
                            op0=ALU.mult, op1=ALU.add)))
            if T > 0:
                EI('ve', sv, (lambda e, T0=T0: e.tensor_tensor(
                    out=I_a[:, T0 - 64:T0 + 64], in0=I_a[:, T0 - 64:T0 + 64],
                    in1=K128s[:], op=ALU.mult)))
                EI('ve', sv, (lambda e, T0=T0: e.tensor_tensor(
                    out=I_a[:, T0 - 64:T0 + 64], in0=I_a[:, T0 - 64:T0 + 64],
                    in1=NEGMs[:], op=ALU.add)))
                nr, off = 4, 64
            else:
                EI('ve', sv, lambda e: e.tensor_tensor(out=I_a[:, 0:TL], in0=I_a[:, 0:TL],
                                                       in1=K0s[:], op=ALU.mult))
                EI('ve', sv, lambda e: e.tensor_tensor(out=I_a[:, 0:TL], in0=I_a[:, 0:TL],
                                                       in1=SURR0s[:], op=ALU.add))
                nr, off = 12, 0
            for r in range(nr):
                cur_t = (I_a, I_b)[r % 2]
                oth_t = (I_b, I_a)[r % 2]
                EI('ve', sv, (lambda e, c=cur_t, Wt=Wt: e.max(out=v8[:], in_=c[:, 0:Wt])))
                W('ve', sv)
                EI('ve', sv, (lambda e, c=cur_t, Wt=Wt, r=r, off=off: e.max_index(
                    out=idx96[:, off + r * 8:off + r * 8 + 8],
                    in_max=v8[:], in_values=c[:, 0:Wt])))
                if r < nr - 1:
                    EI('ve', sv, (lambda e, c=cur_t, o=oth_t, Wt=Wt: e.match_replace(
                        out=o[:, 0:Wt], in_to_replace=v8[:], in_values=c[:, 0:Wt],
                        imm_value=NEG)))
            if T > 0:
                emit_gathers(64)
            else:
                EI('ve', sv, lambda e: e.memset(mrun[:], NEG))
                EI('ve', sv, lambda e: e.memset(dnm[:], 0.0))
                EI('ve', sv, lambda e: e.memset(Aacc[:], 0.0))
            qc_v = qc[:].rearrange("p (h o dd) -> p h o dd", h=NH, o=1).to_broadcast([TL, NH, GSL, DH])
            gk_v = G[:, :, 0:512].rearrange("p k (h dd) -> p h k dd", h=NH)
            pr_v = prods[:].rearrange("p (h k dd) -> p h k dd", h=NH, k=GSL)
            nu1 = G[:, :, 1024:1040]
            nu2 = G[:, :, 1040:1056]
            k1v = kt1[:].rearrange("p (k i) -> p k i", i=16)
            k2v = kt2[:].rearrange("p (k i) -> p k i", i=16)
            ktv = ktt[:].rearrange("p (k i) -> p k i", i=16)
            q1b = qt1[:].rearrange("p (h o i) -> p h o i", h=NH, o=1).to_broadcast([TL, NH, GSL, 16])
            q2b = qt2[:].rearrange("p (h o i) -> p h o i", h=NH, o=1).to_broadcast([TL, NH, GSL, 16])
            k1b = kt1[:].rearrange("p (o k i) -> p o k i", o=1, i=16).to_broadcast([TL, NH, GSL, 16])
            k2b = kt2[:].rearrange("p (o k i) -> p o k i", o=1, i=16).to_broadcast([TL, NH, GSL, 16])
            rpv = rp[:].rearrange("p (h k i) -> p h k i", h=NH, k=GSL)
            rp2v = rp2[:].rearrange("p (h k i) -> p h k i", h=NH, k=GSL)
            Sv = Ssc[:].rearrange("p (h k) -> p h k", h=NH)
            S2v = Ss2[:].rearrange("p (h k) -> p h k", h=NH)
            ev = ebuf[:].rearrange("p (h k) -> p h k", h=NH)
            mb = mnew[:].rearrange("p (h o) -> p h o", o=1).to_broadcast([TL, NH, GSL])
            ab = alph[:].rearrange("p (h o) -> p h o", o=1).to_broadcast([TL, NH, DH])
            Av = Aacc[:].rearrange("p (h dd) -> p h dd", h=NH)
            eb2 = ebuf[:].rearrange("p (h o k) -> p h o k", h=NH, o=1).to_broadcast([TL, NH, DH, GSL])
            gv_v = G[:, :, 512:1024].rearrange("p k (h dd) -> p h dd k", h=NH)
            pr2 = prods[:].rearrange("p (h dd k) -> p h dd k", h=NH, dd=DH)
            avv = avp[:].rearrange("p (h dd) -> p h dd", h=NH)
            if T > 0:
                for j in range(2):
                    for cc in range(4):
                        transpose_to(kcTb[:, cc, j * TL:(j + 1) * TL],
                                     kcb[:, j, cc * TL:(cc + 1) * TL])
                for j in range(2):
                    ctk = CT_s[:, (T - 1 + j) * 16:(T + j) * 16]
                    stk = ST_s[:, (T - 1 + j) * 16:(T + j) * 16]
                    m1 = kmub[:, j, 0:16]
                    m2 = kmub[:, j, 16:32]
                    if j == 1:
                        W('ve', stc)
                    EI('ve', sv, lambda e, m1=m1, c=ctk: e.tensor_tensor(out=Ssc[:, 0:16], in0=m1, in1=c, op=ALU.mult))
                    EI('ve', sv, lambda e, m2=m2, s=stk: e.tensor_tensor(out=Ss2[:, 0:16], in0=m2, in1=s, op=ALU.mult))
                    EI('ve', sv, lambda e: e.tensor_tensor(out=rotb[:, 64:80], in0=Ssc[:, 0:16], in1=Ss2[:, 0:16], op=ALU.subtract))
                    EI('ve', sv, lambda e, m1=m1, s=stk: e.tensor_tensor(out=Ssc[:, 0:16], in0=m1, in1=s, op=ALU.mult))
                    EI('ve', sv, lambda e, m2=m2, c=ctk: e.tensor_tensor(out=Ss2[:, 0:16], in0=m2, in1=c, op=ALU.mult))
                    EI('ve', sv, lambda e: e.tensor_tensor(out=rotb[:, 80:96], in0=Ssc[:, 0:16], in1=Ss2[:, 0:16], op=ALU.add))
                    W('te', sv)
                    transpose_to(krTt[64:96, j, :], rotb[:], r0=64, r1=96)
                W('te', sv)
                krf = krTt[64:96, :, :].rearrange("p j t -> p (j t)")
                for h in range(NH):
                    bk = (pc0, pc1)[h // 4]
                    col = (h % 4) * 256
                    hb = 64 * (h % 2)
                    mms = [(qcT[hb:hb + 64, h // 2, :],
                            kcTb[hb:hb + 64, h // 2, :]),
                           (qropeT[64:96, h, :], krf)]
                    ev_ = []
                    if h % 4 == 3:
                        ev_ = [('ve', sv, (lambda e, bk=bk, q=h // 4: e.scalar_tensor_tensor(
                            out=bscore[:, q * 1024:(q + 1) * 1024], in0=bk.t[:, 0:1024],
                            scalar=SCALE, in1=BANDM_s[:], op0=ALU.mult, op1=ALU.add)))]
                    mmgroup(bk, np.s_[:, col:col + 256], mms, ev_)
                bs8 = bscore[:].rearrange("p (h u) -> p h u", h=NH)
                mb256 = mrun[:].rearrange("p (h o) -> p h o", o=1).to_broadcast([TL, NH, 256])
                EI('ve', sv, lambda e, bs8=bs8: e.tensor_reduce(out=mrun[:], in_=bs8, axis=AX.X, op=ALU.max))
                EI('ve', sv, lambda e, bs8=bs8, mb256=mb256: e.tensor_tensor(out=bs8, in0=bs8, in1=mb256, op=ALU.subtract))
                Wsp('sc', sv)
                EI('sc', sc, lambda e: e.activation(out=bscore[:], in_=bscore[:], func=AF.Exp))
                Wsp('ve', sc)
                EI('ve', sv, lambda e, bs8=bs8: e.tensor_reduce(out=dnm[:], in_=bs8, axis=AX.X, op=ALU.add))
                EI('ve', sv, lambda e: e.tensor_copy(out=battn[:], in_=bscore[:]))
                W('te', sv)
                pa0.guard()
                for h in range(NH):
                    for c in range(2):
                        transpose_to(batT[:, c, :], battn[:, 256 * h + 128 * c:256 * h + 128 * c + 128])
                    W('te', sv)
                    E('te', (lambda e, h=h: e.matmul(
                        pa0.t[:, 64 * h:64 * h + 64], batT[:, 0, :],
                        VSb[:, 0, 64 * h:64 * h + 64], start=True, stop=False)))
                    EI('te', stc, (lambda e, h=h: e.matmul(
                        pa0.t[:, 64 * h:64 * h + 64], batT[:, 1, :],
                        VSb[:, 1, 64 * h:64 * h + 64], start=False, stop=True)))
                W('ve', stc)
                EI('ve', sv, lambda e: e.tensor_copy(out=Aacc[:], in_=pa0.t[:, 0:512]))
                pa0.done(sv)
                sweep_plan = [(64, 80), (80, None)]
            else:
                sweep_plan = [(k0, None) for k0 in range(0, 96, GSL)]
            for k0, pre in sweep_plan:
                if T == 0:
                    emit_gathers(k0)
                ckv_ = CK_s[:].rearrange("p (k i) -> p k i", i=16)[:, k0:k0 + GSL, :]
                skv_ = SK_s[:].rearrange("p (k i) -> p k i", i=16)[:, k0:k0 + GSL, :]
                sv_snap = sv.n
                W('gp', g)
                W('gp', sv, sv_snap)
                EI('gp', sg, lambda e: e.memset(spc3[:, 0:8], 0.0))
                EI('gp', sg, lambda e, ckv_=ckv_: e.tensor_tensor(out=k1v, in0=nu1, in1=ckv_, op=ALU.mult))
                EI('gp', sg, lambda e, skv_=skv_: e.tensor_tensor(out=ktv, in0=nu2, in1=skv_, op=ALU.mult))
                EI('gp', sg, lambda e: e.tensor_tensor(out=k1v, in0=k1v, in1=ktv, op=ALU.subtract))
                EI('gp', sg, lambda e, skv_=skv_: e.tensor_tensor(out=k2v, in0=nu1, in1=skv_, op=ALU.mult))
                EI('gp', sg, lambda e, ckv_=ckv_: e.tensor_tensor(out=ktv, in0=nu2, in1=ckv_, op=ALU.mult))
                EI('gp', sg, lambda e: e.tensor_tensor(out=k2v, in0=k2v, in1=ktv, op=ALU.add))
                EI('gp', sg, lambda e: e.tensor_tensor(out=rpv, in0=q1b, in1=k1b, op=ALU.mult))
                EI('gp', sg, lambda e: e.tensor_tensor(out=rp2v, in0=q2b, in1=k2b, op=ALU.mult))
                EI('gp', sg, lambda e: e.tensor_tensor(out=rpv, in0=rpv, in1=rp2v, op=ALU.add))
                sg_snap = sg.n
                W('ve', g)
                EI('ve', sv, lambda e: e.tensor_tensor(out=pr_v, in0=qc_v, in1=gk_v, op=ALU.mult))
                EI('ve', sv, lambda e: e.tensor_reduce(out=Sv, in_=pr_v, axis=AX.X, op=ALU.add))
                W('ve', sg, sg_snap)
                EI('ve', sv, lambda e: e.memset(spc[:, 0:8], 0.0))
                EI('ve', sv, lambda e: e.tensor_reduce(out=S2v, in_=rpv, axis=AX.X, op=ALU.add))
                EI('ve', sv, lambda e: e.tensor_tensor(out=Ssc[:], in0=Ssc[:], in1=Ss2[:], op=ALU.add))
                EI('ve', sv, lambda e: e.tensor_scalar(out=Ssc[:], in0=Ssc[:], scalar1=SCALE,
                                                       scalar2=None, op0=ALU.mult))
                EI('ve', sv, lambda e: e.tensor_reduce(out=rpart[:], in_=Sv, axis=AX.X, op=ALU.max))
                EI('ve', sv, lambda e: e.tensor_tensor(out=mnew[:], in0=mrun[:], in1=rpart[:], op=ALU.max))
                EI('ve', sv, lambda e: e.tensor_tensor(out=alph[:], in0=mrun[:], in1=mnew[:], op=ALU.subtract))
                Wsp('sc', sv)
                EI('sc', sc, lambda e: e.activation(out=alph[:], in_=alph[:], func=AF.Exp))
                EI('ve', sv, lambda e: e.tensor_tensor(out=Sv, in0=Sv, in1=mb, op=ALU.subtract))
                Wsp('sc', sv)
                EI('sc', sc, lambda e: e.activation(out=ebuf[:], in_=Ssc[:], func=AF.Exp))
                Wsp('ve', sc)
                EI('ve', sv, lambda e: e.tensor_reduce(out=rpart[:], in_=ev, axis=AX.X, op=ALU.add))
                EI('ve', sv, lambda e: e.tensor_tensor(out=dnm[:], in0=dnm[:], in1=alph[:], op=ALU.mult))
                EI('ve', sv, lambda e: e.tensor_tensor(out=dnm[:], in0=dnm[:], in1=rpart[:], op=ALU.add))
                EI('ve', sv, lambda e: e.tensor_copy(out=mrun[:], in_=mnew[:]))
                EI('ve', sv, lambda e: e.tensor_tensor(out=Av, in0=Av, in1=ab, op=ALU.mult))
                EI('ve', sv, lambda e: e.tensor_tensor(out=pr2, in0=eb2, in1=gv_v, op=ALU.mult))
                if pre is not None:
                    emit_gathers(pre)
                EI('ve', sv, lambda e: e.tensor_reduce(out=avv, in_=pr2, axis=AX.X, op=ALU.add))
                EI('ve', sv, lambda e: e.tensor_tensor(out=Aacc[:], in0=Aacc[:], in1=avp[:], op=ALU.add))

            EI('ve', sv, lambda e: e.reciprocal(out=dnm[:], in_=dnm[:]))
            db = dnm[:].rearrange("p (h o) -> p h o", o=1).to_broadcast([TL, NH, DH])
            EI('ve', sv, lambda e: e.tensor_tensor(
                out=attnbf[:].rearrange("p (h dd) -> p h dd", h=NH), in0=Av, in1=db, op=ALU.mult))
            W('te', sv)
            for c in range(4):
                transpose_to(attnT[:, c, :], attnbf[:, c * TL:(c + 1) * TL])
            W('te', sv)
            for half in range(2):
                mmgroup(pc0, np.s_[:, half * 512:(half + 1) * 512],
                        [(attnT[:, c, :], wA["wout"][:, c, half * 512:(half + 1) * 512])
                         for c in range(4)], [])
            W('ve', stc)
            EI('ve', sv, lambda e: e.tensor_tensor(out=part_s[:], in0=pc0.t[:, 0:1024],
                                                   in1=x_s[:], op=ALU.add))
            pc0.done(sv)
            W('sy', sv)
            EI('sy', d, (lambda e, b=b, T0=T0: e.dma_start(
                out=part_d[b, T0:T0 + TL, :], in_=part_s[:])), 16)

    def ffn_phase(half):
        f0 = half * (DFF // 2)
        wiT3 = win["wiT"].rearrange("(k p) f -> k p f", p=TL)
        woT3 = win["woT"].rearrange("(k p) dd -> k p dd", p=TL)
        AT3 = win["AT"].rearrange("(k p) c -> k p c", p=TL)
        gwT3 = win["gwT"].rearrange("(k p) c -> k p c", p=TL)
        jobs = []
        for kc in range(8):
            jobs.append((wiT3[kc, :, f0:f0 + 2048], wiT_s[:, kc, :], TL, 2048))
        for kc in range(16):
            jobs.append((woT3[f0 // TL + kc, :, :], woT_s[:, kc, :], TL, 1024))
        for ee in range(NE):
            jobs.append((win["BT"][ee, :, f0:f0 + 2048], BT_s[0:16, ee, :], 16, 2048))
        jobs.append((win["bi"][:, f0:f0 + 2048], bi_s[0:1, 0:2048], 1, 2048))
        if half == 0:
            for kc in range(8):
                jobs.append((AT3[kc], AT_s[:, kc, :], TL, NE * TL))
            for kc in range(8):
                jobs.append((gwT3[kc], gwT_s[:, kc, :], TL, TL))
            jobs.append((win["bo"], bo_s[:], 1, D))
            jobs.append((win["gb"], gbr[:], 1, TL))
        bufs = (stg, stg2)
        ccnt, dcnt = [], []
        for i, (src, dst, p, c) in enumerate(jobs):
            bf = bufs[i % 2]
            if i >= 2:
                W('sy', sv, ccnt[i - 2])
            EI('sy', d, (lambda e, src=src, bf=bf, p=p, c=c: e.dma_start(
                out=bf[0:p, 0:c], in_=src)), 16)
            dcnt.append(d.n)
            W('ve', d, dcnt[i])
            EI('ve', sv, (lambda e, dst=dst, bf=bf, p=p, c=c: e.tensor_copy(
                out=dst, in_=bf[0:p, 0:c])))
            ccnt.append(sv.n)
        W('sy', sv)
        W('te', sv)
        W('sc', sv)
        W('gp', sv)

        for b in range(NB):
            for T in range(NT):
                T0 = T * TL
                EI('sy', d, (lambda e, b=b, T=T: e.dma_start(
                    out=xnT_b[:].rearrange("p c t -> p (c t)"), in_=xnt_d[b, T])), 16)
                EI('sy', d, (lambda e, b=b, T0=T0: e.dma_start(
                    out=partb[:], in_=part_d[b, T0:T0 + TL, :])), 16)
                W('te', d)
                W('ve', d)
                W('sc', sv)
                W('gp', sv)
                ti = b * NT + T
                gts = gates_all[:, ti * NE:(ti + 1) * NE]
                if half == 0:
                    mms = [(xnT_b[:, k, :], gwT_s[:, k, :]) for k in range(8)]
                    mms.append((ones_r[:], gbr[:]))
                    mmgroup(pa0, np.s_[:, 0:TL], mms,
                            [('ve', sv, lambda e: e.tensor_copy(out=probs[:], in_=pa0.t[:, 0:NE]))])
                    EI('ve', sv, lambda e: e.tensor_reduce(out=c1[:], in_=probs[:], axis=AX.X,
                                                           op=ALU.max, negate=True))
                    W('ve', sv)
                    EI('ve', sv, lambda e: e.tensor_scalar(out=probs[:], in0=probs[:], scalar1=c1[:],
                                                           scalar2=None, op0=ALU.add))
                    Wsp('sc', sv)
                    EI('sc', sc, lambda e: e.activation(out=probs[:], in_=probs[:], func=AF.Exp))
                    Wsp('ve', sc)
                    EI('ve', sv, lambda e: e.tensor_reduce(out=c1[:], in_=probs[:], axis=AX.X, op=ALU.add))
                    W('ve', sv)
                    EI('ve', sv, lambda e: e.reciprocal(out=c1[:], in_=c1[:]))
                    W('ve', sv)
                    EI('ve', sv, lambda e: e.tensor_scalar(out=probs[:], in0=probs[:], scalar1=c1[:],
                                                           scalar2=None, op0=ALU.mult))
                    EI('ve', sv, lambda e: e.tensor_reduce(out=c1[:], in_=probs[:], axis=AX.X, op=ALU.max))
                    W('ve', sv)
                    EI('ve', sv, lambda e: e.tensor_scalar(out=dcol[:], in0=probs[:], scalar1=c1[:],
                                                           scalar2=None, op0=ALU.is_equal))
                    EI('ve', sv, lambda e: e.scalar_tensor_tensor(out=dcol[:], in0=dcol[:], scalar=-2.0,
                                                                  in1=probs[:], op0=ALU.mult, op1=ALU.add))
                    EI('ve', sv, lambda e: e.tensor_reduce(out=c2[:], in_=dcol[:], axis=AX.X, op=ALU.max))
                    W('ve', sv)
                    EI('ve', sv, lambda e: e.tensor_tensor(out=c3[:], in0=c1[:], in1=c2[:], op=ALU.add))
                    W('ve', sv)
                    EI('ve', sv, lambda e, gts=gts: e.tensor_scalar(out=gts, in0=probs[:], scalar1=c2[:],
                                                                    scalar2=c3[:], op0=ALU.is_ge, op1=ALU.mult))
                    EI('ve', sv, lambda e, ti=ti: e.tensor_scalar(out=c3_all[:, ti:ti + 1], in0=c3[:], scalar1=2.0,
                                                                  scalar2=None, op0=ALU.mult))
                else:
                    EI('ve', sv, lambda e: e.memset(gpk[:], 0.0))
                    W('ve', sv)
                    EI('ve', sv, lambda e, ti=ti: e.tensor_copy(out=gpk[:, 0:1], in_=c3_all[:, ti:ti + 1]))
                    W('te', sv)
                    ptr0.guard()
                    EI('te', stc, lambda e: e.transpose(ptr0.t[0:16, :], gpk[:], ident[:]))
                    W('ve', stc)
                    EI('ve', sv, lambda e: e.tensor_copy(out=gsr[:], in_=ptr0.t[0:1, 0:TL]))
                    ptr0.done(sv)
                for ee in range(NE):
                    mmgroup(pa1, np.s_[:, 0:TL],
                            [(AT_s[:, k, ee * TL:(ee + 1) * TL], xnT_b[:, k, :])
                             for k in range(8)],
                            [('ve', sv, (lambda e, ee=ee: e.tensor_copy(
                                out=la_s[0:16, ee * TL:(ee + 1) * TL],
                                in_=pa1.t[0:16, 0:TL])))])
                for grp in range(2):
                    n0 = grp * 1024
                    pb = (pc0, pc1)[grp]
                    for nn in range(2):
                        sub = [(xnT_b[:, k, :],
                                wiT_s[:, k, n0 + nn * 512:n0 + nn * 512 + 512])
                               for k in range(8)]
                        sub.append((ones_r[:],
                                    bi_s[0:1, n0 + nn * 512:n0 + nn * 512 + 512]))
                        ev_ = [] if nn == 0 else [
                            ('ve', sv, (lambda e, grp=grp, pb=pb: e.tensor_copy(
                                out=base_s[:, grp * 1024:(grp + 1) * 1024],
                                in_=pb.t[:, 0:1024])))]
                        mmgroup(pb, np.s_[:, nn * 512:(nn + 1) * 512], sub, ev_)
                    pb.done(sv)
                W('gp', sv)
                EI('gp', sg, lambda e: e.memset(Hacc[:], 0.0))
                hcnt = []
                lbufs = (lorab, sadd)
                i = 0
                for ee in range(NE):
                    for grp in range(2):
                        pb = (pc0, pc1)[grp]
                        lb = lbufs[i % 2]
                        if i >= 2:
                            W('ve', sg, hcnt[i - 2])
                        for nn in range(2):
                            n0 = grp * 1024 + nn * 512
                            ev_ = [] if nn == 0 else [
                                ('ve', sv, (lambda e, pb=pb, grp=grp, lb=lb: e.tensor_tensor(
                                    out=lb[:], in0=pb.t[:, 0:1024],
                                    in1=base_s[:, grp * 1024:(grp + 1) * 1024], op=ALU.add))),
                                ('ve', sv, (lambda e, ee=ee, gts=gts, lb=lb: e.tensor_scalar(
                                    out=lb[:], in0=lb[:], scalar1=0.0,
                                    scalar2=gts[:, ee:ee + 1], op0=ALU.max, op1=ALU.mult)))]
                            mmgroup(pb, np.s_[:, nn * 512:(nn + 1) * 512],
                                    [(la_s[0:16, ee * TL:(ee + 1) * TL],
                                      BT_s[0:16, ee, n0:n0 + 512])],
                                    ev_)
                        pb.done(sv)
                        W('gp', sv)
                        EI('gp', sg, (lambda e, grp=grp, lb=lb: e.tensor_tensor(
                            out=Hacc[:, grp * 1024:(grp + 1) * 1024],
                            in0=Hacc[:, grp * 1024:(grp + 1) * 1024],
                            in1=lb[:], op=ALU.add)))
                        hcnt.append(sg.n)
                        i += 1
                W('ve', sg)
                EI('ve', sv, lambda e: e.memset(spc[:, 0:8], 0.0))
                EI('ve', sv, lambda e: e.tensor_copy(out=Hbf[:], in_=Hacc[:]))
                W('te', sv)
                for c in range(16):
                    transpose_to(HT[:, c, :], Hbf[:, c * TL:(c + 1) * TL])
                W('te', sv)
                for nn in range(2):
                    sub = [(HT[:, c, :], woT_s[:, c, nn * 512:(nn + 1) * 512])
                           for c in range(16)]
                    if half == 1:
                        sub.append((gsr[:], bo_s[0:1, nn * 512:(nn + 1) * 512]))
                    mmgroup(pc0, np.s_[:, nn * 512:(nn + 1) * 512], sub, [])
                W('ve', stc)
                EI('ve', sv, lambda e: e.tensor_tensor(out=outb[:], in0=pc0.t[:, 0:1024],
                                                       in1=partb[:], op=ALU.add))
                pc0.done(sv)
                W('sy', sv)
                dst = part_d if half == 0 else out_d
                EI('sy', d, (lambda e, b=b, T0=T0, dst=dst: e.dma_start(
                    out=dst[b, T0:T0 + TL, :], in_=outb[:])), 16)

    for b in range(NB):
        keys_phase(b)
        attn_phase(b)
    ffn_phase(0)
    ffn_phase(1)
    W('sy', d)
    W('sy', g)

    with nc.Block() as block:
        def mk(ek):
            def run(eng):
                for k, fn in prog:
                    if k == ek:
                        fn(eng)
            return run
        block.sync(mk('sy'))
        block.vector(mk('ve'))
        block.scalar(mk('sc'))
        block.tensor(mk('te'))
        block.gpsimd(mk('gp'))

    stack.close()
    return nc


def kernel(**inputs):
    rd = np.asarray(inputs["L0_attn_raw_delta"], dtype=np.float32)
    assert rd.shape == (16,)
    CT, ST, CT2, ST2, CK, SK = _host_consts()
    _MC = _mask_consts()
    x = np.ascontiguousarray(np.asarray(inputs["L0_attn_x"], dtype=np.float32))
    f32 = lambda a: np.ascontiguousarray(np.asarray(a, dtype=np.float32))
    base_map = {
        "wdkv": f32(inputs["L0_attn_W_dkv"]), "wuk": f32(inputs["L0_attn_W_uk"]),
        "wuv": f32(inputs["L0_attn_W_uv"]), "wdq": f32(inputs["L0_attn_W_dq"]),
        "wuq": f32(inputs["L0_attn_W_uq"]), "wqr": f32(inputs["L0_attn_W_qr"]),
        "wkr": f32(inputs["L0_attn_W_kr"]), "wout": f32(inputs["L0_attn_W_out"]),
        "iwq": f32(inputs["L0_attn_idx_wq"]), "iwk": f32(inputs["L0_attn_idx_wk"]),
        "iw": np.tile(np.asarray(inputs["L0_attn_idx_w"], dtype=np.float32)[None, :], (TL, 1)),
        "wiT": f32(np.asarray(inputs["L0_ffn_wi"], dtype=np.float32).T),
        "woT": f32(np.asarray(inputs["L0_ffn_wo"], dtype=np.float32).T),
        "gwT": f32(np.pad(np.asarray(inputs["L0_ffn_gate_w"], dtype=np.float32).T,
                      ((0, 0), (0, TL - NE)))),
        "AT": f32(np.pad(np.transpose(np.asarray(inputs["L0_ffn_lora_A"], dtype=np.float32),
                                  (2, 0, 1)), ((0, 0), (0, 0), (0, TL - RLO))).reshape(D, NE * TL)),
        "BT": f32(np.transpose(np.asarray(inputs["L0_ffn_lora_B"], dtype=np.float32), (0, 2, 1))),
        "bi": f32(np.asarray(inputs["L0_ffn_bi"], dtype=np.float32).reshape(1, DFF)),
        "bo": f32(np.asarray(inputs["L0_ffn_bo"], dtype=np.float32).reshape(1, D)),
        "gb": f32(np.pad(np.asarray(inputs["L0_ffn_gate_b"], dtype=np.float32).reshape(1, NE),
                     ((0, 0), (0, TL - NE)))),
        "CT": CT, "ST": ST, "CT2": CT2, "ST2": ST2, "CK": CK, "SK": SK,
        "K128": _MC[0], "NEGM": _MC[1], "K0": _MC[2], "SURR0": _MC[3],
        "BANDM": _MC[4], "identw": _MC[5],
    }
    nc = build_nc()
    in_maps = []
    for c in range(8):
        m = dict(base_map)
        m["x"] = np.ascontiguousarray(x[2 * c:2 * c + 2])
        in_maps.append(m)
    res = run_bass_kernel_spmd(nc, in_maps, core_ids=list(range(8)))
    out = np.concatenate([np.asarray(r["out"]) for r in res.results], axis=0)
    return out.astype(np.float32)


# version 10
# speedup vs baseline: 1.0476x; 1.0476x over previous
import contextlib
import numpy as np
import ml_dtypes
import concourse.bass as bass
import concourse.mybir as mybir
from concourse.bass_utils import run_bass_kernel_spmd
from concourse.bass_core_rust import IndirectOffsetOnAxis

F32 = mybir.dt.float32
BF16 = mybir.dt.bfloat16
I32 = mybir.dt.int32
U32 = mybir.dt.uint32
AF = mybir.ActivationFunctionType
ALU = mybir.AluOpType
AX = mybir.AxisListType
bfdt = ml_dtypes.bfloat16

L, D, NB, NT, TL = 2048, 1024, 2, 16, 128
NH, DH, DR = 8, 64, 32
NIH = 4
DFF, NE, RLO = 4096, 8, 16
ROW = 512 + 512 + 32
GSL = 16
NSW = 6
SCALE = (DH + DR) ** -0.5
NEG = -3.0e38
EPS = 1e-5
_PB = 196608


def _host_consts():
    th = 10000.0 ** (-np.arange(16, dtype=np.float64) / 16.0)

    def pack(ang):
        c = np.cos(ang).astype(np.float32)
        s = np.sin(ang).astype(np.float32)
        return (c.reshape(NT, TL, 16).transpose(1, 0, 2).reshape(TL, NT * 16).copy(),
                s.reshape(NT, TL, 16).transpose(1, 0, 2).reshape(TL, NT * 16).copy())
    CT, ST = pack(np.arange(L, dtype=np.float64)[:, None] * th[None, :])
    CT2, ST2 = pack((2.0 * np.arange(L, dtype=np.float64)[:, None] - 63.0) * th[None, :])
    ang_k = np.arange(96, dtype=np.float64)[:, None] * th[None, :]
    CK = np.repeat(np.cos(ang_k).astype(np.float32).reshape(1, 96 * 16), TL, axis=0).copy()
    SK = np.repeat(np.sin(ang_k).astype(np.float32).reshape(1, 96 * 16), TL, axis=0).copy()
    return CT, ST, CT2, ST2, CK, SK



def _mask_consts():
    p = np.arange(TL)[:, None]
    s = np.arange(TL)[None, :]
    K128 = (p - s >= 64).astype(np.float32)
    u = np.arange(TL)[None, :]
    K128 = (u <= p).astype(np.float32)
    NEGM = np.where(u <= p, 0.0, NEG).astype(np.float32)
    d = (p - s).astype(np.float64)
    K0 = (d >= 64).astype(np.float32)
    SURR = np.where((d >= 0) & (d <= 63), 1e30 + d * 1e24,
                    np.where(d < 0, -1e30 + d * 1e24, 0.0)).astype(np.float32)
    u2 = np.arange(256)[None, :]
    BANDM = np.where((u2 >= p + 65) & (u2 <= p + 128), 0.0, NEG).astype(np.float32)
    BANDM4 = np.tile(BANDM, (1, 4)).copy()
    IDENT = np.eye(TL, dtype=np.float32).astype(bfdt)
    return K128, NEGM, K0, SURR, BANDM4, IDENT


def build_nc(dbg=False):
    nc = bass.Bass("TRN2", target_bir_lowering=False)

    x_d = nc.dram_tensor("x", [NB, L, D], F32, kind="ExternalInput").ap()
    out_d = nc.dram_tensor("out", [NB, L, D], F32, kind="ExternalOutput").ap()
    win = {}
    for name, shape in [
        ("wdkv", [D, 256]), ("wuk", [256, 512]), ("wuv", [256, 512]),
        ("wdq", [D, 512]), ("wuq", [512, 512]), ("wqr", [512, 256]),
        ("wkr", [D, DR]), ("wout", [512, D]), ("iwq", [D, 256]), ("iwk", [D, 64]),
        ("iw", [TL, NIH]),
        ("wiT", [D, DFF]), ("woT", [DFF, D]), ("gwT", [D, TL]),
        ("AT", [D, NE * TL]), ("BT", [NE, RLO, DFF]),
        ("bi", [1, DFF]), ("bo", [1, D]), ("gb", [1, TL]),
        ("CT", [TL, NT * 16]), ("ST", [TL, NT * 16]),
        ("CT2", [TL, NT * 16]), ("ST2", [TL, NT * 16]),
        ("CK", [TL, 96 * 16]), ("SK", [TL, 96 * 16]),
        ("K128", [TL, TL]), ("NEGM", [TL, TL]),
        ("K0", [TL, TL]), ("SURR0", [TL, TL]), ("BANDM", [TL, 1024]),
    ]:
        win[name] = nc.dram_tensor(name, shape, F32, kind="ExternalInput").ap()
    ident_d = nc.dram_tensor("identw", [TL, TL], BF16, kind="ExternalInput").ap()
    kvtab_d = nc.dram_tensor("kvtab", [L, ROW], BF16, kind="Internal").ap()
    xnt_d = nc.dram_tensor("xnt", [NB, NT, TL, 8 * TL], BF16, kind="Internal").ap()
    part_d = nc.dram_tensor("part", [NB, L, D], F32, kind="Internal").ap()

    cur = [16448]

    def alloc(name, shape, dt):
        per = int(np.prod(shape[1:])) * mybir.dt.size(dt)
        per = (per + 31) // 32 * 32
        off = cur[0]
        cur[0] = off + per
        assert cur[0] <= _PB - 64, f"sbuf overflow at {name}"
        return nc.alloc_sbuf_tensor_at(name, shape, dt, offset=off)

    ident = alloc("ident", [TL, TL], BF16)
    ones_r = alloc("ones_r", [1, TL], BF16)
    cstc = alloc("cstc", [TL, 8], F32)
    spc = alloc("spc", [TL, 8], F32)
    spc2 = alloc("spc2", [TL, 8], F32)
    spc3 = alloc("spc3", [TL, 8], F32)

    a0 = cur[0]
    CT_s = alloc("CT_s", [TL, NT * 16], F32)
    ST_s = alloc("ST_s", [TL, NT * 16], F32)
    CT2_s = alloc("CT2_s", [TL, NT * 16], F32)
    ST2_s = alloc("ST2_s", [TL, NT * 16], F32)
    CK_s = alloc("CK_s", [TL, 96 * 16], BF16)
    SK_s = alloc("SK_s", [TL, 96 * 16], BF16)
    BANDM_s = alloc("BANDM_s", [TL, 1024], F32)
    iw_s = alloc("iw_s", [TL, NIH], F32)
    K128s = alloc("K128s", [TL, TL], F32)
    NEGMs = alloc("NEGMs", [TL, TL], F32)
    K0s = alloc("K0s", [TL, TL], F32)
    SURR0s = alloc("SURR0s", [TL, TL], F32)
    wA = {}
    for name, shape in [
        ("wdkv", [TL, 8, 256]), ("wuk", [TL, 2, 512]), ("wuv", [TL, 2, 512]),
        ("wdq", [TL, 8, 512]), ("wuq", [TL, 4, 512]), ("wqr", [TL, 4, 256]),
        ("wkr", [TL, 8, DR]), ("wout", [TL, 4, D]), ("iwq", [TL, 8, 256]),
        ("iwk", [TL, 8, 64]),
    ]:
        wA[name] = alloc("A_" + name, shape, BF16)
    kiT = alloc("kiT", [64, L], BF16)
    G = alloc("G", [TL, GSL, ROW], BF16)
    prods = alloc("prods", [TL, NH * GSL * DH], BF16)
    I_a = alloc("I_a", [TL, L], F32)
    I_b = alloc("I_b", [TL, L], F32)
    bscore = I_b
    kcb = alloc("kcb", [TL, 2, 512], BF16)
    VSb = alloc("VSb", [TL, 2, 512], BF16)
    kmub = alloc("kmub", [TL, 2, DR], BF16)
    rotb = alloc("rotb", [TL, TL], BF16)
    kcTb = alloc("kcTb", [TL, 4, 256], BF16)
    krTt = alloc("krTt", [96, 2, TL], BF16)
    qcT = alloc("qcT", [TL, 4, TL], BF16)
    qropeT = alloc("qropeT", [96, NH, TL], BF16)
    qropeb = alloc("qropeb", [TL, NH, TL], BF16)
    battn = alloc("battn", [TL, L], BF16)
    batT = alloc("batT", [TL, 2, TL], BF16)
    x_s = alloc("x_s", [TL, D], F32)
    xn_s = I_a[:, 0:D]
    xnbf = alloc("xnbf", [TL, D], BF16)
    xnT = alloc("xnT", [TL, 8, TL], BF16)
    tab = alloc("tab", [TL, ROW], BF16)
    ckvT = alloc("ckvT", [TL, 2, TL], BF16)
    cqT = alloc("cqT", [TL, 4, TL], BF16)
    qiT = alloc("qiT", [64, NIH, TL], BF16)
    qc = alloc("qc", [TL, 512], BF16)
    qr = alloc("qr", [TL, 256], F32)
    mu = alloc("mu", [TL, 256], F32)
    qt1 = alloc("qt1", [TL, 128], BF16)
    qt2 = alloc("qt2", [TL, 128], BF16)
    kt1 = alloc("kt1", [TL, GSL * 16], BF16)
    kt2 = alloc("kt2", [TL, GSL * 16], BF16)
    ktt = alloc("ktt", [TL, GSL * 16], BF16)
    rp = battn
    rp2 = alloc("rp2", [TL, NH * GSL * 16], BF16)
    Ssc = alloc("Ssc", [TL, NH * GSL], F32)
    Ss2 = alloc("Ss2", [TL, NH * GSL], F32)
    ebuf = alloc("ebuf", [TL, NH * GSL], BF16)
    mrun = alloc("mrun", [TL, NH], F32)
    mnew = alloc("mnew", [TL, NH], F32)
    alph = alloc("alph", [TL, NH], F32)
    dnm = alloc("dnm", [TL, NH], F32)
    rpart = alloc("rpart", [TL, NH], F32)
    Aacc = alloc("Aacc", [TL, 512], F32)
    avp = alloc("avp", [TL, 512], F32)
    attnbf = alloc("attnbf", [TL, 512], BF16)
    attnT = alloc("attnT", [TL, 4, TL], BF16)
    part_s = alloc("part_s", [TL, D], F32)
    idx96 = alloc("idx96", [TL, 96], U32)
    v8 = alloc("v8", [TL, 8], F32)
    col1 = alloc("col1", [TL, 1], F32)
    col2 = alloc("col2", [TL, 1], F32)
    w1t = alloc("w1t", [TL, 512], F32)
    assert cur[0] <= _PB - 64

    cur[0] = a0
    wiT_s = alloc("wiT_s", [TL, 8, DFF // 2], BF16)
    woT_s = alloc("woT_s", [TL, 16, D], BF16)
    BT_s = alloc("BT_s", [16, NE, DFF // 2], BF16)
    AT_s = alloc("AT_s", [TL, 8, NE * TL], BF16)
    gwT_s = alloc("gwT_s", [TL, 8, TL], BF16)
    bi_s = alloc("bi_s", [1, DFF // 2], BF16)
    bo_s = alloc("bo_s", [1, D], BF16)
    gbr = alloc("gbr", [1, TL], BF16)
    Hacc = alloc("Hacc", [TL, DFF // 2], F32)
    stg = Hacc
    base_s = alloc("base_s", [TL, DFF // 2], BF16)
    lorab = alloc("lorab", [TL, 1024], BF16)
    sadd = alloc("sadd", [TL, 1024], BF16)
    Hbf = alloc("Hbf", [TL, DFF // 2], BF16)
    HT = alloc("HT", [TL, 16, TL], BF16)
    xnT_b = alloc("xnT_b", [TL, 8, TL], BF16)
    partb = alloc("partb", [TL, D], F32)
    outb = alloc("outb", [TL, D], F32)
    la_s = alloc("la_s", [16, NE * TL], BF16)
    probs = alloc("probs", [TL, NE], F32)
    gates = alloc("gates", [TL, NE], F32)
    dcol = alloc("dcol", [TL, NE], F32)
    c1 = alloc("c1", [TL, 1], F32)
    c2 = alloc("c2", [TL, 1], F32)
    c3 = alloc("c3", [TL, 1], F32)
    gpk = alloc("gpk", [TL, 16], BF16)
    gsr = alloc("gsr", [1, TL], BF16)
    gates_all = alloc("gates_all", [TL, NB * NT * NE], F32)
    c3_all = alloc("c3_all", [TL, NB * NT], F32)
    stg2 = alloc("stg2", [TL, DFF // 2], F32)
    assert cur[0] <= _PB - 64

    prog = []

    class Ctr:
        __slots__ = ("sem", "n")

        def __init__(self, sem):
            self.sem, self.n = sem, 0

    def E(ek, fn):
        prog.append((ek, fn))

    def EI(ek, ctr, fn, k=1):
        selfw = ek in ("ve", "sc", "gp")
        n_after = ctr.n + k

        def go(eng, fn=fn, s=ctr.sem, k=k, selfw=selfw, n_after=n_after):
            fn(eng).then_inc(s, k)
            if selfw:
                eng.wait_ge(s, n_after)
        prog.append((ek, go))
        ctr.n += k

    def W(ek, ctr, val=None):
        v = ctr.n if val is None else val
        prog.append((ek, lambda eng, s=ctr.sem, v=v: eng.wait_ge(s, v)))

    def Wsp(ek, ctr):
        W(ek, ctr)
        if ek == 've':
            EI('ve', sv, lambda e: e.memset(spc[:, 0:8], 0.0))
        elif ek == 'sc':
            EI('sc', sc, lambda e: e.memzero(spc2[:, 0:8]))

    stack = contextlib.ExitStack()
    d = Ctr(stack.enter_context(nc.semaphore("d")))
    g = Ctr(stack.enter_context(nc.semaphore("g")))
    sv = Ctr(stack.enter_context(nc.semaphore("sv")))
    sc = Ctr(stack.enter_context(nc.semaphore("sc")))
    stc = Ctr(stack.enter_context(nc.semaphore("stc")))
    sg = Ctr(stack.enter_context(nc.semaphore("sg")))

    class Bank:
        def __init__(self, name, shape, dt):
            self.t = nc.alloc_psum_tensor(name, shape, dt)
            self.rel = None

        def guard(self):
            if self.rel is not None:
                W('te', self.rel[0], self.rel[1])

        def done(self, ctr):
            self.rel = (ctr, ctr.n)

    pa0 = Bank("pa0", [TL, 512], F32)
    pa1 = Bank("pa1", [TL, 512], F32)
    pc0 = Bank("pc0", [TL, 1024], F32)
    pc1 = Bank("pc1", [TL, 1024], F32)
    ptr0 = Bank("ptr0", [TL, TL], BF16)
    ptr1 = Bank("ptr1", [TL, TL], BF16)

    def mmgroup(bank, sl, mms, evacs):
        bank.guard()
        n = len(mms)
        for i, (lh, rh) in enumerate(mms):
            fn = (lambda e, lh=lh, rh=rh, s=(i == 0), p=(i == n - 1), b=bank, sl=sl:
                  e.matmul(b.t[sl], lh, rh, start=s, stop=p))
            if i == n - 1:
                EI('te', stc, fn)
            else:
                E('te', fn)
        last = None
        for ek, ctr, fn in evacs:
            W(ek, stc)
            EI(ek, ctr, fn)
            last = ctr
        if last is not None:
            bank.done(last)

    tr_i = [0]

    def transpose_to(dst, src, r0=0, r1=TL):
        bank = (ptr0, ptr1)[tr_i[0] % 2]
        tr_i[0] += 1
        bank.guard()
        EI('te', stc, lambda e, b=bank, s=src: e.transpose(b.t[:], s, ident[:]))
        W('ve', stc)
        EI('ve', sv, lambda e, b=bank, dd=dst, r0=r0, r1=r1:
           e.tensor_copy(out=dd, in_=b.t[r0:r1, :]))
        bank.done(sv)

    EI('ve', sv, lambda e: e.memset(ones_r[:], 1.0))
    _consts = [EPS, 1.0 / D, -1.0, 1e24, 1e30]
    for ci, cval in enumerate(_consts):
        EI('ve', sv, lambda e, ci=ci, cval=cval: e.memset(cstc[:, ci:ci + 1], cval))
        nc.const_aps.register(F32, cval, cstc[:, ci:ci + 1])
    EI('sy', d, lambda e: e.dma_start(out=ident[:], in_=ident_d), 16)
    for nm, t in [("CT", CT_s), ("ST", ST_s), ("CT2", CT2_s), ("ST2", ST2_s),
                  ("iw", iw_s), ("BANDM", BANDM_s),
                  ("K128", K128s), ("NEGM", NEGMs), ("K0", K0s), ("SURR0", SURR0s)]:
        EI('sy', d, lambda e, a=win[nm], t=t: e.dma_start(out=t[:], in_=a), 16)
    for nm, t in [("CK", CK_s), ("SK", SK_s)]:
        EI('sy', d, lambda e, a=win[nm]: e.dma_start(out=I_a[:, 0:1536], in_=a), 16)
        W('ve', d)
        EI('ve', sv, lambda e, t=t: e.tensor_copy(out=t[:], in_=I_a[:, 0:1536]))
        W('sy', sv)

    for nm in ["wdkv", "wuk", "wuv", "wdq", "wuq", "wqr", "wkr", "wout", "iwq", "iwk"]:
        K, C = wA[nm].shape[1], wA[nm].shape[2]
        src3 = win[nm].rearrange("(k p) c -> k p c", p=TL)
        for k in range(K):
            EI('sy', d, lambda e, s=src3, k=k, C=C:
               e.dma_start(out=I_a[:, 0:C], in_=s[k]), 16)
            W('ve', d)
            EI('ve', sv, lambda e, nm=nm, k=k, C=C:
               e.tensor_copy(out=wA[nm][:, k, :], in_=I_a[:, 0:C]))
            W('sy', sv)

    def emit_ln():
        EI('ve', sv, lambda e: e.tensor_reduce(out=col1[:], in_=x_s[:], axis=AX.X, op=ALU.add))
        W('ve', sv)
        EI('ve', sv, lambda e: e.tensor_scalar(out=col1[:], in0=col1[:], scalar1=1.0 / D,
                                               scalar2=None, op0=ALU.mult))
        W('ve', sv)
        EI('ve', sv, lambda e: e.tensor_scalar(out=xn_s[:], in0=x_s[:], scalar1=col1[:],
                                               scalar2=None, op0=ALU.subtract))
        Wsp('sc', sv)
        EI('sc', sc, lambda e: e.activation(out=I_b[:, 0:D], in_=xn_s[:], func=AF.Square))
        Wsp('ve', sc)
        EI('ve', sv, lambda e: e.tensor_reduce(out=col2[:], in_=I_b[:, 0:D], axis=AX.X, op=ALU.add))
        Wsp('sc', sv)
        EI('sc', sc, lambda e: e.activation(out=col2[:], in_=col2[:], func=AF.Sqrt,
                                            scale=1.0 / D, bias=EPS))
        Wsp('ve', sc)
        EI('ve', sv, lambda e: e.reciprocal(out=col2[:], in_=col2[:]))
        W('ve', sv)
        EI('ve', sv, lambda e: e.tensor_scalar(out=xn_s[:], in0=xn_s[:], scalar1=col2[:],
                                               scalar2=None, op0=ALU.mult))
        EI('ve', sv, lambda e: e.tensor_copy(out=xnbf[:], in_=xn_s[:]))
        W('te', sv)
        for c in range(8):
            transpose_to(xnT[:, c, :], xnbf[:, c * TL:(c + 1) * TL])

    def keys_phase(b):
        for T in range(NT):
            T0 = T * TL
            W('sy', g)
            EI('sy', d, lambda e, b=b, T0=T0: e.dma_start(
                out=x_s[:], in_=x_d[b, T0:T0 + TL, :]), 16)
            W('ve', d)
            W('sc', sv)
            emit_ln()
            W('sy', sv)
            EI('sy', d, lambda e, b=b, T=T: e.dma_start(
                out=xnt_d[b, T], in_=xnT[:].rearrange("p c t -> p (c t)")), 16)
            for cc in range(2):
                mmgroup((pa0, pa1)[cc], np.s_[:, 0:TL],
                        [(wA["wdkv"][:, k, cc * TL:(cc + 1) * TL], xnT[:, k, :])
                         for k in range(8)],
                        [('ve', sv, (lambda e, cc=cc, bk=(pa0, pa1)[cc]: e.tensor_copy(
                            out=ckvT[:, cc, :], in_=bk.t[:, 0:TL])))])
            W('te', sv)
            mmgroup(pc0, np.s_[:, 0:512],
                    [(ckvT[:, c, :], wA["wuk"][:, c, :]) for c in range(2)],
                    [('ve', sv, lambda e: e.tensor_copy(out=tab[:, 0:512],
                                                        in_=pc0.t[:, 0:512]))])
            mmgroup(pc1, np.s_[:, 0:512],
                    [(ckvT[:, c, :], wA["wuv"][:, c, :]) for c in range(2)],
                    [('ve', sv, lambda e: e.tensor_copy(out=tab[:, 512:1024],
                                                        in_=pc1.t[:, 0:512]))])
            mmgroup(pa0, np.s_[:, 0:DR],
                    [(xnT[:, k, :], wA["wkr"][:, k, :]) for k in range(8)],
                    [('ve', sv, lambda e: e.tensor_copy(out=w1t[:, 0:DR],
                                                        in_=pa0.t[:, 0:DR]))])
            Wsp('sc', sv)
            EI('sc', sc, lambda e: e.activation(out=w1t[:, 64:64 + DR], in_=w1t[:, 0:DR],
                                                func=AF.Abs))
            EI('sc', sc, lambda e: e.activation(out=w1t[:, 64:64 + DR], in_=w1t[:, 64:64 + DR],
                                                func=AF.Exp, scale=-1.0))
            EI('sc', sc, lambda e: e.activation(out=w1t[:, 64:64 + DR], in_=w1t[:, 64:64 + DR],
                                                func=AF.Ln, bias=1.0))
            EI('sc', sc, lambda e: e.activation(out=w1t[:, 128:128 + DR], in_=w1t[:, 0:DR],
                                                func=AF.Relu))
            Wsp('ve', sc)
            EI('ve', sv, lambda e: e.tensor_tensor(out=tab[:, 1024:1056],
                                                   in0=w1t[:, 64:64 + DR],
                                                   in1=w1t[:, 128:128 + DR], op=ALU.add))
            mmgroup(pa1, np.s_[0:64, 0:TL],
                    [(wA["iwk"][:, k, :], xnT[:, k, :]) for k in range(8)],
                    [('ve', sv, (lambda e, T0=T0: e.tensor_copy(
                        out=kiT[0:64, T0:T0 + TL], in_=pa1.t[0:64, 0:TL])))])
            W('sy', sv)
            EI('sy', d, lambda e, T0=T0: e.dma_start(
                out=kvtab_d[T0:T0 + TL, :], in_=tab[:]), 16)

    def emit_gathers(k0):
        W('gp', sv)
        W('gp', d)
        for j in range(GSL):
            EI('gp', g, (lambda e, j=j, k0=k0: e.indirect_dma_start(
                out=G[:, j, :], out_offset=None, in_=kvtab_d,
                in_offset=IndirectOffsetOnAxis(idx96[:, k0 + j:k0 + j + 1], 0))), 16)

    def attn_phase(b):
        for T in range(NT):
            T0 = T * TL
            EI('sy', d, lambda e, b=b, T=T: e.dma_start(
                out=xnT[:].rearrange("p c t -> p (c t)"), in_=xnt_d[b, T]), 16)
            EI('sy', d, lambda e, b=b, T0=T0: e.dma_start(
                out=x_s[:], in_=x_d[b, T0:T0 + TL, :]), 16)
            if T > 0:
                W('sy', stc)
                EI('sy', d, (lambda e, T0=T0: e.dma_start(
                    out=kcb[:],
                    in_=kvtab_d[T0 - TL:T0 + TL, 0:512].rearrange(
                        "(j p) c -> p j c", p=TL))), 16)
                EI('sy', d, (lambda e, T0=T0: e.dma_start(
                    out=VSb[:],
                    in_=kvtab_d[T0 - TL:T0 + TL, 512:1024].rearrange(
                        "(j p) c -> p j c", p=TL))), 16)
                EI('sy', d, (lambda e, T0=T0: e.dma_start(
                    out=kmub[:],
                    in_=kvtab_d[T0 - TL:T0 + TL, 1024:1056].rearrange(
                        "(j p) c -> p j c", p=TL))), 16)
            W('te', d)
            W('ve', d)
            W('ve', sg)
            W('sc', sv)
            W('gp', sv)
            for h in range(NIH):
                mmgroup((pa0, pa1)[h % 2], np.s_[0:64, 0:TL],
                        [(wA["iwq"][:, k, h * 64:(h + 1) * 64], xnT[:, k, :])
                         for k in range(8)],
                        [('ve', sv, (lambda e, h=h, bk=(pa0, pa1)[h % 2]: e.tensor_copy(
                            out=qiT[0:64, h, :], in_=bk.t[0:64, 0:TL])))])
            W('te', sv)
            Wt = T0 + 64 if T > 0 else TL
            nch = (Wt + 511) // 512
            for j in range(nch):
                w = min(512, Wt - j * 512)
                for h in range(NIH):
                    bk = (pa0, pa1)[(j * NIH + h) % 2]
                    lh = qiT[0:64, h, :]
                    rh = kiT[0:64, j * 512:j * 512 + w]
                    if h == 0:
                        mmgroup(bk, np.s_[:, 0:w], [(lh, rh)],
                                [('ve', sv, (lambda e, bk=bk, w=w, j=j: e.tensor_scalar(
                                    out=I_a[:, j * 512:j * 512 + w], in0=bk.t[:, 0:w],
                                    scalar1=0.0, scalar2=iw_s[:, 0:1],
                                    op0=ALU.max, op1=ALU.mult)))])
                    else:
                        W('sc', sv)
                        mmgroup(bk, np.s_[:, 0:w], [(lh, rh)],
                                [('sc', sc, (lambda e, bk=bk, w=w: e.activation(
                                    out=I_b[:, 0:w], in_=bk.t[:, 0:w], func=AF.Relu)))])
                        Wsp('ve', sc)
                        EI('ve', sv, (lambda e, w=w, j=j, h=h: e.scalar_tensor_tensor(
                            out=I_a[:, j * 512:j * 512 + w], in0=I_b[:, 0:w],
                            scalar=iw_s[:, h:h + 1], in1=I_a[:, j * 512:j * 512 + w],
                            op0=ALU.mult, op1=ALU.add)))
            if T > 0:
                EI('ve', sv, (lambda e, T0=T0: e.tensor_tensor(
                    out=I_a[:, T0 - 64:T0 + 64], in0=I_a[:, T0 - 64:T0 + 64],
                    in1=K128s[:], op=ALU.mult)))
                EI('ve', sv, (lambda e, T0=T0: e.tensor_tensor(
                    out=I_a[:, T0 - 64:T0 + 64], in0=I_a[:, T0 - 64:T0 + 64],
                    in1=NEGMs[:], op=ALU.add)))
                nr, off = 4, 64
            else:
                EI('ve', sv, lambda e: e.tensor_tensor(out=I_a[:, 0:TL], in0=I_a[:, 0:TL],
                                                       in1=K0s[:], op=ALU.mult))
                EI('ve', sv, lambda e: e.tensor_tensor(out=I_a[:, 0:TL], in0=I_a[:, 0:TL],
                                                       in1=SURR0s[:], op=ALU.add))
                nr, off = 12, 0
            for r in range(nr):
                cur_t = (I_a, I_b)[r % 2]
                oth_t = (I_b, I_a)[r % 2]
                EI('ve', sv, (lambda e, c=cur_t, Wt=Wt: e.max(out=v8[:], in_=c[:, 0:Wt])))
                W('ve', sv)
                EI('ve', sv, (lambda e, c=cur_t, Wt=Wt, r=r, off=off: e.max_index(
                    out=idx96[:, off + r * 8:off + r * 8 + 8],
                    in_max=v8[:], in_values=c[:, 0:Wt])))
                if r < nr - 1:
                    EI('ve', sv, (lambda e, c=cur_t, o=oth_t, Wt=Wt: e.match_replace(
                        out=o[:, 0:Wt], in_to_replace=v8[:], in_values=c[:, 0:Wt],
                        imm_value=NEG)))
            emit_gathers(64 if T > 0 else 0)
            EI('ve', sv, lambda e: e.memset(mrun[:], NEG))
            EI('ve', sv, lambda e: e.memset(dnm[:], 0.0))
            EI('ve', sv, lambda e: e.memset(Aacc[:], 0.0))
            for cc in range(4):
                mmgroup((pa0, pa1)[cc % 2], np.s_[:, 0:TL],
                        [(wA["wdq"][:, k, cc * TL:(cc + 1) * TL], xnT[:, k, :])
                         for k in range(8)],
                        [('ve', sv, (lambda e, cc=cc, bk=(pa0, pa1)[cc % 2]: e.tensor_copy(
                            out=cqT[:, cc, :], in_=bk.t[:, 0:TL])))])
            W('te', sv)
            mmgroup(pc0, np.s_[:, 0:512],
                    [(cqT[:, c, :], wA["wuq"][:, c, :]) for c in range(4)],
                    [('ve', sv, lambda e: e.tensor_copy(out=qc[:], in_=pc0.t[:, 0:512]))])
            mmgroup(pc1, np.s_[:, 0:256],
                    [(cqT[:, c, :], wA["wqr"][:, c, :]) for c in range(4)],
                    [('ve', sv, lambda e: e.tensor_copy(out=qr[:], in_=pc1.t[:, 0:256]))])
            Wsp('sc', sv)
            EI('sc', sc, lambda e: e.activation(out=mu[:], in_=qr[:], func=AF.Abs))
            EI('sc', sc, lambda e: e.activation(out=mu[:], in_=mu[:], func=AF.Exp, scale=-1.0))
            EI('sc', sc, lambda e: e.activation(out=mu[:], in_=mu[:], func=AF.Ln, bias=1.0))
            EI('sc', sc, lambda e: e.activation(out=qr[:], in_=qr[:], func=AF.Relu))
            Wsp('ve', sc)
            EI('ve', sv, lambda e: e.tensor_tensor(out=mu[:], in0=mu[:], in1=qr[:], op=ALU.add))
            mu_v = mu[:].rearrange("p (h two i) -> p h two i", h=NH, two=2)
            ct_v = CT_s[:, T * 16:(T + 1) * 16].rearrange("p (o i) -> p o i", o=1).to_broadcast([TL, NH, 16])
            st_v = ST_s[:, T * 16:(T + 1) * 16].rearrange("p (o i) -> p o i", o=1).to_broadcast([TL, NH, 16])
            q1v = qt1[:].rearrange("p (h i) -> p h i", h=NH)
            q2v = qt2[:].rearrange("p (h i) -> p h i", h=NH)
            tmpv = rp[:, 0:NH * 16].rearrange("p (h i) -> p h i", h=NH)
            EI('ve', sv, lambda e, a=mu_v, b=ct_v, o=q1v: e.tensor_tensor(out=o, in0=a[:, :, 0, :], in1=b, op=ALU.mult))
            EI('ve', sv, lambda e, a=mu_v, b=st_v, o=tmpv: e.tensor_tensor(out=o, in0=a[:, :, 1, :], in1=b, op=ALU.mult))
            EI('ve', sv, lambda e, o=q1v, b=tmpv: e.tensor_tensor(out=o, in0=o, in1=b, op=ALU.subtract))
            EI('ve', sv, lambda e, a=mu_v, b=st_v, o=q2v: e.tensor_tensor(out=o, in0=a[:, :, 0, :], in1=b, op=ALU.mult))
            EI('ve', sv, lambda e, a=mu_v, b=ct_v, o=tmpv: e.tensor_tensor(out=o, in0=a[:, :, 1, :], in1=b, op=ALU.mult))
            EI('ve', sv, lambda e, o=q2v, b=tmpv: e.tensor_tensor(out=o, in0=o, in1=b, op=ALU.add))
            if T > 0:
                W('te', sv)
                for c in range(4):
                    transpose_to(qcT[:, c, :], qc[:, c * TL:(c + 1) * TL])
                ct2_v = CT2_s[:, T * 16:(T + 1) * 16].rearrange("p (o i) -> p o i", o=1).to_broadcast([TL, NH, 16])
                st2_v = ST2_s[:, T * 16:(T + 1) * 16].rearrange("p (o i) -> p o i", o=1).to_broadcast([TL, NH, 16])
                s1v = Ssc[:].rearrange("p (h i) -> p h i", h=NH)
                s2v = Ss2[:].rearrange("p (h i) -> p h i", h=NH)
                qb1 = qropeb[:, :, 64:80]
                qb2 = qropeb[:, :, 80:96]
                EI('ve', sv, lambda e, a=mu_v, b=ct2_v, o=s1v: e.tensor_tensor(out=o, in0=a[:, :, 0, :], in1=b, op=ALU.mult))
                EI('ve', sv, lambda e, a=mu_v, b=st2_v, o=s2v: e.tensor_tensor(out=o, in0=a[:, :, 1, :], in1=b, op=ALU.mult))
                EI('ve', sv, lambda e, o=qb1, x=s1v, y=s2v: e.tensor_tensor(out=o, in0=x, in1=y, op=ALU.subtract))
                EI('ve', sv, lambda e, a=mu_v, b=st2_v, o=s1v: e.tensor_tensor(out=o, in0=a[:, :, 0, :], in1=b, op=ALU.mult))
                EI('ve', sv, lambda e, a=mu_v, b=ct2_v, o=s2v: e.tensor_tensor(out=o, in0=a[:, :, 1, :], in1=b, op=ALU.mult))
                EI('ve', sv, lambda e, o=qb2, x=s1v, y=s2v: e.tensor_tensor(out=o, in0=x, in1=y, op=ALU.add))
                W('te', sv)
                for h in range(NH):
                    transpose_to(qropeT[64:96, h, :], qropeb[:, h, :], r0=64, r1=96)
            qc_v = qc[:].rearrange("p (h o dd) -> p h o dd", h=NH, o=1).to_broadcast([TL, NH, GSL, DH])
            gk_v = G[:, :, 0:512].rearrange("p k (h dd) -> p h k dd", h=NH)
            pr_v = prods[:].rearrange("p (h k dd) -> p h k dd", h=NH, k=GSL)
            nu1 = G[:, :, 1024:1040]
            nu2 = G[:, :, 1040:1056]
            k1v = kt1[:].rearrange("p (k i) -> p k i", i=16)
            k2v = kt2[:].rearrange("p (k i) -> p k i", i=16)
            ktv = ktt[:].rearrange("p (k i) -> p k i", i=16)
            q1b = qt1[:].rearrange("p (h o i) -> p h o i", h=NH, o=1).to_broadcast([TL, NH, GSL, 16])
            q2b = qt2[:].rearrange("p (h o i) -> p h o i", h=NH, o=1).to_broadcast([TL, NH, GSL, 16])
            k1b = kt1[:].rearrange("p (o k i) -> p o k i", o=1, i=16).to_broadcast([TL, NH, GSL, 16])
            k2b = kt2[:].rearrange("p (o k i) -> p o k i", o=1, i=16).to_broadcast([TL, NH, GSL, 16])
            rpv = rp[:].rearrange("p (h k i) -> p h k i", h=NH, k=GSL)
            rp2v = rp2[:].rearrange("p (h k i) -> p h k i", h=NH, k=GSL)
            Sv = Ssc[:].rearrange("p (h k) -> p h k", h=NH)
            S2v = Ss2[:].rearrange("p (h k) -> p h k", h=NH)
            ev = ebuf[:].rearrange("p (h k) -> p h k", h=NH)
            mb = mnew[:].rearrange("p (h o) -> p h o", o=1).to_broadcast([TL, NH, GSL])
            ab = alph[:].rearrange("p (h o) -> p h o", o=1).to_broadcast([TL, NH, DH])
            Av = Aacc[:].rearrange("p (h dd) -> p h dd", h=NH)
            eb2 = ebuf[:].rearrange("p (h o k) -> p h o k", h=NH, o=1).to_broadcast([TL, NH, DH, GSL])
            gv_v = G[:, :, 512:1024].rearrange("p k (h dd) -> p h dd k", h=NH)
            pr2 = prods[:].rearrange("p (h dd k) -> p h dd k", h=NH, dd=DH)
            avv = avp[:].rearrange("p (h dd) -> p h dd", h=NH)
            def emit_sweep(k0, pre):
                ckv_ = CK_s[:].rearrange("p (k i) -> p k i", i=16)[:, k0:k0 + GSL, :]
                skv_ = SK_s[:].rearrange("p (k i) -> p k i", i=16)[:, k0:k0 + GSL, :]
                sv_snap = sv.n
                W('gp', g)
                W('gp', sv, sv_snap)
                EI('gp', sg, lambda e: e.memset(spc3[:, 0:8], 0.0))
                EI('gp', sg, lambda e, ckv_=ckv_: e.tensor_tensor(out=k1v, in0=nu1, in1=ckv_, op=ALU.mult))
                EI('gp', sg, lambda e, skv_=skv_: e.tensor_tensor(out=ktv, in0=nu2, in1=skv_, op=ALU.mult))
                EI('gp', sg, lambda e: e.tensor_tensor(out=k1v, in0=k1v, in1=ktv, op=ALU.subtract))
                EI('gp', sg, lambda e, skv_=skv_: e.tensor_tensor(out=k2v, in0=nu1, in1=skv_, op=ALU.mult))
                EI('gp', sg, lambda e, ckv_=ckv_: e.tensor_tensor(out=ktv, in0=nu2, in1=ckv_, op=ALU.mult))
                EI('gp', sg, lambda e: e.tensor_tensor(out=k2v, in0=k2v, in1=ktv, op=ALU.add))
                EI('gp', sg, lambda e: e.tensor_tensor(out=rpv, in0=q1b, in1=k1b, op=ALU.mult))
                EI('gp', sg, lambda e: e.tensor_tensor(out=rp2v, in0=q2b, in1=k2b, op=ALU.mult))
                EI('gp', sg, lambda e: e.tensor_tensor(out=rpv, in0=rpv, in1=rp2v, op=ALU.add))
                sg_snap = sg.n
                W('ve', g)
                EI('ve', sv, lambda e: e.tensor_tensor(out=pr_v, in0=qc_v, in1=gk_v, op=ALU.mult))
                EI('ve', sv, lambda e: e.tensor_reduce(out=Sv, in_=pr_v, axis=AX.X, op=ALU.add))
                W('ve', sg, sg_snap)
                EI('ve', sv, lambda e: e.memset(spc[:, 0:8], 0.0))
                EI('ve', sv, lambda e: e.tensor_reduce(out=S2v, in_=rpv, axis=AX.X, op=ALU.add))
                EI('ve', sv, lambda e: e.tensor_tensor(out=Ssc[:], in0=Ssc[:], in1=Ss2[:], op=ALU.add))
                EI('ve', sv, lambda e: e.tensor_scalar(out=Ssc[:], in0=Ssc[:], scalar1=SCALE,
                                                       scalar2=None, op0=ALU.mult))
                EI('ve', sv, lambda e: e.tensor_reduce(out=rpart[:], in_=Sv, axis=AX.X, op=ALU.max))
                EI('ve', sv, lambda e: e.tensor_tensor(out=mnew[:], in0=mrun[:], in1=rpart[:], op=ALU.max))
                EI('ve', sv, lambda e: e.tensor_tensor(out=alph[:], in0=mrun[:], in1=mnew[:], op=ALU.subtract))
                Wsp('sc', sv)
                EI('sc', sc, lambda e: e.activation(out=alph[:], in_=alph[:], func=AF.Exp))
                EI('ve', sv, lambda e: e.tensor_tensor(out=Sv, in0=Sv, in1=mb, op=ALU.subtract))
                Wsp('sc', sv)
                EI('sc', sc, lambda e: e.activation(out=ebuf[:], in_=Ssc[:], func=AF.Exp))
                Wsp('ve', sc)
                EI('ve', sv, lambda e: e.tensor_reduce(out=rpart[:], in_=ev, axis=AX.X, op=ALU.add))
                EI('ve', sv, lambda e: e.tensor_tensor(out=dnm[:], in0=dnm[:], in1=alph[:], op=ALU.mult))
                EI('ve', sv, lambda e: e.tensor_tensor(out=dnm[:], in0=dnm[:], in1=rpart[:], op=ALU.add))
                EI('ve', sv, lambda e: e.tensor_copy(out=mrun[:], in_=mnew[:]))
                EI('ve', sv, lambda e: e.tensor_tensor(out=Av, in0=Av, in1=ab, op=ALU.mult))
                EI('ve', sv, lambda e: e.tensor_tensor(out=pr2, in0=eb2, in1=gv_v, op=ALU.mult))
                if pre is not None:
                    emit_gathers(pre)
                EI('ve', sv, lambda e: e.tensor_reduce(out=avv, in_=pr2, axis=AX.X, op=ALU.add))
                EI('ve', sv, lambda e: e.tensor_tensor(out=Aacc[:], in0=Aacc[:], in1=avp[:], op=ALU.add))

            if T > 0:
                for j in range(2):
                    for cc in range(4):
                        transpose_to(kcTb[:, cc, j * TL:(j + 1) * TL],
                                     kcb[:, j, cc * TL:(cc + 1) * TL])
                for j in range(2):
                    ctk = CT_s[:, (T - 1 + j) * 16:(T + j) * 16]
                    stk = ST_s[:, (T - 1 + j) * 16:(T + j) * 16]
                    m1 = kmub[:, j, 0:16]
                    m2 = kmub[:, j, 16:32]
                    if j == 1:
                        W('ve', stc)
                    EI('ve', sv, lambda e, m1=m1, c=ctk: e.tensor_tensor(out=Ssc[:, 0:16], in0=m1, in1=c, op=ALU.mult))
                    EI('ve', sv, lambda e, m2=m2, s=stk: e.tensor_tensor(out=Ss2[:, 0:16], in0=m2, in1=s, op=ALU.mult))
                    EI('ve', sv, lambda e: e.tensor_tensor(out=rotb[:, 64:80], in0=Ssc[:, 0:16], in1=Ss2[:, 0:16], op=ALU.subtract))
                    EI('ve', sv, lambda e, m1=m1, s=stk: e.tensor_tensor(out=Ssc[:, 0:16], in0=m1, in1=s, op=ALU.mult))
                    EI('ve', sv, lambda e, m2=m2, c=ctk: e.tensor_tensor(out=Ss2[:, 0:16], in0=m2, in1=c, op=ALU.mult))
                    EI('ve', sv, lambda e: e.tensor_tensor(out=rotb[:, 80:96], in0=Ssc[:, 0:16], in1=Ss2[:, 0:16], op=ALU.add))
                    W('te', sv)
                    transpose_to(krTt[64:96, j, :], rotb[:], r0=64, r1=96)
                W('te', sv)
                krf = krTt[64:96, :, :].rearrange("p j t -> p (j t)")
                emit_sweep(64, 80)
                for h in range(NH):
                    bk = (pc0, pc1)[h // 4]
                    col = (h % 4) * 256
                    hb = 64 * (h % 2)
                    mms = [(qcT[hb:hb + 64, h // 2, :],
                            kcTb[hb:hb + 64, h // 2, :]),
                           (qropeT[64:96, h, :], krf)]
                    ev_ = []
                    if h % 4 == 3:
                        ev_ = [('ve', sv, (lambda e, bk=bk, q=h // 4: e.scalar_tensor_tensor(
                            out=bscore[:, q * 1024:(q + 1) * 1024], in0=bk.t[:, 0:1024],
                            scalar=SCALE, in1=BANDM_s[:], op0=ALU.mult, op1=ALU.add)))]
                    mmgroup(bk, np.s_[:, col:col + 256], mms, ev_)
                bs8 = bscore[:].rearrange("p (h u) -> p h u", h=NH)
                mbn256 = mnew[:].rearrange("p (h o) -> p h o", o=1).to_broadcast([TL, NH, 256])
                EI('ve', sv, lambda e, bs8=bs8: e.tensor_reduce(out=rpart[:], in_=bs8, axis=AX.X, op=ALU.max))
                EI('ve', sv, lambda e: e.tensor_tensor(out=mnew[:], in0=mrun[:], in1=rpart[:], op=ALU.max))
                EI('ve', sv, lambda e: e.tensor_tensor(out=alph[:], in0=mrun[:], in1=mnew[:], op=ALU.subtract))
                Wsp('sc', sv)
                EI('sc', sc, lambda e: e.activation(out=alph[:], in_=alph[:], func=AF.Exp))
                EI('ve', sv, lambda e, bs8=bs8, mbn256=mbn256: e.tensor_tensor(out=bs8, in0=bs8, in1=mbn256, op=ALU.subtract))
                Wsp('sc', sv)
                EI('sc', sc, lambda e: e.activation(out=bscore[:], in_=bscore[:], func=AF.Exp))
                Wsp('ve', sc)
                EI('ve', sv, lambda e, bs8=bs8: e.tensor_reduce(out=rpart[:], in_=bs8, axis=AX.X, op=ALU.add))
                EI('ve', sv, lambda e: e.tensor_tensor(out=dnm[:], in0=dnm[:], in1=alph[:], op=ALU.mult))
                EI('ve', sv, lambda e: e.tensor_tensor(out=dnm[:], in0=dnm[:], in1=rpart[:], op=ALU.add))
                EI('ve', sv, lambda e: e.tensor_copy(out=mrun[:], in_=mnew[:]))
                EI('ve', sv, lambda e: e.tensor_tensor(out=Av, in0=Av, in1=ab, op=ALU.mult))
                EI('ve', sv, lambda e: e.tensor_copy(out=battn[:], in_=bscore[:]))
                W('te', sv)
                pa0.guard()
                for h in range(NH):
                    for c in range(2):
                        transpose_to(batT[:, c, :], battn[:, 256 * h + 128 * c:256 * h + 128 * c + 128])
                    W('te', sv)
                    E('te', (lambda e, h=h: e.matmul(
                        pa0.t[:, 64 * h:64 * h + 64], batT[:, 0, :],
                        VSb[:, 0, 64 * h:64 * h + 64], start=True, stop=False)))
                    EI('te', stc, (lambda e, h=h: e.matmul(
                        pa0.t[:, 64 * h:64 * h + 64], batT[:, 1, :],
                        VSb[:, 1, 64 * h:64 * h + 64], start=False, stop=True)))
                W('ve', stc)
                EI('ve', sv, lambda e: e.tensor_tensor(out=Aacc[:], in0=pa0.t[:, 0:512],
                                                       in1=Aacc[:], op=ALU.add))
                pa0.done(sv)
                emit_sweep(80, None)
            else:
                for s in range(6):
                    emit_sweep(s * 16, (s + 1) * 16 if s < 5 else None)

            EI('ve', sv, lambda e: e.reciprocal(out=dnm[:], in_=dnm[:]))
            db = dnm[:].rearrange("p (h o) -> p h o", o=1).to_broadcast([TL, NH, DH])
            EI('ve', sv, lambda e: e.tensor_tensor(
                out=attnbf[:].rearrange("p (h dd) -> p h dd", h=NH), in0=Av, in1=db, op=ALU.mult))
            W('te', sv)
            for c in range(4):
                transpose_to(attnT[:, c, :], attnbf[:, c * TL:(c + 1) * TL])
            W('te', sv)
            for half in range(2):
                mmgroup(pc0, np.s_[:, half * 512:(half + 1) * 512],
                        [(attnT[:, c, :], wA["wout"][:, c, half * 512:(half + 1) * 512])
                         for c in range(4)], [])
            W('ve', stc)
            EI('ve', sv, lambda e: e.tensor_tensor(out=part_s[:], in0=pc0.t[:, 0:1024],
                                                   in1=x_s[:], op=ALU.add))
            pc0.done(sv)
            W('sy', sv)
            EI('sy', d, (lambda e, b=b, T0=T0: e.dma_start(
                out=part_d[b, T0:T0 + TL, :], in_=part_s[:])), 16)

    def ffn_phase(half):
        f0 = half * (DFF // 2)
        wiT3 = win["wiT"].rearrange("(k p) f -> k p f", p=TL)
        woT3 = win["woT"].rearrange("(k p) dd -> k p dd", p=TL)
        AT3 = win["AT"].rearrange("(k p) c -> k p c", p=TL)
        gwT3 = win["gwT"].rearrange("(k p) c -> k p c", p=TL)
        jobs = []
        for kc in range(8):
            jobs.append((wiT3[kc, :, f0:f0 + 2048], wiT_s[:, kc, :], TL, 2048))
        for kc in range(16):
            jobs.append((woT3[f0 // TL + kc, :, :], woT_s[:, kc, :], TL, 1024))
        for ee in range(NE):
            jobs.append((win["BT"][ee, :, f0:f0 + 2048], BT_s[0:16, ee, :], 16, 2048))
        jobs.append((win["bi"][:, f0:f0 + 2048], bi_s[0:1, 0:2048], 1, 2048))
        if half == 0:
            for kc in range(8):
                jobs.append((AT3[kc], AT_s[:, kc, :], TL, NE * TL))
            for kc in range(8):
                jobs.append((gwT3[kc], gwT_s[:, kc, :], TL, TL))
            jobs.append((win["bo"], bo_s[:], 1, D))
            jobs.append((win["gb"], gbr[:], 1, TL))
        bufs = (stg, stg2)
        ccnt, dcnt = [], []
        for i, (src, dst, p, c) in enumerate(jobs):
            bf = bufs[i % 2]
            if i >= 2:
                W('sy', sv, ccnt[i - 2])
            EI('sy', d, (lambda e, src=src, bf=bf, p=p, c=c: e.dma_start(
                out=bf[0:p, 0:c], in_=src)), 16)
            dcnt.append(d.n)
            W('ve', d, dcnt[i])
            EI('ve', sv, (lambda e, dst=dst, bf=bf, p=p, c=c: e.tensor_copy(
                out=dst, in_=bf[0:p, 0:c])))
            ccnt.append(sv.n)
        W('sy', sv)
        W('te', sv)
        W('sc', sv)
        W('gp', sv)

        for b in range(NB):
            for T in range(NT):
                T0 = T * TL
                EI('sy', d, (lambda e, b=b, T=T: e.dma_start(
                    out=xnT_b[:].rearrange("p c t -> p (c t)"), in_=xnt_d[b, T])), 16)
                EI('sy', d, (lambda e, b=b, T0=T0: e.dma_start(
                    out=partb[:], in_=part_d[b, T0:T0 + TL, :])), 16)
                W('te', d)
                W('ve', d)
                W('sc', sv)
                W('gp', sv)
                ti = b * NT + T
                gts = gates_all[:, ti * NE:(ti + 1) * NE]
                if half == 0:
                    mms = [(xnT_b[:, k, :], gwT_s[:, k, :]) for k in range(8)]
                    mms.append((ones_r[:], gbr[:]))
                    mmgroup(pa0, np.s_[:, 0:TL], mms,
                            [('ve', sv, lambda e: e.tensor_copy(out=probs[:], in_=pa0.t[:, 0:NE]))])
                    EI('ve', sv, lambda e: e.tensor_reduce(out=c1[:], in_=probs[:], axis=AX.X,
                                                           op=ALU.max, negate=True))
                    W('ve', sv)
                    EI('ve', sv, lambda e: e.tensor_scalar(out=probs[:], in0=probs[:], scalar1=c1[:],
                                                           scalar2=None, op0=ALU.add))
                    Wsp('sc', sv)
                    EI('sc', sc, lambda e: e.activation(out=probs[:], in_=probs[:], func=AF.Exp))
                    Wsp('ve', sc)
                    EI('ve', sv, lambda e: e.tensor_reduce(out=c1[:], in_=probs[:], axis=AX.X, op=ALU.add))
                    W('ve', sv)
                    EI('ve', sv, lambda e: e.reciprocal(out=c1[:], in_=c1[:]))
                    W('ve', sv)
                    EI('ve', sv, lambda e: e.tensor_scalar(out=probs[:], in0=probs[:], scalar1=c1[:],
                                                           scalar2=None, op0=ALU.mult))
                    EI('ve', sv, lambda e: e.tensor_reduce(out=c1[:], in_=probs[:], axis=AX.X, op=ALU.max))
                    W('ve', sv)
                    EI('ve', sv, lambda e: e.tensor_scalar(out=dcol[:], in0=probs[:], scalar1=c1[:],
                                                           scalar2=None, op0=ALU.is_equal))
                    EI('ve', sv, lambda e: e.scalar_tensor_tensor(out=dcol[:], in0=dcol[:], scalar=-2.0,
                                                                  in1=probs[:], op0=ALU.mult, op1=ALU.add))
                    EI('ve', sv, lambda e: e.tensor_reduce(out=c2[:], in_=dcol[:], axis=AX.X, op=ALU.max))
                    W('ve', sv)
                    EI('ve', sv, lambda e: e.tensor_tensor(out=c3[:], in0=c1[:], in1=c2[:], op=ALU.add))
                    W('ve', sv)
                    EI('ve', sv, lambda e, gts=gts: e.tensor_scalar(out=gts, in0=probs[:], scalar1=c2[:],
                                                                    scalar2=c3[:], op0=ALU.is_ge, op1=ALU.mult))
                    EI('ve', sv, lambda e, ti=ti: e.tensor_scalar(out=c3_all[:, ti:ti + 1], in0=c3[:], scalar1=2.0,
                                                                  scalar2=None, op0=ALU.mult))
                else:
                    EI('ve', sv, lambda e: e.memset(gpk[:], 0.0))
                    W('ve', sv)
                    EI('ve', sv, lambda e, ti=ti: e.tensor_copy(out=gpk[:, 0:1], in_=c3_all[:, ti:ti + 1]))
                    W('te', sv)
                    ptr0.guard()
                    EI('te', stc, lambda e: e.transpose(ptr0.t[0:16, :], gpk[:], ident[:]))
                    W('ve', stc)
                    EI('ve', sv, lambda e: e.tensor_copy(out=gsr[:], in_=ptr0.t[0:1, 0:TL]))
                    ptr0.done(sv)
                for ee in range(NE):
                    mmgroup(pa1, np.s_[:, 0:TL],
                            [(AT_s[:, k, ee * TL:(ee + 1) * TL], xnT_b[:, k, :])
                             for k in range(8)],
                            [('ve', sv, (lambda e, ee=ee: e.tensor_copy(
                                out=la_s[0:16, ee * TL:(ee + 1) * TL],
                                in_=pa1.t[0:16, 0:TL])))])
                for grp in range(2):
                    n0 = grp * 1024
                    pb = (pc0, pc1)[grp]
                    for nn in range(2):
                        sub = [(xnT_b[:, k, :],
                                wiT_s[:, k, n0 + nn * 512:n0 + nn * 512 + 512])
                               for k in range(8)]
                        sub.append((ones_r[:],
                                    bi_s[0:1, n0 + nn * 512:n0 + nn * 512 + 512]))
                        ev_ = [] if nn == 0 else [
                            ('ve', sv, (lambda e, grp=grp, pb=pb: e.tensor_copy(
                                out=base_s[:, grp * 1024:(grp + 1) * 1024],
                                in_=pb.t[:, 0:1024])))]
                        mmgroup(pb, np.s_[:, nn * 512:(nn + 1) * 512], sub, ev_)
                    pb.done(sv)
                W('gp', sv)
                EI('gp', sg, lambda e: e.memset(Hacc[:], 0.0))
                hcnt = []
                lbufs = (lorab, sadd)
                i = 0
                for ee in range(NE):
                    for grp in range(2):
                        pb = (pc0, pc1)[grp]
                        lb = lbufs[i % 2]
                        if i >= 2:
                            W('ve', sg, hcnt[i - 2])
                        for nn in range(2):
                            n0 = grp * 1024 + nn * 512
                            ev_ = [] if nn == 0 else [
                                ('ve', sv, (lambda e, pb=pb, ee=ee, gts=gts, lb=lb: e.tensor_scalar(
                                    out=lb[:], in0=pb.t[:, 0:1024], scalar1=0.0,
                                    scalar2=gts[:, ee:ee + 1], op0=ALU.max, op1=ALU.mult)))]
                            mmgroup(pb, np.s_[:, nn * 512:(nn + 1) * 512],
                                    [(la_s[0:16, ee * TL:(ee + 1) * TL],
                                      BT_s[0:16, ee, n0:n0 + 512]),
                                     (ident[:], base_s[:, n0:n0 + 512])],
                                    ev_)
                        pb.done(sv)
                        W('gp', sv)
                        EI('gp', sg, (lambda e, grp=grp, lb=lb: e.tensor_tensor(
                            out=Hacc[:, grp * 1024:(grp + 1) * 1024],
                            in0=Hacc[:, grp * 1024:(grp + 1) * 1024],
                            in1=lb[:], op=ALU.add)))
                        hcnt.append(sg.n)
                        i += 1
                W('ve', sg)
                EI('ve', sv, lambda e: e.memset(spc[:, 0:8], 0.0))
                EI('ve', sv, lambda e: e.tensor_copy(out=Hbf[:], in_=Hacc[:]))
                W('te', sv)
                for c in range(16):
                    transpose_to(HT[:, c, :], Hbf[:, c * TL:(c + 1) * TL])
                W('te', sv)
                for nn in range(2):
                    sub = [(HT[:, c, :], woT_s[:, c, nn * 512:(nn + 1) * 512])
                           for c in range(16)]
                    if half == 1:
                        sub.append((gsr[:], bo_s[0:1, nn * 512:(nn + 1) * 512]))
                    mmgroup(pc0, np.s_[:, nn * 512:(nn + 1) * 512], sub, [])
                W('ve', stc)
                EI('ve', sv, lambda e: e.tensor_tensor(out=outb[:], in0=pc0.t[:, 0:1024],
                                                       in1=partb[:], op=ALU.add))
                pc0.done(sv)
                W('sy', sv)
                dst = part_d if half == 0 else out_d
                EI('sy', d, (lambda e, b=b, T0=T0, dst=dst: e.dma_start(
                    out=dst[b, T0:T0 + TL, :], in_=outb[:])), 16)

    for b in range(NB):
        keys_phase(b)
        attn_phase(b)
    ffn_phase(0)
    ffn_phase(1)
    W('sy', d)
    W('sy', g)

    with nc.Block() as block:
        def mk(ek):
            def run(eng):
                for k, fn in prog:
                    if k == ek:
                        fn(eng)
            return run
        block.sync(mk('sy'))
        block.vector(mk('ve'))
        block.scalar(mk('sc'))
        block.tensor(mk('te'))
        block.gpsimd(mk('gp'))

    stack.close()
    return nc


def kernel(**inputs):
    rd = np.asarray(inputs["L0_attn_raw_delta"], dtype=np.float32)
    assert rd.shape == (16,)
    CT, ST, CT2, ST2, CK, SK = _host_consts()
    _MC = _mask_consts()
    x = np.ascontiguousarray(np.asarray(inputs["L0_attn_x"], dtype=np.float32))
    f32 = lambda a: np.ascontiguousarray(np.asarray(a, dtype=np.float32))
    base_map = {
        "wdkv": f32(inputs["L0_attn_W_dkv"]), "wuk": f32(inputs["L0_attn_W_uk"]),
        "wuv": f32(inputs["L0_attn_W_uv"]), "wdq": f32(inputs["L0_attn_W_dq"]),
        "wuq": f32(inputs["L0_attn_W_uq"]), "wqr": f32(inputs["L0_attn_W_qr"]),
        "wkr": f32(inputs["L0_attn_W_kr"]), "wout": f32(inputs["L0_attn_W_out"]),
        "iwq": f32(inputs["L0_attn_idx_wq"]), "iwk": f32(inputs["L0_attn_idx_wk"]),
        "iw": np.tile(np.asarray(inputs["L0_attn_idx_w"], dtype=np.float32)[None, :], (TL, 1)),
        "wiT": f32(np.asarray(inputs["L0_ffn_wi"], dtype=np.float32).T),
        "woT": f32(np.asarray(inputs["L0_ffn_wo"], dtype=np.float32).T),
        "gwT": f32(np.pad(np.asarray(inputs["L0_ffn_gate_w"], dtype=np.float32).T,
                      ((0, 0), (0, TL - NE)))),
        "AT": f32(np.pad(np.transpose(np.asarray(inputs["L0_ffn_lora_A"], dtype=np.float32),
                                  (2, 0, 1)), ((0, 0), (0, 0), (0, TL - RLO))).reshape(D, NE * TL)),
        "BT": f32(np.transpose(np.asarray(inputs["L0_ffn_lora_B"], dtype=np.float32), (0, 2, 1))),
        "bi": f32(np.asarray(inputs["L0_ffn_bi"], dtype=np.float32).reshape(1, DFF)),
        "bo": f32(np.asarray(inputs["L0_ffn_bo"], dtype=np.float32).reshape(1, D)),
        "gb": f32(np.pad(np.asarray(inputs["L0_ffn_gate_b"], dtype=np.float32).reshape(1, NE),
                     ((0, 0), (0, TL - NE)))),
        "CT": CT, "ST": ST, "CT2": CT2, "ST2": ST2, "CK": CK, "SK": SK,
        "K128": _MC[0], "NEGM": _MC[1], "K0": _MC[2], "SURR0": _MC[3],
        "BANDM": _MC[4], "identw": _MC[5],
    }
    nc = build_nc()
    in_maps = []
    for c in range(8):
        m = dict(base_map)
        m["x"] = np.ascontiguousarray(x[2 * c:2 * c + 2])
        in_maps.append(m)
    res = run_bass_kernel_spmd(nc, in_maps, core_ids=list(range(8)))
    out = np.concatenate([np.asarray(r["out"]) for r in res.results], axis=0)
    return out.astype(np.float32)


# version 11
# speedup vs baseline: 1.0837x; 1.0345x over previous
import contextlib
import numpy as np
import ml_dtypes
import concourse.bass as bass
import concourse.mybir as mybir
from concourse.bass_utils import run_bass_kernel_spmd
from concourse.bass_core_rust import IndirectOffsetOnAxis

F32 = mybir.dt.float32
BF16 = mybir.dt.bfloat16
I32 = mybir.dt.int32
U32 = mybir.dt.uint32
AF = mybir.ActivationFunctionType
ALU = mybir.AluOpType
AX = mybir.AxisListType
bfdt = ml_dtypes.bfloat16

L, D, NB, NT, TL = 2048, 1024, 2, 16, 128
NH, DH, DR = 8, 64, 32
NIH = 4
DFF, NE, RLO = 4096, 8, 16
ROW = 512 + 512 + 32
GSL = 16
NSW = 6
SCALE = (DH + DR) ** -0.5
NEG = -3.0e38
EPS = 1e-5
_PB = 196608


def _host_consts():
    th = 10000.0 ** (-np.arange(16, dtype=np.float64) / 16.0)

    def pack(ang):
        c = np.cos(ang).astype(np.float32)
        s = np.sin(ang).astype(np.float32)
        return (c.reshape(NT, TL, 16).transpose(1, 0, 2).reshape(TL, NT * 16).copy(),
                s.reshape(NT, TL, 16).transpose(1, 0, 2).reshape(TL, NT * 16).copy())
    CT, ST = pack(np.arange(L, dtype=np.float64)[:, None] * th[None, :])
    CT2, ST2 = pack((2.0 * np.arange(L, dtype=np.float64)[:, None] - 63.0) * th[None, :])
    ang_k = np.arange(96, dtype=np.float64)[:, None] * th[None, :]
    CK = np.repeat(np.cos(ang_k).astype(np.float32).reshape(1, 96 * 16), TL, axis=0).copy()
    SK = np.repeat(np.sin(ang_k).astype(np.float32).reshape(1, 96 * 16), TL, axis=0).copy()
    return CT, ST, CT2, ST2, CK, SK



def _mask_consts():
    p = np.arange(TL)[:, None]
    s = np.arange(TL)[None, :]
    K128 = (p - s >= 64).astype(np.float32)
    u = np.arange(TL)[None, :]
    K128 = (u <= p).astype(np.float32)
    NEGM = np.where(u <= p, 0.0, NEG).astype(np.float32)
    d = (p - s).astype(np.float64)
    K0 = (d >= 64).astype(np.float32)
    SURR = np.where((d >= 0) & (d <= 63), 1e30 + d * 1e24,
                    np.where(d < 0, -1e30 + d * 1e24, 0.0)).astype(np.float32)
    u2 = np.arange(256)[None, :]
    BANDM = np.where((u2 >= p + 65) & (u2 <= p + 128), 0.0, NEG).astype(np.float32)
    BANDM4 = np.tile(BANDM, (1, 4)).copy()
    IDENT = np.eye(TL, dtype=np.float32).astype(bfdt)
    return K128, NEGM, K0, SURR, BANDM4, IDENT


def build_nc(dbg=False):
    nc = bass.Bass("TRN2", target_bir_lowering=False)

    x_d = nc.dram_tensor("x", [NB, L, D], F32, kind="ExternalInput").ap()
    out_d = nc.dram_tensor("out", [NB, L, D], F32, kind="ExternalOutput").ap()
    win = {}
    for name, shape in [
        ("wdkv", [D, 256]), ("wuk", [256, 512]), ("wuv", [256, 512]),
        ("wdq", [D, 512]), ("wuq", [512, 512]), ("wqr", [512, 256]),
        ("wkr", [D, DR]), ("wout", [512, D]), ("iwq", [D, 256]), ("iwk", [D, 64]),
        ("iw", [TL, NIH]),
        ("wiT", [D, DFF]), ("woT", [DFF, D]), ("gwT", [D, TL]),
        ("AT", [D, NE * TL]), ("BT", [NE, RLO, DFF]),
        ("bi", [1, DFF]), ("bo", [1, D]), ("gb", [1, TL]),
        ("CT", [TL, NT * 16]), ("ST", [TL, NT * 16]),
        ("CT2", [TL, NT * 16]), ("ST2", [TL, NT * 16]),
        ("CK", [TL, 96 * 16]), ("SK", [TL, 96 * 16]),
        ("K128", [TL, TL]), ("NEGM", [TL, TL]),
        ("K0", [TL, TL]), ("SURR0", [TL, TL]), ("BANDM", [TL, 1024]),
    ]:
        win[name] = nc.dram_tensor(name, shape, F32, kind="ExternalInput").ap()
    ident_d = nc.dram_tensor("identw", [TL, TL], BF16, kind="ExternalInput").ap()
    kvtab_d = nc.dram_tensor("kvtab", [L, ROW], BF16, kind="Internal").ap()
    xnt_d = nc.dram_tensor("xnt", [NB, NT, TL, 8 * TL], BF16, kind="Internal").ap()
    part_d = nc.dram_tensor("part", [NB, L, D], F32, kind="Internal").ap()

    cur = [16448]

    def alloc(name, shape, dt):
        per = int(np.prod(shape[1:])) * mybir.dt.size(dt)
        per = (per + 31) // 32 * 32
        off = cur[0]
        cur[0] = off + per
        assert cur[0] <= _PB - 64, f"sbuf overflow at {name}"
        return nc.alloc_sbuf_tensor_at(name, shape, dt, offset=off)

    ident = alloc("ident", [TL, TL], BF16)
    ones_r = alloc("ones_r", [1, TL], BF16)
    cstc = alloc("cstc", [TL, 8], F32)
    spc = alloc("spc", [TL, 8], F32)
    spc2 = alloc("spc2", [TL, 8], F32)
    spc3 = alloc("spc3", [TL, 8], F32)

    a0 = cur[0]
    CT_s = alloc("CT_s", [TL, NT * 16], F32)
    ST_s = alloc("ST_s", [TL, NT * 16], F32)
    CT2_s = alloc("CT2_s", [TL, NT * 16], F32)
    ST2_s = alloc("ST2_s", [TL, NT * 16], F32)
    CK_s = alloc("CK_s", [TL, 96 * 16], BF16)
    SK_s = alloc("SK_s", [TL, 96 * 16], BF16)
    BANDM_s = alloc("BANDM_s", [TL, 1024], F32)
    iw_s = alloc("iw_s", [TL, NIH], F32)
    K128s = alloc("K128s", [TL, TL], F32)
    NEGMs = alloc("NEGMs", [TL, TL], F32)
    K0s = alloc("K0s", [TL, TL], F32)
    SURR0s = alloc("SURR0s", [TL, TL], F32)
    wA = {}
    for name, shape in [
        ("wdkv", [TL, 8, 256]), ("wuk", [TL, 2, 512]), ("wuv", [TL, 2, 512]),
        ("wdq", [TL, 8, 512]), ("wuq", [TL, 4, 512]), ("wqr", [TL, 4, 256]),
        ("wkr", [TL, 8, DR]), ("wout", [TL, 4, D]), ("iwq", [TL, 8, 256]),
        ("iwk", [TL, 8, 64]),
    ]:
        wA[name] = alloc("A_" + name, shape, BF16)
    kiT = alloc("kiT", [64, L], BF16)
    G = alloc("G", [TL, GSL, ROW], BF16)
    prods = alloc("prods", [TL, NH * GSL * DH], BF16)
    I_a = alloc("I_a", [TL, L], F32)
    I_b = alloc("I_b", [TL, L], F32)
    bscore = I_b
    kcb = alloc("kcb", [TL, 2, 512], BF16)
    VSb = alloc("VSb", [TL, 2, 512], BF16)
    kmub = alloc("kmub", [TL, 2, DR], BF16)
    rotb = alloc("rotb", [TL, TL], BF16)
    kcTb = alloc("kcTb", [TL, 4, 256], BF16)
    krTt = alloc("krTt", [96, 2, TL], BF16)
    qcT = alloc("qcT", [TL, 4, TL], BF16)
    qropeT = alloc("qropeT", [96, NH, TL], BF16)
    qropeb = alloc("qropeb", [TL, NH, TL], BF16)
    battn = alloc("battn", [TL, L], BF16)
    batT = alloc("batT", [TL, 2, TL], BF16)
    x_s = alloc("x_s", [TL, D], F32)
    xn_s = I_a[:, 0:D]
    xnbf = alloc("xnbf", [TL, D], BF16)
    xnT = alloc("xnT", [TL, 8, TL], BF16)
    tab = alloc("tab", [TL, ROW], BF16)
    ckvT = alloc("ckvT", [TL, 2, TL], BF16)
    cqT = alloc("cqT", [TL, 4, TL], BF16)
    qiT = alloc("qiT", [64, NIH, TL], BF16)
    qc = alloc("qc", [TL, 512], BF16)
    qr = alloc("qr", [TL, 256], F32)
    mu = alloc("mu", [TL, 256], F32)
    qt1 = alloc("qt1", [TL, 128], BF16)
    qt2 = alloc("qt2", [TL, 128], BF16)
    kt1 = alloc("kt1", [TL, GSL * 16], BF16)
    kt2 = alloc("kt2", [TL, GSL * 16], BF16)
    ktt = alloc("ktt", [TL, GSL * 16], BF16)
    rp = battn
    rp2 = alloc("rp2", [TL, NH * GSL * 16], BF16)
    Ssc = alloc("Ssc", [TL, NH * GSL], F32)
    Ss2 = alloc("Ss2", [TL, NH * GSL], F32)
    ebuf = alloc("ebuf", [TL, NH * GSL], BF16)
    mrun = alloc("mrun", [TL, NH], F32)
    mnew = alloc("mnew", [TL, NH], F32)
    alph = alloc("alph", [TL, NH], F32)
    dnm = alloc("dnm", [TL, NH], F32)
    rpart = alloc("rpart", [TL, NH], F32)
    Aacc = alloc("Aacc", [TL, 512], F32)
    avp = alloc("avp", [TL, 512], F32)
    attnbf = alloc("attnbf", [TL, 512], BF16)
    attnT = alloc("attnT", [TL, 4, TL], BF16)
    part_s = alloc("part_s", [TL, D], F32)
    idx96 = alloc("idx96", [TL, 96], U32)
    v8 = alloc("v8", [TL, 8], F32)
    col1 = alloc("col1", [TL, 1], F32)
    col2 = alloc("col2", [TL, 1], F32)
    w1t = alloc("w1t", [TL, 512], F32)
    assert cur[0] <= _PB - 64

    cur[0] = a0
    wiT_s = alloc("wiT_s", [TL, 8, DFF // 2], BF16)
    woT_s = alloc("woT_s", [TL, 16, D], BF16)
    BT_s = alloc("BT_s", [16, NE, DFF // 2], BF16)
    AT_s = alloc("AT_s", [TL, 8, NE * TL], BF16)
    gwT_s = alloc("gwT_s", [TL, 8, TL], BF16)
    bi_s = alloc("bi_s", [1, DFF // 2], BF16)
    bo_s = alloc("bo_s", [1, D], BF16)
    gbr = alloc("gbr", [1, TL], BF16)
    Hacc = alloc("Hacc", [TL, DFF // 2], F32)
    stg = Hacc
    base_s = alloc("base_s", [TL, DFF // 2], BF16)
    lorab = alloc("lorab", [TL, 1024], BF16)
    sadd = alloc("sadd", [TL, 1024], BF16)
    Hbf = alloc("Hbf", [TL, DFF // 2], BF16)
    HT = alloc("HT", [TL, 16, TL], BF16)
    xnT_b = alloc("xnT_b", [TL, 8, TL], BF16)
    partb = alloc("partb", [TL, D], F32)
    outb = alloc("outb", [TL, D], F32)
    la_s = alloc("la_s", [16, NE * TL], BF16)
    probs = alloc("probs", [TL, NE], F32)
    gates = alloc("gates", [TL, NE], F32)
    dcol = alloc("dcol", [TL, NE], F32)
    c1 = alloc("c1", [TL, 1], F32)
    c2 = alloc("c2", [TL, 1], F32)
    c3 = alloc("c3", [TL, 1], F32)
    gpk = alloc("gpk", [TL, 16], BF16)
    gsr = alloc("gsr", [1, TL], BF16)
    gates_all = alloc("gates_all", [TL, NB * NT * NE], F32)
    c3_all = alloc("c3_all", [TL, NB * NT], F32)
    stg2 = alloc("stg2", [TL, DFF // 2], F32)
    assert cur[0] <= _PB - 64

    prog = []

    class Ctr:
        __slots__ = ("sem", "n")

        def __init__(self, sem):
            self.sem, self.n = sem, 0

    def E(ek, fn):
        prog.append((ek, fn))

    def EI(ek, ctr, fn, k=1):
        selfw = ek in ("ve", "sc", "gp")
        n_after = ctr.n + k

        def go(eng, fn=fn, s=ctr.sem, k=k, selfw=selfw, n_after=n_after):
            fn(eng).then_inc(s, k)
            if selfw:
                eng.wait_ge(s, n_after)
        prog.append((ek, go))
        ctr.n += k

    def W(ek, ctr, val=None):
        v = ctr.n if val is None else val
        prog.append((ek, lambda eng, s=ctr.sem, v=v: eng.wait_ge(s, v)))

    def Wsp(ek, ctr):
        W(ek, ctr)
        if ek == 've':
            EI('ve', sv, lambda e: e.memset(spc[:, 0:8], 0.0))
        elif ek == 'sc':
            EI('sc', sc, lambda e: e.memzero(spc2[:, 0:8]))

    stack = contextlib.ExitStack()
    d = Ctr(stack.enter_context(nc.semaphore("d")))
    g = Ctr(stack.enter_context(nc.semaphore("g")))
    sv = Ctr(stack.enter_context(nc.semaphore("sv")))
    sc = Ctr(stack.enter_context(nc.semaphore("sc")))
    stc = Ctr(stack.enter_context(nc.semaphore("stc")))
    sg = Ctr(stack.enter_context(nc.semaphore("sg")))

    class Bank:
        def __init__(self, name, shape, dt):
            self.t = nc.alloc_psum_tensor(name, shape, dt)
            self.rel = None

        def guard(self):
            if self.rel is not None:
                W('te', self.rel[0], self.rel[1])

        def done(self, ctr):
            self.rel = (ctr, ctr.n)

    pa0 = Bank("pa0", [TL, 512], F32)
    pa1 = Bank("pa1", [TL, 512], F32)
    pc0 = Bank("pc0", [TL, 1024], F32)
    pc1 = Bank("pc1", [TL, 1024], F32)
    ptr0 = Bank("ptr0", [TL, TL], BF16)
    ptr1 = Bank("ptr1", [TL, TL], BF16)

    def mmgroup(bank, sl, mms, evacs):
        bank.guard()
        n = len(mms)
        for i, (lh, rh) in enumerate(mms):
            fn = (lambda e, lh=lh, rh=rh, s=(i == 0), p=(i == n - 1), b=bank, sl=sl:
                  e.matmul(b.t[sl], lh, rh, start=s, stop=p))
            if i == n - 1:
                EI('te', stc, fn)
            else:
                E('te', fn)
        last = None
        for ek, ctr, fn in evacs:
            W(ek, stc)
            EI(ek, ctr, fn)
            last = ctr
        if last is not None:
            bank.done(last)

    tr_i = [0]

    def transpose_to(dst, src, r0=0, r1=TL):
        bank = (ptr0, ptr1)[tr_i[0] % 2]
        tr_i[0] += 1
        bank.guard()
        EI('te', stc, lambda e, b=bank, s=src: e.transpose(b.t[:], s, ident[:]))
        W('ve', stc)
        EI('ve', sv, lambda e, b=bank, dd=dst, r0=r0, r1=r1:
           e.tensor_copy(out=dd, in_=b.t[r0:r1, :]))
        bank.done(sv)

    EI('ve', sv, lambda e: e.memset(ones_r[:], 1.0))
    _consts = [EPS, 1.0 / D, -1.0, 1e24, 1e30]
    for ci, cval in enumerate(_consts):
        EI('ve', sv, lambda e, ci=ci, cval=cval: e.memset(cstc[:, ci:ci + 1], cval))
        nc.const_aps.register(F32, cval, cstc[:, ci:ci + 1])
    EI('sy', d, lambda e: e.dma_start(out=ident[:], in_=ident_d), 16)
    for nm, t in [("CT", CT_s), ("ST", ST_s), ("CT2", CT2_s), ("ST2", ST2_s),
                  ("iw", iw_s), ("BANDM", BANDM_s),
                  ("K128", K128s), ("NEGM", NEGMs), ("K0", K0s), ("SURR0", SURR0s)]:
        EI('sy', d, lambda e, a=win[nm], t=t: e.dma_start(out=t[:], in_=a), 16)
    for nm, t in [("CK", CK_s), ("SK", SK_s)]:
        EI('sy', d, lambda e, a=win[nm]: e.dma_start(out=I_a[:, 0:1536], in_=a), 16)
        W('ve', d)
        EI('ve', sv, lambda e, t=t: e.tensor_copy(out=t[:], in_=I_a[:, 0:1536]))
        W('sy', sv)

    for nm in ["wdkv", "wuk", "wuv", "wdq", "wuq", "wqr", "wkr", "wout", "iwq", "iwk"]:
        K, C = wA[nm].shape[1], wA[nm].shape[2]
        src3 = win[nm].rearrange("(k p) c -> k p c", p=TL)
        for k in range(K):
            EI('sy', d, lambda e, s=src3, k=k, C=C:
               e.dma_start(out=I_a[:, 0:C], in_=s[k]), 16)
            W('ve', d)
            EI('ve', sv, lambda e, nm=nm, k=k, C=C:
               e.tensor_copy(out=wA[nm][:, k, :], in_=I_a[:, 0:C]))
            W('sy', sv)

    def emit_ln():
        EI('ve', sv, lambda e: e.tensor_reduce(out=col1[:], in_=x_s[:], axis=AX.X, op=ALU.add))
        W('ve', sv)
        EI('ve', sv, lambda e: e.tensor_scalar(out=col1[:], in0=col1[:], scalar1=1.0 / D,
                                               scalar2=None, op0=ALU.mult))
        W('ve', sv)
        EI('ve', sv, lambda e: e.tensor_scalar(out=xn_s[:], in0=x_s[:], scalar1=col1[:],
                                               scalar2=None, op0=ALU.subtract))
        Wsp('sc', sv)
        EI('sc', sc, lambda e: e.activation(out=I_b[:, 0:D], in_=xn_s[:], func=AF.Square))
        Wsp('ve', sc)
        EI('ve', sv, lambda e: e.tensor_reduce(out=col2[:], in_=I_b[:, 0:D], axis=AX.X, op=ALU.add))
        Wsp('sc', sv)
        EI('sc', sc, lambda e: e.activation(out=col2[:], in_=col2[:], func=AF.Sqrt,
                                            scale=1.0 / D, bias=EPS))
        Wsp('ve', sc)
        EI('ve', sv, lambda e: e.reciprocal(out=col2[:], in_=col2[:]))
        W('ve', sv)
        EI('ve', sv, lambda e: e.tensor_scalar(out=xn_s[:], in0=xn_s[:], scalar1=col2[:],
                                               scalar2=None, op0=ALU.mult))
        EI('ve', sv, lambda e: e.tensor_copy(out=xnbf[:], in_=xn_s[:]))
        W('te', sv)
        for c in range(8):
            transpose_to(xnT[:, c, :], xnbf[:, c * TL:(c + 1) * TL])

    def keys_phase(b):
        for T in range(NT):
            T0 = T * TL
            W('sy', g)
            EI('sy', d, lambda e, b=b, T0=T0: e.dma_start(
                out=x_s[:], in_=x_d[b, T0:T0 + TL, :]), 16)
            W('ve', d)
            W('sc', sv)
            emit_ln()
            W('sy', sv)
            EI('sy', d, lambda e, b=b, T=T: e.dma_start(
                out=xnt_d[b, T], in_=xnT[:].rearrange("p c t -> p (c t)")), 16)
            for cc in range(2):
                mmgroup((pa0, pa1)[cc], np.s_[:, 0:TL],
                        [(wA["wdkv"][:, k, cc * TL:(cc + 1) * TL], xnT[:, k, :])
                         for k in range(8)],
                        [('ve', sv, (lambda e, cc=cc, bk=(pa0, pa1)[cc]: e.tensor_copy(
                            out=ckvT[:, cc, :], in_=bk.t[:, 0:TL])))])
            W('te', sv)
            mmgroup(pc0, np.s_[:, 0:512],
                    [(ckvT[:, c, :], wA["wuk"][:, c, :]) for c in range(2)],
                    [('ve', sv, lambda e: e.tensor_copy(out=tab[:, 0:512],
                                                        in_=pc0.t[:, 0:512]))])
            mmgroup(pc1, np.s_[:, 0:512],
                    [(ckvT[:, c, :], wA["wuv"][:, c, :]) for c in range(2)],
                    [('ve', sv, lambda e: e.tensor_copy(out=tab[:, 512:1024],
                                                        in_=pc1.t[:, 0:512]))])
            mmgroup(pa0, np.s_[:, 0:DR],
                    [(xnT[:, k, :], wA["wkr"][:, k, :]) for k in range(8)],
                    [('ve', sv, lambda e: e.tensor_copy(out=w1t[:, 0:DR],
                                                        in_=pa0.t[:, 0:DR]))])
            Wsp('sc', sv)
            EI('sc', sc, lambda e: e.activation(out=w1t[:, 64:64 + DR], in_=w1t[:, 0:DR],
                                                func=AF.Abs))
            EI('sc', sc, lambda e: e.activation(out=w1t[:, 64:64 + DR], in_=w1t[:, 64:64 + DR],
                                                func=AF.Exp, scale=-1.0))
            EI('sc', sc, lambda e: e.activation(out=w1t[:, 64:64 + DR], in_=w1t[:, 64:64 + DR],
                                                func=AF.Ln, bias=1.0))
            EI('sc', sc, lambda e: e.activation(out=w1t[:, 128:128 + DR], in_=w1t[:, 0:DR],
                                                func=AF.Relu))
            Wsp('ve', sc)
            EI('ve', sv, lambda e: e.tensor_tensor(out=tab[:, 1024:1056],
                                                   in0=w1t[:, 64:64 + DR],
                                                   in1=w1t[:, 128:128 + DR], op=ALU.add))
            mmgroup(pa1, np.s_[0:64, 0:TL],
                    [(wA["iwk"][:, k, :], xnT[:, k, :]) for k in range(8)],
                    [('ve', sv, (lambda e, T0=T0: e.tensor_copy(
                        out=kiT[0:64, T0:T0 + TL], in_=pa1.t[0:64, 0:TL])))])
            W('sy', sv)
            EI('sy', d, lambda e, T0=T0: e.dma_start(
                out=kvtab_d[T0:T0 + TL, :], in_=tab[:]), 16)

    def emit_gathers(k0, j0=0, j1=GSL):
        W('gp', sv)
        W('gp', d)
        for j in range(j0, j1):
            EI('gp', g, (lambda e, j=j, k0=k0: e.indirect_dma_start(
                out=G[:, j, :], out_offset=None, in_=kvtab_d,
                in_offset=IndirectOffsetOnAxis(idx96[:, k0 + j:k0 + j + 1], 0))), 16)
        return g.n

    def attn_phase(b):
        for T in range(NT):
            T0 = T * TL
            EI('sy', d, lambda e, b=b, T=T: e.dma_start(
                out=xnT[:].rearrange("p c t -> p (c t)"), in_=xnt_d[b, T]), 16)
            EI('sy', d, lambda e, b=b, T0=T0: e.dma_start(
                out=x_s[:], in_=x_d[b, T0:T0 + TL, :]), 16)
            if T > 0:
                W('sy', stc)
                EI('sy', d, (lambda e, T0=T0: e.dma_start(
                    out=kcb[:],
                    in_=kvtab_d[T0 - TL:T0 + TL, 0:512].rearrange(
                        "(j p) c -> p j c", p=TL))), 16)
                EI('sy', d, (lambda e, T0=T0: e.dma_start(
                    out=VSb[:],
                    in_=kvtab_d[T0 - TL:T0 + TL, 512:1024].rearrange(
                        "(j p) c -> p j c", p=TL))), 16)
                EI('sy', d, (lambda e, T0=T0: e.dma_start(
                    out=kmub[:],
                    in_=kvtab_d[T0 - TL:T0 + TL, 1024:1056].rearrange(
                        "(j p) c -> p j c", p=TL))), 16)
            W('te', d)
            W('ve', d)
            W('ve', sg)
            W('sc', sv)
            W('gp', sv)
            for h in range(NIH):
                mmgroup((pa0, pa1)[h % 2], np.s_[0:64, 0:TL],
                        [(wA["iwq"][:, k, h * 64:(h + 1) * 64], xnT[:, k, :])
                         for k in range(8)],
                        [('ve', sv, (lambda e, h=h, bk=(pa0, pa1)[h % 2]: e.tensor_copy(
                            out=qiT[0:64, h, :], in_=bk.t[0:64, 0:TL])))])
            W('te', sv)
            Wt = T0 + 64 if T > 0 else TL
            nch = (Wt + 511) // 512
            for j in range(nch):
                w = min(512, Wt - j * 512)
                for h in range(NIH):
                    bk = (pa0, pa1)[(j * NIH + h) % 2]
                    lh = qiT[0:64, h, :]
                    rh = kiT[0:64, j * 512:j * 512 + w]
                    if h == 0:
                        mmgroup(bk, np.s_[:, 0:w], [(lh, rh)],
                                [('ve', sv, (lambda e, bk=bk, w=w, j=j: e.tensor_scalar(
                                    out=I_a[:, j * 512:j * 512 + w], in0=bk.t[:, 0:w],
                                    scalar1=0.0, scalar2=iw_s[:, 0:1],
                                    op0=ALU.max, op1=ALU.mult)))])
                    else:
                        W('sc', sv)
                        mmgroup(bk, np.s_[:, 0:w], [(lh, rh)],
                                [('sc', sc, (lambda e, bk=bk, w=w: e.activation(
                                    out=I_b[:, 0:w], in_=bk.t[:, 0:w], func=AF.Relu)))])
                        Wsp('ve', sc)
                        EI('ve', sv, (lambda e, w=w, j=j, h=h: e.scalar_tensor_tensor(
                            out=I_a[:, j * 512:j * 512 + w], in0=I_b[:, 0:w],
                            scalar=iw_s[:, h:h + 1], in1=I_a[:, j * 512:j * 512 + w],
                            op0=ALU.mult, op1=ALU.add)))
            if T > 0:
                EI('ve', sv, (lambda e, T0=T0: e.tensor_tensor(
                    out=I_a[:, T0 - 64:T0 + 64], in0=I_a[:, T0 - 64:T0 + 64],
                    in1=K128s[:], op=ALU.mult)))
                EI('ve', sv, (lambda e, T0=T0: e.tensor_tensor(
                    out=I_a[:, T0 - 64:T0 + 64], in0=I_a[:, T0 - 64:T0 + 64],
                    in1=NEGMs[:], op=ALU.add)))
                nr, off = 4, 64
            else:
                EI('ve', sv, lambda e: e.tensor_tensor(out=I_a[:, 0:TL], in0=I_a[:, 0:TL],
                                                       in1=K0s[:], op=ALU.mult))
                EI('ve', sv, lambda e: e.tensor_tensor(out=I_a[:, 0:TL], in0=I_a[:, 0:TL],
                                                       in1=SURR0s[:], op=ALU.add))
                nr, off = 12, 0
            gmA = gfA = None
            for r in range(nr):
                cur_t = (I_a, I_b)[r % 2]
                oth_t = (I_b, I_a)[r % 2]
                EI('ve', sv, (lambda e, c=cur_t, Wt=Wt: e.max(out=v8[:], in_=c[:, 0:Wt])))
                W('ve', sv)
                EI('ve', sv, (lambda e, c=cur_t, Wt=Wt, r=r, off=off: e.max_index(
                    out=idx96[:, off + r * 8:off + r * 8 + 8],
                    in_max=v8[:], in_values=c[:, 0:Wt])))
                if r < nr - 1:
                    EI('ve', sv, (lambda e, c=cur_t, o=oth_t, Wt=Wt: e.match_replace(
                        out=o[:, 0:Wt], in_to_replace=v8[:], in_values=c[:, 0:Wt],
                        imm_value=NEG)))
                if r == 0:
                    gmA = emit_gathers(off, 0, 8)
                elif r == 1:
                    gfA = emit_gathers(off, 8, 16)
            EI('ve', sv, lambda e: e.memset(mrun[:], NEG))
            EI('ve', sv, lambda e: e.memset(dnm[:], 0.0))
            EI('ve', sv, lambda e: e.memset(Aacc[:], 0.0))
            for cc in range(4):
                mmgroup((pa0, pa1)[cc % 2], np.s_[:, 0:TL],
                        [(wA["wdq"][:, k, cc * TL:(cc + 1) * TL], xnT[:, k, :])
                         for k in range(8)],
                        [('ve', sv, (lambda e, cc=cc, bk=(pa0, pa1)[cc % 2]: e.tensor_copy(
                            out=cqT[:, cc, :], in_=bk.t[:, 0:TL])))])
            W('te', sv)
            mmgroup(pc0, np.s_[:, 0:512],
                    [(cqT[:, c, :], wA["wuq"][:, c, :]) for c in range(4)],
                    [('ve', sv, lambda e: e.tensor_copy(out=qc[:], in_=pc0.t[:, 0:512]))])
            mmgroup(pc1, np.s_[:, 0:256],
                    [(cqT[:, c, :], wA["wqr"][:, c, :]) for c in range(4)],
                    [('ve', sv, lambda e: e.tensor_copy(out=qr[:], in_=pc1.t[:, 0:256]))])
            Wsp('sc', sv)
            EI('sc', sc, lambda e: e.activation(out=mu[:], in_=qr[:], func=AF.Abs))
            EI('sc', sc, lambda e: e.activation(out=mu[:], in_=mu[:], func=AF.Exp, scale=-1.0))
            EI('sc', sc, lambda e: e.activation(out=mu[:], in_=mu[:], func=AF.Ln, bias=1.0))
            EI('sc', sc, lambda e: e.activation(out=qr[:], in_=qr[:], func=AF.Relu))
            Wsp('ve', sc)
            EI('ve', sv, lambda e: e.tensor_tensor(out=mu[:], in0=mu[:], in1=qr[:], op=ALU.add))
            mu_v = mu[:].rearrange("p (h two i) -> p h two i", h=NH, two=2)
            ct_v = CT_s[:, T * 16:(T + 1) * 16].rearrange("p (o i) -> p o i", o=1).to_broadcast([TL, NH, 16])
            st_v = ST_s[:, T * 16:(T + 1) * 16].rearrange("p (o i) -> p o i", o=1).to_broadcast([TL, NH, 16])
            q1v = qt1[:].rearrange("p (h i) -> p h i", h=NH)
            q2v = qt2[:].rearrange("p (h i) -> p h i", h=NH)
            tmpv = rp[:, 0:NH * 16].rearrange("p (h i) -> p h i", h=NH)
            EI('ve', sv, lambda e, a=mu_v, b=ct_v, o=q1v: e.tensor_tensor(out=o, in0=a[:, :, 0, :], in1=b, op=ALU.mult))
            EI('ve', sv, lambda e, a=mu_v, b=st_v, o=tmpv: e.tensor_tensor(out=o, in0=a[:, :, 1, :], in1=b, op=ALU.mult))
            EI('ve', sv, lambda e, o=q1v, b=tmpv: e.tensor_tensor(out=o, in0=o, in1=b, op=ALU.subtract))
            EI('ve', sv, lambda e, a=mu_v, b=st_v, o=q2v: e.tensor_tensor(out=o, in0=a[:, :, 0, :], in1=b, op=ALU.mult))
            EI('ve', sv, lambda e, a=mu_v, b=ct_v, o=tmpv: e.tensor_tensor(out=o, in0=a[:, :, 1, :], in1=b, op=ALU.mult))
            EI('ve', sv, lambda e, o=q2v, b=tmpv: e.tensor_tensor(out=o, in0=o, in1=b, op=ALU.add))
            if T > 0:
                W('te', sv)
                for c in range(4):
                    transpose_to(qcT[:, c, :], qc[:, c * TL:(c + 1) * TL])
                ct2_v = CT2_s[:, T * 16:(T + 1) * 16].rearrange("p (o i) -> p o i", o=1).to_broadcast([TL, NH, 16])
                st2_v = ST2_s[:, T * 16:(T + 1) * 16].rearrange("p (o i) -> p o i", o=1).to_broadcast([TL, NH, 16])
                s1v = Ssc[:].rearrange("p (h i) -> p h i", h=NH)
                s2v = Ss2[:].rearrange("p (h i) -> p h i", h=NH)
                qb1 = qropeb[:, :, 64:80]
                qb2 = qropeb[:, :, 80:96]
                EI('ve', sv, lambda e, a=mu_v, b=ct2_v, o=s1v: e.tensor_tensor(out=o, in0=a[:, :, 0, :], in1=b, op=ALU.mult))
                EI('ve', sv, lambda e, a=mu_v, b=st2_v, o=s2v: e.tensor_tensor(out=o, in0=a[:, :, 1, :], in1=b, op=ALU.mult))
                EI('ve', sv, lambda e, o=qb1, x=s1v, y=s2v: e.tensor_tensor(out=o, in0=x, in1=y, op=ALU.subtract))
                EI('ve', sv, lambda e, a=mu_v, b=st2_v, o=s1v: e.tensor_tensor(out=o, in0=a[:, :, 0, :], in1=b, op=ALU.mult))
                EI('ve', sv, lambda e, a=mu_v, b=ct2_v, o=s2v: e.tensor_tensor(out=o, in0=a[:, :, 1, :], in1=b, op=ALU.mult))
                EI('ve', sv, lambda e, o=qb2, x=s1v, y=s2v: e.tensor_tensor(out=o, in0=x, in1=y, op=ALU.add))
                W('te', sv)
                for h in range(NH):
                    transpose_to(qropeT[64:96, h, :], qropeb[:, h, :], r0=64, r1=96)
            qc_v = qc[:].rearrange("p (h o dd) -> p h o dd", h=NH, o=1).to_broadcast([TL, NH, GSL, DH])
            gk_v = G[:, :, 0:512].rearrange("p k (h dd) -> p h k dd", h=NH)
            pr_v = prods[:].rearrange("p (h k dd) -> p h k dd", h=NH, k=GSL)
            nu1 = G[:, :, 1024:1040]
            nu2 = G[:, :, 1040:1056]
            k1v = kt1[:].rearrange("p (k i) -> p k i", i=16)
            k2v = kt2[:].rearrange("p (k i) -> p k i", i=16)
            ktv = ktt[:].rearrange("p (k i) -> p k i", i=16)
            q1b = qt1[:].rearrange("p (h o i) -> p h o i", h=NH, o=1).to_broadcast([TL, NH, GSL, 16])
            q2b = qt2[:].rearrange("p (h o i) -> p h o i", h=NH, o=1).to_broadcast([TL, NH, GSL, 16])
            k1b = kt1[:].rearrange("p (o k i) -> p o k i", o=1, i=16).to_broadcast([TL, NH, GSL, 16])
            k2b = kt2[:].rearrange("p (o k i) -> p o k i", o=1, i=16).to_broadcast([TL, NH, GSL, 16])
            rpv = rp[:].rearrange("p (h k i) -> p h k i", h=NH, k=GSL)
            rp2v = rp2[:].rearrange("p (h k i) -> p h k i", h=NH, k=GSL)
            Sv = Ssc[:].rearrange("p (h k) -> p h k", h=NH)
            S2v = Ss2[:].rearrange("p (h k) -> p h k", h=NH)
            ev = ebuf[:].rearrange("p (h k) -> p h k", h=NH)
            mb = mnew[:].rearrange("p (h o) -> p h o", o=1).to_broadcast([TL, NH, GSL])
            ab = alph[:].rearrange("p (h o) -> p h o", o=1).to_broadcast([TL, NH, DH])
            Av = Aacc[:].rearrange("p (h dd) -> p h dd", h=NH)
            eb2 = ebuf[:].rearrange("p (h o k) -> p h o k", h=NH, o=1).to_broadcast([TL, NH, DH, GSL])
            gv_v = G[:, :, 512:1024].rearrange("p k (h dd) -> p h dd k", h=NH)
            pr2 = prods[:].rearrange("p (h dd k) -> p h dd k", h=NH, dd=DH)
            avv = avp[:].rearrange("p (h dd) -> p h dd", h=NH)
            def emit_sweep(k0, gm, gf, pre):
                ckv_ = CK_s[:].rearrange("p (k i) -> p k i", i=16)[:, k0:k0 + GSL, :]
                skv_ = SK_s[:].rearrange("p (k i) -> p k i", i=16)[:, k0:k0 + GSL, :]
                sv_snap = sv.n
                W('gp', g, gf)
                W('gp', sv, sv_snap)
                EI('gp', sg, lambda e: e.memset(spc3[:, 0:8], 0.0))
                EI('gp', sg, lambda e, ckv_=ckv_: e.tensor_tensor(out=k1v, in0=nu1, in1=ckv_, op=ALU.mult))
                EI('gp', sg, lambda e, skv_=skv_: e.tensor_tensor(out=ktv, in0=nu2, in1=skv_, op=ALU.mult))
                EI('gp', sg, lambda e: e.tensor_tensor(out=k1v, in0=k1v, in1=ktv, op=ALU.subtract))
                EI('gp', sg, lambda e, skv_=skv_: e.tensor_tensor(out=k2v, in0=nu1, in1=skv_, op=ALU.mult))
                EI('gp', sg, lambda e, ckv_=ckv_: e.tensor_tensor(out=ktv, in0=nu2, in1=ckv_, op=ALU.mult))
                EI('gp', sg, lambda e: e.tensor_tensor(out=k2v, in0=k2v, in1=ktv, op=ALU.add))
                EI('gp', sg, lambda e: e.tensor_tensor(out=rpv, in0=q1b, in1=k1b, op=ALU.mult))
                EI('gp', sg, lambda e: e.tensor_tensor(out=rp2v, in0=q2b, in1=k2b, op=ALU.mult))
                EI('gp', sg, lambda e: e.tensor_tensor(out=rpv, in0=rpv, in1=rp2v, op=ALU.add))
                sg_snap = sg.n
                for ja, jb, gw in ((0, 8, gm), (8, GSL, gf)):
                    W('ve', g, gw)
                    gkh = G[:, ja:jb, 0:512].rearrange("p k (h dd) -> p h k dd", h=NH)
                    qch = qc[:].rearrange("p (h o dd) -> p h o dd", h=NH, o=1).to_broadcast([TL, NH, jb - ja, DH])
                    EI('ve', sv, (lambda e, prh=pr_v[:, :, ja:jb, :], qch=qch, gkh=gkh:
                                  e.tensor_tensor(out=prh, in0=qch, in1=gkh, op=ALU.mult)))
                    EI('ve', sv, (lambda e, prh=pr_v[:, :, ja:jb, :], svh=Sv[:, :, ja:jb]:
                                  e.tensor_reduce(out=svh, in_=prh, axis=AX.X, op=ALU.add)))
                W('ve', sg, sg_snap)
                EI('ve', sv, lambda e: e.memset(spc[:, 0:8], 0.0))
                EI('ve', sv, lambda e: e.tensor_reduce(out=S2v, in_=rpv, axis=AX.X, op=ALU.add))
                EI('ve', sv, lambda e: e.tensor_tensor(out=Ssc[:], in0=Ssc[:], in1=Ss2[:], op=ALU.add))
                EI('ve', sv, lambda e: e.tensor_scalar(out=Ssc[:], in0=Ssc[:], scalar1=SCALE,
                                                       scalar2=None, op0=ALU.mult))
                EI('ve', sv, lambda e: e.tensor_reduce(out=rpart[:], in_=Sv, axis=AX.X, op=ALU.max))
                EI('ve', sv, lambda e: e.tensor_tensor(out=mnew[:], in0=mrun[:], in1=rpart[:], op=ALU.max))
                EI('ve', sv, lambda e: e.tensor_tensor(out=alph[:], in0=mrun[:], in1=mnew[:], op=ALU.subtract))
                Wsp('sc', sv)
                EI('sc', sc, lambda e: e.activation(out=alph[:], in_=alph[:], func=AF.Exp))
                EI('ve', sv, lambda e: e.tensor_tensor(out=Sv, in0=Sv, in1=mb, op=ALU.subtract))
                Wsp('sc', sv)
                EI('sc', sc, lambda e: e.activation(out=ebuf[:], in_=Ssc[:], func=AF.Exp))
                Wsp('ve', sc)
                EI('ve', sv, lambda e: e.tensor_reduce(out=rpart[:], in_=ev, axis=AX.X, op=ALU.add))
                EI('ve', sv, lambda e: e.tensor_tensor(out=dnm[:], in0=dnm[:], in1=alph[:], op=ALU.mult))
                EI('ve', sv, lambda e: e.tensor_tensor(out=dnm[:], in0=dnm[:], in1=rpart[:], op=ALU.add))
                EI('ve', sv, lambda e: e.tensor_copy(out=mrun[:], in_=mnew[:]))
                EI('ve', sv, lambda e: e.tensor_tensor(out=Av, in0=Av, in1=ab, op=ALU.mult))
                EI('ve', sv, lambda e: e.tensor_tensor(out=pr2, in0=eb2, in1=gv_v, op=ALU.mult))
                nxt = None
                if pre is not None:
                    nxt = (emit_gathers(pre, 0, 8), emit_gathers(pre, 8, 16))
                EI('ve', sv, lambda e: e.tensor_reduce(out=avv, in_=pr2, axis=AX.X, op=ALU.add))
                EI('ve', sv, lambda e: e.tensor_tensor(out=Aacc[:], in0=Aacc[:], in1=avp[:], op=ALU.add))
                return nxt

            if T > 0:
                for j in range(2):
                    for cc in range(4):
                        transpose_to(kcTb[:, cc, j * TL:(j + 1) * TL],
                                     kcb[:, j, cc * TL:(cc + 1) * TL])
                for j in range(2):
                    ctk = CT_s[:, (T - 1 + j) * 16:(T + j) * 16]
                    stk = ST_s[:, (T - 1 + j) * 16:(T + j) * 16]
                    m1 = kmub[:, j, 0:16]
                    m2 = kmub[:, j, 16:32]
                    if j == 1:
                        W('ve', stc)
                    EI('ve', sv, lambda e, m1=m1, c=ctk: e.tensor_tensor(out=Ssc[:, 0:16], in0=m1, in1=c, op=ALU.mult))
                    EI('ve', sv, lambda e, m2=m2, s=stk: e.tensor_tensor(out=Ss2[:, 0:16], in0=m2, in1=s, op=ALU.mult))
                    EI('ve', sv, lambda e: e.tensor_tensor(out=rotb[:, 64:80], in0=Ssc[:, 0:16], in1=Ss2[:, 0:16], op=ALU.subtract))
                    EI('ve', sv, lambda e, m1=m1, s=stk: e.tensor_tensor(out=Ssc[:, 0:16], in0=m1, in1=s, op=ALU.mult))
                    EI('ve', sv, lambda e, m2=m2, c=ctk: e.tensor_tensor(out=Ss2[:, 0:16], in0=m2, in1=c, op=ALU.mult))
                    EI('ve', sv, lambda e: e.tensor_tensor(out=rotb[:, 80:96], in0=Ssc[:, 0:16], in1=Ss2[:, 0:16], op=ALU.add))
                    W('te', sv)
                    transpose_to(krTt[64:96, j, :], rotb[:], r0=64, r1=96)
                W('te', sv)
                krf = krTt[64:96, :, :].rearrange("p j t -> p (j t)")
                nxtB = emit_sweep(64, gmA, gfA, 80)
                for h in range(NH):
                    bk = (pc0, pc1)[h // 4]
                    col = (h % 4) * 256
                    hb = 64 * (h % 2)
                    mms = [(qcT[hb:hb + 64, h // 2, :],
                            kcTb[hb:hb + 64, h // 2, :]),
                           (qropeT[64:96, h, :], krf)]
                    ev_ = []
                    if h % 4 == 3:
                        ev_ = [('ve', sv, (lambda e, bk=bk, q=h // 4: e.scalar_tensor_tensor(
                            out=bscore[:, q * 1024:(q + 1) * 1024], in0=bk.t[:, 0:1024],
                            scalar=SCALE, in1=BANDM_s[:], op0=ALU.mult, op1=ALU.add)))]
                    mmgroup(bk, np.s_[:, col:col + 256], mms, ev_)
                bs8 = bscore[:].rearrange("p (h u) -> p h u", h=NH)
                mbn256 = mnew[:].rearrange("p (h o) -> p h o", o=1).to_broadcast([TL, NH, 256])
                EI('ve', sv, lambda e, bs8=bs8: e.tensor_reduce(out=rpart[:], in_=bs8, axis=AX.X, op=ALU.max))
                EI('ve', sv, lambda e: e.tensor_tensor(out=mnew[:], in0=mrun[:], in1=rpart[:], op=ALU.max))
                EI('ve', sv, lambda e: e.tensor_tensor(out=alph[:], in0=mrun[:], in1=mnew[:], op=ALU.subtract))
                Wsp('sc', sv)
                EI('sc', sc, lambda e: e.activation(out=alph[:], in_=alph[:], func=AF.Exp))
                EI('ve', sv, lambda e, bs8=bs8, mbn256=mbn256: e.tensor_tensor(out=bs8, in0=bs8, in1=mbn256, op=ALU.subtract))
                Wsp('sc', sv)
                EI('sc', sc, lambda e: e.activation(out=bscore[:], in_=bscore[:], func=AF.Exp))
                Wsp('ve', sc)
                EI('ve', sv, lambda e, bs8=bs8: e.tensor_reduce(out=rpart[:], in_=bs8, axis=AX.X, op=ALU.add))
                EI('ve', sv, lambda e: e.tensor_tensor(out=dnm[:], in0=dnm[:], in1=alph[:], op=ALU.mult))
                EI('ve', sv, lambda e: e.tensor_tensor(out=dnm[:], in0=dnm[:], in1=rpart[:], op=ALU.add))
                EI('ve', sv, lambda e: e.tensor_copy(out=mrun[:], in_=mnew[:]))
                EI('ve', sv, lambda e: e.tensor_tensor(out=Av, in0=Av, in1=ab, op=ALU.mult))
                EI('ve', sv, lambda e: e.tensor_copy(out=battn[:], in_=bscore[:]))
                W('te', sv)
                pa0.guard()
                for h in range(NH):
                    for c in range(2):
                        transpose_to(batT[:, c, :], battn[:, 256 * h + 128 * c:256 * h + 128 * c + 128])
                    W('te', sv)
                    E('te', (lambda e, h=h: e.matmul(
                        pa0.t[:, 64 * h:64 * h + 64], batT[:, 0, :],
                        VSb[:, 0, 64 * h:64 * h + 64], start=True, stop=False)))
                    EI('te', stc, (lambda e, h=h: e.matmul(
                        pa0.t[:, 64 * h:64 * h + 64], batT[:, 1, :],
                        VSb[:, 1, 64 * h:64 * h + 64], start=False, stop=True)))
                W('ve', stc)
                EI('ve', sv, lambda e: e.tensor_tensor(out=Aacc[:], in0=pa0.t[:, 0:512],
                                                       in1=Aacc[:], op=ALU.add))
                pa0.done(sv)
                emit_sweep(80, nxtB[0], nxtB[1], None)
            else:
                cur = (gmA, gfA)
                for s in range(6):
                    cur = emit_sweep(s * 16, cur[0], cur[1],
                                     (s + 1) * 16 if s < 5 else None)

            EI('ve', sv, lambda e: e.reciprocal(out=dnm[:], in_=dnm[:]))
            db = dnm[:].rearrange("p (h o) -> p h o", o=1).to_broadcast([TL, NH, DH])
            EI('ve', sv, lambda e: e.tensor_tensor(
                out=attnbf[:].rearrange("p (h dd) -> p h dd", h=NH), in0=Av, in1=db, op=ALU.mult))
            W('te', sv)
            for c in range(4):
                transpose_to(attnT[:, c, :], attnbf[:, c * TL:(c + 1) * TL])
            W('te', sv)
            for half in range(2):
                mmgroup(pc0, np.s_[:, half * 512:(half + 1) * 512],
                        [(attnT[:, c, :], wA["wout"][:, c, half * 512:(half + 1) * 512])
                         for c in range(4)], [])
            W('ve', stc)
            EI('ve', sv, lambda e: e.tensor_tensor(out=part_s[:], in0=pc0.t[:, 0:1024],
                                                   in1=x_s[:], op=ALU.add))
            pc0.done(sv)
            W('sy', sv)
            EI('sy', d, (lambda e, b=b, T0=T0: e.dma_start(
                out=part_d[b, T0:T0 + TL, :], in_=part_s[:])), 16)

    def ffn_phase(half):
        f0 = half * (DFF // 2)
        wiT3 = win["wiT"].rearrange("(k p) f -> k p f", p=TL)
        woT3 = win["woT"].rearrange("(k p) dd -> k p dd", p=TL)
        AT3 = win["AT"].rearrange("(k p) c -> k p c", p=TL)
        gwT3 = win["gwT"].rearrange("(k p) c -> k p c", p=TL)
        jobs = []
        for kc in range(8):
            jobs.append((wiT3[kc, :, f0:f0 + 2048], wiT_s[:, kc, :], TL, 2048))
        for kc in range(16):
            jobs.append((woT3[f0 // TL + kc, :, :], woT_s[:, kc, :], TL, 1024))
        for ee in range(NE):
            jobs.append((win["BT"][ee, :, f0:f0 + 2048], BT_s[0:16, ee, :], 16, 2048))
        jobs.append((win["bi"][:, f0:f0 + 2048], bi_s[0:1, 0:2048], 1, 2048))
        if half == 0:
            for kc in range(8):
                jobs.append((AT3[kc], AT_s[:, kc, :], TL, NE * TL))
            for kc in range(8):
                jobs.append((gwT3[kc], gwT_s[:, kc, :], TL, TL))
            jobs.append((win["bo"], bo_s[:], 1, D))
            jobs.append((win["gb"], gbr[:], 1, TL))
        bufs = (stg, stg2)
        ccnt, dcnt = [], []
        for i, (src, dst, p, c) in enumerate(jobs):
            bf = bufs[i % 2]
            if i >= 2:
                W('sy', sv, ccnt[i - 2])
            EI('sy', d, (lambda e, src=src, bf=bf, p=p, c=c: e.dma_start(
                out=bf[0:p, 0:c], in_=src)), 16)
            dcnt.append(d.n)
            W('ve', d, dcnt[i])
            EI('ve', sv, (lambda e, dst=dst, bf=bf, p=p, c=c: e.tensor_copy(
                out=dst, in_=bf[0:p, 0:c])))
            ccnt.append(sv.n)
        W('sy', sv)
        W('te', sv)
        W('sc', sv)
        W('gp', sv)

        for b in range(NB):
            for T in range(NT):
                T0 = T * TL
                EI('sy', d, (lambda e, b=b, T=T: e.dma_start(
                    out=xnT_b[:].rearrange("p c t -> p (c t)"), in_=xnt_d[b, T])), 16)
                EI('sy', d, (lambda e, b=b, T0=T0: e.dma_start(
                    out=partb[:], in_=part_d[b, T0:T0 + TL, :])), 16)
                W('te', d)
                W('ve', d)
                W('sc', sv)
                W('gp', sv)
                ti = b * NT + T
                gts = gates_all[:, ti * NE:(ti + 1) * NE]
                if half == 0:
                    mms = [(xnT_b[:, k, :], gwT_s[:, k, :]) for k in range(8)]
                    mms.append((ones_r[:], gbr[:]))
                    mmgroup(pa0, np.s_[:, 0:TL], mms,
                            [('ve', sv, lambda e: e.tensor_copy(out=probs[:], in_=pa0.t[:, 0:NE]))])
                    EI('ve', sv, lambda e: e.tensor_reduce(out=c1[:], in_=probs[:], axis=AX.X,
                                                           op=ALU.max, negate=True))
                    W('ve', sv)
                    EI('ve', sv, lambda e: e.tensor_scalar(out=probs[:], in0=probs[:], scalar1=c1[:],
                                                           scalar2=None, op0=ALU.add))
                    Wsp('sc', sv)
                    EI('sc', sc, lambda e: e.activation(out=probs[:], in_=probs[:], func=AF.Exp))
                    Wsp('ve', sc)
                    EI('ve', sv, lambda e: e.tensor_reduce(out=c1[:], in_=probs[:], axis=AX.X, op=ALU.add))
                    W('ve', sv)
                    EI('ve', sv, lambda e: e.reciprocal(out=c1[:], in_=c1[:]))
                    W('ve', sv)
                    EI('ve', sv, lambda e: e.tensor_scalar(out=probs[:], in0=probs[:], scalar1=c1[:],
                                                           scalar2=None, op0=ALU.mult))
                    EI('ve', sv, lambda e: e.tensor_reduce(out=c1[:], in_=probs[:], axis=AX.X, op=ALU.max))
                    W('ve', sv)
                    EI('ve', sv, lambda e: e.tensor_scalar(out=dcol[:], in0=probs[:], scalar1=c1[:],
                                                           scalar2=None, op0=ALU.is_equal))
                    EI('ve', sv, lambda e: e.scalar_tensor_tensor(out=dcol[:], in0=dcol[:], scalar=-2.0,
                                                                  in1=probs[:], op0=ALU.mult, op1=ALU.add))
                    EI('ve', sv, lambda e: e.tensor_reduce(out=c2[:], in_=dcol[:], axis=AX.X, op=ALU.max))
                    W('ve', sv)
                    EI('ve', sv, lambda e: e.tensor_tensor(out=c3[:], in0=c1[:], in1=c2[:], op=ALU.add))
                    W('ve', sv)
                    EI('ve', sv, lambda e, gts=gts: e.tensor_scalar(out=gts, in0=probs[:], scalar1=c2[:],
                                                                    scalar2=c3[:], op0=ALU.is_ge, op1=ALU.mult))
                    EI('ve', sv, lambda e, ti=ti: e.tensor_scalar(out=c3_all[:, ti:ti + 1], in0=c3[:], scalar1=2.0,
                                                                  scalar2=None, op0=ALU.mult))
                else:
                    EI('ve', sv, lambda e: e.memset(gpk[:], 0.0))
                    W('ve', sv)
                    EI('ve', sv, lambda e, ti=ti: e.tensor_copy(out=gpk[:, 0:1], in_=c3_all[:, ti:ti + 1]))
                    W('te', sv)
                    ptr0.guard()
                    EI('te', stc, lambda e: e.transpose(ptr0.t[0:16, :], gpk[:], ident[:]))
                    W('ve', stc)
                    EI('ve', sv, lambda e: e.tensor_copy(out=gsr[:], in_=ptr0.t[0:1, 0:TL]))
                    ptr0.done(sv)
                for ee in range(NE):
                    mmgroup(pa1, np.s_[:, 0:TL],
                            [(AT_s[:, k, ee * TL:(ee + 1) * TL], xnT_b[:, k, :])
                             for k in range(8)],
                            [('ve', sv, (lambda e, ee=ee: e.tensor_copy(
                                out=la_s[0:16, ee * TL:(ee + 1) * TL],
                                in_=pa1.t[0:16, 0:TL])))])
                for grp in range(2):
                    n0 = grp * 1024
                    pb = (pc0, pc1)[grp]
                    for nn in range(2):
                        sub = [(xnT_b[:, k, :],
                                wiT_s[:, k, n0 + nn * 512:n0 + nn * 512 + 512])
                               for k in range(8)]
                        sub.append((ones_r[:],
                                    bi_s[0:1, n0 + nn * 512:n0 + nn * 512 + 512]))
                        ev_ = [] if nn == 0 else [
                            ('ve', sv, (lambda e, grp=grp, pb=pb: e.tensor_copy(
                                out=base_s[:, grp * 1024:(grp + 1) * 1024],
                                in_=pb.t[:, 0:1024])))]
                        mmgroup(pb, np.s_[:, nn * 512:(nn + 1) * 512], sub, ev_)
                    pb.done(sv)
                W('gp', sv)
                EI('gp', sg, lambda e: e.memset(Hacc[:], 0.0))
                hcnt = []
                lbufs = (lorab, sadd)
                i = 0
                for ee in range(NE):
                    for grp in range(2):
                        pb = (pc0, pc1)[grp]
                        lb = lbufs[i % 2]
                        if i >= 2:
                            W('ve', sg, hcnt[i - 2])
                        for nn in range(2):
                            n0 = grp * 1024 + nn * 512
                            ev_ = [] if nn == 0 else [
                                ('ve', sv, (lambda e, pb=pb, ee=ee, gts=gts, lb=lb: e.tensor_scalar(
                                    out=lb[:], in0=pb.t[:, 0:1024], scalar1=0.0,
                                    scalar2=gts[:, ee:ee + 1], op0=ALU.max, op1=ALU.mult)))]
                            mmgroup(pb, np.s_[:, nn * 512:(nn + 1) * 512],
                                    [(la_s[0:16, ee * TL:(ee + 1) * TL],
                                      BT_s[0:16, ee, n0:n0 + 512]),
                                     (ident[:], base_s[:, n0:n0 + 512])],
                                    ev_)
                        pb.done(sv)
                        W('gp', sv)
                        EI('gp', sg, (lambda e, grp=grp, lb=lb: e.tensor_tensor(
                            out=Hacc[:, grp * 1024:(grp + 1) * 1024],
                            in0=Hacc[:, grp * 1024:(grp + 1) * 1024],
                            in1=lb[:], op=ALU.add)))
                        hcnt.append(sg.n)
                        i += 1
                W('ve', sg)
                EI('ve', sv, lambda e: e.memset(spc[:, 0:8], 0.0))
                EI('ve', sv, lambda e: e.tensor_copy(out=Hbf[:], in_=Hacc[:]))
                W('te', sv)
                for c in range(16):
                    transpose_to(HT[:, c, :], Hbf[:, c * TL:(c + 1) * TL])
                W('te', sv)
                for nn in range(2):
                    sub = [(HT[:, c, :], woT_s[:, c, nn * 512:(nn + 1) * 512])
                           for c in range(16)]
                    if half == 1:
                        sub.append((gsr[:], bo_s[0:1, nn * 512:(nn + 1) * 512]))
                    mmgroup(pc0, np.s_[:, nn * 512:(nn + 1) * 512], sub, [])
                W('ve', stc)
                EI('ve', sv, lambda e: e.tensor_tensor(out=outb[:], in0=pc0.t[:, 0:1024],
                                                       in1=partb[:], op=ALU.add))
                pc0.done(sv)
                W('sy', sv)
                dst = part_d if half == 0 else out_d
                EI('sy', d, (lambda e, b=b, T0=T0, dst=dst: e.dma_start(
                    out=dst[b, T0:T0 + TL, :], in_=outb[:])), 16)

    for b in range(NB):
        keys_phase(b)
        attn_phase(b)
    ffn_phase(0)
    ffn_phase(1)
    W('sy', d)
    W('sy', g)

    with nc.Block() as block:
        def mk(ek):
            def run(eng):
                for k, fn in prog:
                    if k == ek:
                        fn(eng)
            return run
        block.sync(mk('sy'))
        block.vector(mk('ve'))
        block.scalar(mk('sc'))
        block.tensor(mk('te'))
        block.gpsimd(mk('gp'))

    stack.close()
    return nc


def kernel(**inputs):
    rd = np.asarray(inputs["L0_attn_raw_delta"], dtype=np.float32)
    assert rd.shape == (16,)
    CT, ST, CT2, ST2, CK, SK = _host_consts()
    _MC = _mask_consts()
    x = np.ascontiguousarray(np.asarray(inputs["L0_attn_x"], dtype=np.float32))
    f32 = lambda a: np.ascontiguousarray(np.asarray(a, dtype=np.float32))
    base_map = {
        "wdkv": f32(inputs["L0_attn_W_dkv"]), "wuk": f32(inputs["L0_attn_W_uk"]),
        "wuv": f32(inputs["L0_attn_W_uv"]), "wdq": f32(inputs["L0_attn_W_dq"]),
        "wuq": f32(inputs["L0_attn_W_uq"]), "wqr": f32(inputs["L0_attn_W_qr"]),
        "wkr": f32(inputs["L0_attn_W_kr"]), "wout": f32(inputs["L0_attn_W_out"]),
        "iwq": f32(inputs["L0_attn_idx_wq"]), "iwk": f32(inputs["L0_attn_idx_wk"]),
        "iw": np.tile(np.asarray(inputs["L0_attn_idx_w"], dtype=np.float32)[None, :], (TL, 1)),
        "wiT": f32(np.asarray(inputs["L0_ffn_wi"], dtype=np.float32).T),
        "woT": f32(np.asarray(inputs["L0_ffn_wo"], dtype=np.float32).T),
        "gwT": f32(np.pad(np.asarray(inputs["L0_ffn_gate_w"], dtype=np.float32).T,
                      ((0, 0), (0, TL - NE)))),
        "AT": f32(np.pad(np.transpose(np.asarray(inputs["L0_ffn_lora_A"], dtype=np.float32),
                                  (2, 0, 1)), ((0, 0), (0, 0), (0, TL - RLO))).reshape(D, NE * TL)),
        "BT": f32(np.transpose(np.asarray(inputs["L0_ffn_lora_B"], dtype=np.float32), (0, 2, 1))),
        "bi": f32(np.asarray(inputs["L0_ffn_bi"], dtype=np.float32).reshape(1, DFF)),
        "bo": f32(np.asarray(inputs["L0_ffn_bo"], dtype=np.float32).reshape(1, D)),
        "gb": f32(np.pad(np.asarray(inputs["L0_ffn_gate_b"], dtype=np.float32).reshape(1, NE),
                     ((0, 0), (0, TL - NE)))),
        "CT": CT, "ST": ST, "CT2": CT2, "ST2": ST2, "CK": CK, "SK": SK,
        "K128": _MC[0], "NEGM": _MC[1], "K0": _MC[2], "SURR0": _MC[3],
        "BANDM": _MC[4], "identw": _MC[5],
    }
    nc = build_nc()
    in_maps = []
    for c in range(8):
        m = dict(base_map)
        m["x"] = np.ascontiguousarray(x[2 * c:2 * c + 2])
        in_maps.append(m)
    res = run_bass_kernel_spmd(nc, in_maps, core_ids=list(range(8)))
    out = np.concatenate([np.asarray(r["out"]) for r in res.results], axis=0)
    return out.astype(np.float32)


# version 12
# speedup vs baseline: 1.0881x; 1.0041x over previous
import contextlib
import numpy as np
import ml_dtypes
import concourse.bass as bass
import concourse.mybir as mybir
from concourse.bass_utils import run_bass_kernel_spmd
from concourse.bass_core_rust import IndirectOffsetOnAxis

F32 = mybir.dt.float32
BF16 = mybir.dt.bfloat16
I32 = mybir.dt.int32
U32 = mybir.dt.uint32
AF = mybir.ActivationFunctionType
ALU = mybir.AluOpType
AX = mybir.AxisListType
bfdt = ml_dtypes.bfloat16

L, D, NB, NT, TL = 2048, 1024, 2, 16, 128
NH, DH, DR = 8, 64, 32
NIH = 4
DFF, NE, RLO = 4096, 8, 16
ROW = 512 + 512 + 32
GSL = 16
NSW = 6
SCALE = (DH + DR) ** -0.5
NEG = -3.0e38
EPS = 1e-5
_PB = 196608


def _host_consts():
    th = 10000.0 ** (-np.arange(16, dtype=np.float64) / 16.0)

    def pack(ang):
        c = np.cos(ang).astype(np.float32)
        s = np.sin(ang).astype(np.float32)
        return (c.reshape(NT, TL, 16).transpose(1, 0, 2).reshape(TL, NT * 16).copy(),
                s.reshape(NT, TL, 16).transpose(1, 0, 2).reshape(TL, NT * 16).copy())
    CT, ST = pack(np.arange(L, dtype=np.float64)[:, None] * th[None, :])
    CT2, ST2 = pack((2.0 * np.arange(L, dtype=np.float64)[:, None] - 63.0) * th[None, :])
    ang_k = np.arange(96, dtype=np.float64)[:, None] * th[None, :]
    CK = np.repeat(np.cos(ang_k).astype(np.float32).reshape(1, 96 * 16), TL, axis=0).copy()
    SK = np.repeat(np.sin(ang_k).astype(np.float32).reshape(1, 96 * 16), TL, axis=0).copy()
    return CT, ST, CT2, ST2, CK, SK



def _mask_consts():
    p = np.arange(TL)[:, None]
    s = np.arange(TL)[None, :]
    K128 = (p - s >= 64).astype(np.float32)
    u = np.arange(TL)[None, :]
    K128 = (u <= p).astype(np.float32)
    NEGM = np.where(u <= p, 0.0, NEG).astype(np.float32)
    d = (p - s).astype(np.float64)
    K0 = (d >= 64).astype(np.float32)
    SURR = np.where((d >= 0) & (d <= 63), 1e30 + d * 1e24,
                    np.where(d < 0, -1e30 + d * 1e24, 0.0)).astype(np.float32)
    u2 = np.arange(256)[None, :]
    BANDM = np.where((u2 >= p + 65) & (u2 <= p + 128), 0.0, NEG).astype(np.float32)
    BANDM4 = np.tile(BANDM, (1, 4)).copy()
    IDENT = np.eye(TL, dtype=np.float32).astype(bfdt)
    return K128, NEGM, K0, SURR, BANDM4, IDENT


def build_nc(dbg=False):
    nc = bass.Bass("TRN2", target_bir_lowering=False)

    x_d = nc.dram_tensor("x", [NB, L, D], F32, kind="ExternalInput").ap()
    out_d = nc.dram_tensor("out", [NB, L, D], F32, kind="ExternalOutput").ap()
    win = {}
    for name, shape in [
        ("wdkv", [D, 256]), ("wuk", [256, 512]), ("wuv", [256, 512]),
        ("wdq", [D, 512]), ("wuq", [512, 512]), ("wqr", [512, 256]),
        ("wkr", [D, DR]), ("wout", [512, D]), ("iwq", [D, 256]), ("iwk", [D, 64]),
        ("iw", [TL, NIH]),
        ("wiT", [D, DFF]), ("woT", [DFF, D]), ("gwT", [D, TL]),
        ("AT", [D, NE * TL]), ("BT", [NE, RLO, DFF]),
        ("bi", [1, DFF]), ("bo", [1, D]), ("gb", [1, TL]),
        ("CT", [TL, NT * 16]), ("ST", [TL, NT * 16]),
        ("CT2", [TL, NT * 16]), ("ST2", [TL, NT * 16]),
        ("CK", [TL, 96 * 16]), ("SK", [TL, 96 * 16]),
        ("K128", [TL, TL]), ("NEGM", [TL, TL]),
        ("K0", [TL, TL]), ("SURR0", [TL, TL]), ("BANDM", [TL, 1024]),
    ]:
        win[name] = nc.dram_tensor(name, shape, F32, kind="ExternalInput").ap()
    ident_d = nc.dram_tensor("identw", [TL, TL], BF16, kind="ExternalInput").ap()
    kvtab_d = nc.dram_tensor("kvtab", [L, ROW], BF16, kind="Internal").ap()
    xnt_d = nc.dram_tensor("xnt", [NB, NT, TL, 8 * TL], BF16, kind="Internal").ap()
    part_d = nc.dram_tensor("part", [NB, L, D], F32, kind="Internal").ap()

    cur = [16448]

    def alloc(name, shape, dt):
        per = int(np.prod(shape[1:])) * mybir.dt.size(dt)
        per = (per + 31) // 32 * 32
        off = cur[0]
        cur[0] = off + per
        assert cur[0] <= _PB - 64, f"sbuf overflow at {name}"
        return nc.alloc_sbuf_tensor_at(name, shape, dt, offset=off)

    ident = alloc("ident", [TL, TL], BF16)
    ones_r = alloc("ones_r", [1, TL], BF16)
    cstc = alloc("cstc", [TL, 8], F32)
    spc = alloc("spc", [TL, 8], F32)
    spc2 = alloc("spc2", [TL, 8], F32)
    spc3 = alloc("spc3", [TL, 8], F32)

    a0 = cur[0]
    CT_s = alloc("CT_s", [TL, NT * 16], F32)
    ST_s = alloc("ST_s", [TL, NT * 16], F32)
    CT2_s = alloc("CT2_s", [TL, NT * 16], F32)
    ST2_s = alloc("ST2_s", [TL, NT * 16], F32)
    CK_s = alloc("CK_s", [TL, 96 * 16], BF16)
    SK_s = alloc("SK_s", [TL, 96 * 16], BF16)
    BANDM_s = alloc("BANDM_s", [TL, 1024], F32)
    iw_s = alloc("iw_s", [TL, NIH], F32)
    K128s = alloc("K128s", [TL, TL], F32)
    NEGMs = alloc("NEGMs", [TL, TL], F32)
    K0s = alloc("K0s", [TL, TL], F32)
    SURR0s = alloc("SURR0s", [TL, TL], F32)
    wA = {}
    for name, shape in [
        ("wdkv", [TL, 8, 256]), ("wuk", [TL, 2, 512]), ("wuv", [TL, 2, 512]),
        ("wdq", [TL, 8, 512]), ("wuq", [TL, 4, 512]), ("wqr", [TL, 4, 256]),
        ("wkr", [TL, 8, DR]), ("wout", [TL, 4, D]), ("iwq", [TL, 8, 256]),
        ("iwk", [TL, 8, 64]),
    ]:
        wA[name] = alloc("A_" + name, shape, BF16)
    kiT = alloc("kiT", [64, L], BF16)
    G = alloc("G", [TL, GSL, ROW], BF16)
    prods = alloc("prods", [TL, NH * GSL * DH], BF16)
    I_a = alloc("I_a", [TL, L], F32)
    I_b = alloc("I_b", [TL, L], F32)
    bscore = I_b
    kcb = alloc("kcb", [TL, 2, 512], BF16)
    VSb = alloc("VSb", [TL, 2, 512], BF16)
    kmub = alloc("kmub", [TL, 2, DR], BF16)
    rotb = alloc("rotb", [TL, TL], BF16)
    kcTb = alloc("kcTb", [TL, 4, 256], BF16)
    krTt = alloc("krTt", [96, 2, TL], BF16)
    qcT = alloc("qcT", [TL, 4, TL], BF16)
    qropeT = alloc("qropeT", [96, NH, TL], BF16)
    qropeb = alloc("qropeb", [TL, NH, TL], BF16)
    battn = alloc("battn", [TL, L], BF16)
    batT = alloc("batT", [TL, 2, TL], BF16)
    x_s = alloc("x_s", [TL, D], F32)
    xn_s = I_a[:, 0:D]
    xnbf = alloc("xnbf", [TL, D], BF16)
    xnT = alloc("xnT", [TL, 8, TL], BF16)
    tab = alloc("tab", [TL, ROW], BF16)
    ckvT = alloc("ckvT", [TL, 2, TL], BF16)
    cqT = alloc("cqT", [TL, 4, TL], BF16)
    qiT = alloc("qiT", [64, NIH, TL], BF16)
    qc = alloc("qc", [TL, 512], BF16)
    qr = alloc("qr", [TL, 256], F32)
    mu = alloc("mu", [TL, 256], F32)
    qt1 = alloc("qt1", [TL, 128], BF16)
    qt2 = alloc("qt2", [TL, 128], BF16)
    kt1 = alloc("kt1", [TL, GSL * 16], BF16)
    kt2 = alloc("kt2", [TL, GSL * 16], BF16)
    ktt = alloc("ktt", [TL, GSL * 16], BF16)
    rp = battn
    rp2 = alloc("rp2", [TL, NH * GSL * 16], BF16)
    Ssc = alloc("Ssc", [TL, NH * GSL], F32)
    Ss2 = alloc("Ss2", [TL, NH * GSL], F32)
    ebuf = alloc("ebuf", [TL, NH * GSL], BF16)
    mrun = alloc("mrun", [TL, NH], F32)
    mnew = alloc("mnew", [TL, NH], F32)
    alph = alloc("alph", [TL, NH], F32)
    dnm = alloc("dnm", [TL, NH], F32)
    rpart = alloc("rpart", [TL, NH], F32)
    Aacc = alloc("Aacc", [TL, 512], F32)
    avp = alloc("avp", [TL, 512], F32)
    attnbf = alloc("attnbf", [TL, 512], BF16)
    attnT = alloc("attnT", [TL, 4, TL], BF16)
    part_s = alloc("part_s", [TL, D], F32)
    idx96 = alloc("idx96", [TL, 96], U32)
    v8 = alloc("v8", [TL, 8], F32)
    col1 = alloc("col1", [TL, 1], F32)
    col2 = alloc("col2", [TL, 1], F32)
    w1t = alloc("w1t", [TL, 512], F32)
    assert cur[0] <= _PB - 64

    cur[0] = a0
    wiT_s = alloc("wiT_s", [TL, 8, DFF // 2], BF16)
    woT_s = alloc("woT_s", [TL, 16, D], BF16)
    BT_s = alloc("BT_s", [16, NE, DFF // 2], BF16)
    AT_s = alloc("AT_s", [TL, 8, NE * TL], BF16)
    gwT_s = alloc("gwT_s", [TL, 8, TL], BF16)
    bi_s = alloc("bi_s", [1, DFF // 2], BF16)
    bo_s = alloc("bo_s", [1, D], BF16)
    gbr = alloc("gbr", [1, TL], BF16)
    Hacc = alloc("Hacc", [TL, DFF // 2], F32)
    stg = Hacc
    base_s = alloc("base_s", [TL, DFF // 2], BF16)
    lorab = alloc("lorab", [TL, 1024], BF16)
    sadd = alloc("sadd", [TL, 1024], BF16)
    Hbf = alloc("Hbf", [TL, DFF // 2], BF16)
    HT = alloc("HT", [TL, 16, TL], BF16)
    xnT_b = alloc("xnT_b", [TL, 8, TL], BF16)
    partb = alloc("partb", [TL, D], F32)
    outb = alloc("outb", [TL, D], F32)
    la_s = alloc("la_s", [16, NE * TL], BF16)
    probs = alloc("probs", [TL, NE], F32)
    gates = alloc("gates", [TL, NE], F32)
    dcol = alloc("dcol", [TL, NE], F32)
    c1 = alloc("c1", [TL, 1], F32)
    c2 = alloc("c2", [TL, 1], F32)
    c3 = alloc("c3", [TL, 1], F32)
    gpk = alloc("gpk", [TL, 16], BF16)
    gsr = alloc("gsr", [1, TL], BF16)
    gates_all = alloc("gates_all", [TL, NB * NT * NE], F32)
    c3_all = alloc("c3_all", [TL, NB * NT], F32)
    stg2 = alloc("stg2", [TL, DFF // 2], F32)
    assert cur[0] <= _PB - 64

    prog = []

    class Ctr:
        __slots__ = ("sem", "n")

        def __init__(self, sem):
            self.sem, self.n = sem, 0

    def E(ek, fn):
        prog.append((ek, fn))

    def EI(ek, ctr, fn, k=1):
        selfw = ek in ("ve", "sc", "gp")
        n_after = ctr.n + k

        def go(eng, fn=fn, s=ctr.sem, k=k, selfw=selfw, n_after=n_after):
            fn(eng).then_inc(s, k)
            if selfw:
                eng.wait_ge(s, n_after)
        prog.append((ek, go))
        ctr.n += k

    def W(ek, ctr, val=None):
        v = ctr.n if val is None else val
        prog.append((ek, lambda eng, s=ctr.sem, v=v: eng.wait_ge(s, v)))

    def Wsp(ek, ctr):
        W(ek, ctr)
        if ek == 've':
            EI('ve', sv, lambda e: e.memset(spc[:, 0:8], 0.0))
        elif ek == 'sc':
            EI('sc', sc, lambda e: e.memzero(spc2[:, 0:8]))

    stack = contextlib.ExitStack()
    d = Ctr(stack.enter_context(nc.semaphore("d")))
    g = Ctr(stack.enter_context(nc.semaphore("g")))
    sv = Ctr(stack.enter_context(nc.semaphore("sv")))
    sc = Ctr(stack.enter_context(nc.semaphore("sc")))
    stc = Ctr(stack.enter_context(nc.semaphore("stc")))
    sg = Ctr(stack.enter_context(nc.semaphore("sg")))

    class Bank:
        def __init__(self, name, shape, dt):
            self.t = nc.alloc_psum_tensor(name, shape, dt)
            self.rel = None

        def guard(self):
            if self.rel is not None:
                W('te', self.rel[0], self.rel[1])

        def done(self, ctr):
            self.rel = (ctr, ctr.n)

    pa0 = Bank("pa0", [TL, 512], F32)
    pa1 = Bank("pa1", [TL, 512], F32)
    pc0 = Bank("pc0", [TL, 1024], F32)
    pc1 = Bank("pc1", [TL, 1024], F32)
    ptr0 = Bank("ptr0", [TL, TL], BF16)
    ptr1 = Bank("ptr1", [TL, TL], BF16)

    def mmgroup(bank, sl, mms, evacs):
        bank.guard()
        n = len(mms)
        for i, (lh, rh) in enumerate(mms):
            fn = (lambda e, lh=lh, rh=rh, s=(i == 0), p=(i == n - 1), b=bank, sl=sl:
                  e.matmul(b.t[sl], lh, rh, start=s, stop=p))
            if i == n - 1:
                EI('te', stc, fn)
            else:
                E('te', fn)
        last = None
        for ek, ctr, fn in evacs:
            W(ek, stc)
            EI(ek, ctr, fn)
            last = ctr
        if last is not None:
            bank.done(last)

    tr_i = [0]

    def transpose_to(dst, src, r0=0, r1=TL):
        bank = (ptr0, ptr1)[tr_i[0] % 2]
        tr_i[0] += 1
        bank.guard()
        EI('te', stc, lambda e, b=bank, s=src: e.transpose(b.t[:], s, ident[:]))
        W('ve', stc)
        EI('ve', sv, lambda e, b=bank, dd=dst, r0=r0, r1=r1:
           e.tensor_copy(out=dd, in_=b.t[r0:r1, :]))
        bank.done(sv)

    EI('ve', sv, lambda e: e.memset(ones_r[:], 1.0))
    _consts = [EPS, 1.0 / D, -1.0, 1e24, 1e30]
    for ci, cval in enumerate(_consts):
        EI('ve', sv, lambda e, ci=ci, cval=cval: e.memset(cstc[:, ci:ci + 1], cval))
        nc.const_aps.register(F32, cval, cstc[:, ci:ci + 1])
    EI('sy', d, lambda e: e.dma_start(out=ident[:], in_=ident_d), 16)
    for nm, t in [("CT", CT_s), ("ST", ST_s), ("CT2", CT2_s), ("ST2", ST2_s),
                  ("iw", iw_s), ("BANDM", BANDM_s),
                  ("K128", K128s), ("NEGM", NEGMs), ("K0", K0s), ("SURR0", SURR0s)]:
        EI('sy', d, lambda e, a=win[nm], t=t: e.dma_start(out=t[:], in_=a), 16)
    for nm, t in [("CK", CK_s), ("SK", SK_s)]:
        EI('sy', d, lambda e, a=win[nm]: e.dma_start(out=I_a[:, 0:1536], in_=a), 16)
        W('ve', d)
        EI('ve', sv, lambda e, t=t: e.tensor_copy(out=t[:], in_=I_a[:, 0:1536]))
        W('sy', sv)

    for nm in ["wdkv", "wuk", "wuv", "wdq", "wuq", "wqr", "wkr", "wout", "iwq", "iwk"]:
        K, C = wA[nm].shape[1], wA[nm].shape[2]
        src3 = win[nm].rearrange("(k p) c -> k p c", p=TL)
        for k in range(K):
            EI('sy', d, lambda e, s=src3, k=k, C=C:
               e.dma_start(out=I_a[:, 0:C], in_=s[k]), 16)
            W('ve', d)
            EI('ve', sv, lambda e, nm=nm, k=k, C=C:
               e.tensor_copy(out=wA[nm][:, k, :], in_=I_a[:, 0:C]))
            W('sy', sv)

    def emit_ln():
        EI('ve', sv, lambda e: e.tensor_reduce(out=col1[:], in_=x_s[:], axis=AX.X, op=ALU.add))
        W('ve', sv)
        EI('ve', sv, lambda e: e.tensor_scalar(out=col1[:], in0=col1[:], scalar1=1.0 / D,
                                               scalar2=None, op0=ALU.mult))
        W('ve', sv)
        EI('ve', sv, lambda e: e.tensor_scalar(out=xn_s[:], in0=x_s[:], scalar1=col1[:],
                                               scalar2=None, op0=ALU.subtract))
        Wsp('sc', sv)
        EI('sc', sc, lambda e: e.activation(out=I_b[:, 0:D], in_=xn_s[:], func=AF.Square))
        Wsp('ve', sc)
        EI('ve', sv, lambda e: e.tensor_reduce(out=col2[:], in_=I_b[:, 0:D], axis=AX.X, op=ALU.add))
        Wsp('sc', sv)
        EI('sc', sc, lambda e: e.activation(out=col2[:], in_=col2[:], func=AF.Sqrt,
                                            scale=1.0 / D, bias=EPS))
        Wsp('ve', sc)
        EI('ve', sv, lambda e: e.reciprocal(out=col2[:], in_=col2[:]))
        W('ve', sv)
        EI('ve', sv, lambda e: e.tensor_scalar(out=xn_s[:], in0=xn_s[:], scalar1=col2[:],
                                               scalar2=None, op0=ALU.mult))
        EI('ve', sv, lambda e: e.tensor_copy(out=xnbf[:], in_=xn_s[:]))
        W('te', sv)
        for c in range(8):
            transpose_to(xnT[:, c, :], xnbf[:, c * TL:(c + 1) * TL])

    def keys_phase(b):
        for T in range(NT):
            T0 = T * TL
            W('sy', g)
            EI('sy', d, lambda e, b=b, T0=T0: e.dma_start(
                out=x_s[:], in_=x_d[b, T0:T0 + TL, :]), 16)
            W('ve', d)
            W('sc', sv)
            emit_ln()
            W('sy', sv)
            EI('sy', d, lambda e, b=b, T=T: e.dma_start(
                out=xnt_d[b, T], in_=xnT[:].rearrange("p c t -> p (c t)")), 16)
            for cc in range(2):
                mmgroup((pa0, pa1)[cc], np.s_[:, 0:TL],
                        [(wA["wdkv"][:, k, cc * TL:(cc + 1) * TL], xnT[:, k, :])
                         for k in range(8)],
                        [('ve', sv, (lambda e, cc=cc, bk=(pa0, pa1)[cc]: e.tensor_copy(
                            out=ckvT[:, cc, :], in_=bk.t[:, 0:TL])))])
            W('te', sv)
            mmgroup(pc0, np.s_[:, 0:512],
                    [(ckvT[:, c, :], wA["wuk"][:, c, :]) for c in range(2)],
                    [('ve', sv, lambda e: e.tensor_copy(out=tab[:, 0:512],
                                                        in_=pc0.t[:, 0:512]))])
            mmgroup(pc1, np.s_[:, 0:512],
                    [(ckvT[:, c, :], wA["wuv"][:, c, :]) for c in range(2)],
                    [('ve', sv, lambda e: e.tensor_copy(out=tab[:, 512:1024],
                                                        in_=pc1.t[:, 0:512]))])
            mmgroup(pa0, np.s_[:, 0:DR],
                    [(xnT[:, k, :], wA["wkr"][:, k, :]) for k in range(8)],
                    [('ve', sv, lambda e: e.tensor_copy(out=w1t[:, 0:DR],
                                                        in_=pa0.t[:, 0:DR]))])
            Wsp('sc', sv)
            EI('sc', sc, lambda e: e.activation(out=w1t[:, 64:64 + DR], in_=w1t[:, 0:DR],
                                                func=AF.Abs))
            EI('sc', sc, lambda e: e.activation(out=w1t[:, 64:64 + DR], in_=w1t[:, 64:64 + DR],
                                                func=AF.Exp, scale=-1.0))
            EI('sc', sc, lambda e: e.activation(out=w1t[:, 64:64 + DR], in_=w1t[:, 64:64 + DR],
                                                func=AF.Ln, bias=1.0))
            EI('sc', sc, lambda e: e.activation(out=w1t[:, 128:128 + DR], in_=w1t[:, 0:DR],
                                                func=AF.Relu))
            Wsp('ve', sc)
            EI('ve', sv, lambda e: e.tensor_tensor(out=tab[:, 1024:1056],
                                                   in0=w1t[:, 64:64 + DR],
                                                   in1=w1t[:, 128:128 + DR], op=ALU.add))
            mmgroup(pa1, np.s_[0:64, 0:TL],
                    [(wA["iwk"][:, k, :], xnT[:, k, :]) for k in range(8)],
                    [('ve', sv, (lambda e, T0=T0: e.tensor_copy(
                        out=kiT[0:64, T0:T0 + TL], in_=pa1.t[0:64, 0:TL])))])
            W('sy', sv)
            EI('sy', d, lambda e, T0=T0: e.dma_start(
                out=kvtab_d[T0:T0 + TL, :], in_=tab[:]), 16)

    def emit_gathers(k0, j0=0, j1=GSL):
        W('gp', sv)
        W('gp', d)
        for j in range(j0, j1):
            EI('gp', g, (lambda e, j=j, k0=k0: e.indirect_dma_start(
                out=G[:, j, :], out_offset=None, in_=kvtab_d,
                in_offset=IndirectOffsetOnAxis(idx96[:, k0 + j:k0 + j + 1], 0))), 16)
        return g.n

    def attn_phase(b):
        for T in range(NT):
            T0 = T * TL
            EI('sy', d, lambda e, b=b, T=T: e.dma_start(
                out=xnT[:].rearrange("p c t -> p (c t)"), in_=xnt_d[b, T]), 16)
            EI('sy', d, lambda e, b=b, T0=T0: e.dma_start(
                out=x_s[:], in_=x_d[b, T0:T0 + TL, :]), 16)
            if T > 0:
                W('sy', stc)
                EI('sy', d, (lambda e, T0=T0: e.dma_start(
                    out=kcb[:],
                    in_=kvtab_d[T0 - TL:T0 + TL, 0:512].rearrange(
                        "(j p) c -> p j c", p=TL))), 16)
                EI('sy', d, (lambda e, T0=T0: e.dma_start(
                    out=VSb[:],
                    in_=kvtab_d[T0 - TL:T0 + TL, 512:1024].rearrange(
                        "(j p) c -> p j c", p=TL))), 16)
                EI('sy', d, (lambda e, T0=T0: e.dma_start(
                    out=kmub[:],
                    in_=kvtab_d[T0 - TL:T0 + TL, 1024:1056].rearrange(
                        "(j p) c -> p j c", p=TL))), 16)
            W('te', d)
            W('ve', d)
            W('ve', sg)
            W('sc', sv)
            W('gp', sv)
            for h in range(NIH):
                mmgroup((pa0, pa1)[h % 2], np.s_[0:64, 0:TL],
                        [(wA["iwq"][:, k, h * 64:(h + 1) * 64], xnT[:, k, :])
                         for k in range(8)],
                        [('ve', sv, (lambda e, h=h, bk=(pa0, pa1)[h % 2]: e.tensor_copy(
                            out=qiT[0:64, h, :], in_=bk.t[0:64, 0:TL])))])
            W('te', sv)
            Wt = T0 + 64 if T > 0 else TL
            nch = (Wt + 511) // 512
            for j in range(nch):
                w = min(512, Wt - j * 512)
                for h in range(NIH):
                    bk = (pa0, pa1)[(j * NIH + h) % 2]
                    lh = qiT[0:64, h, :]
                    rh = kiT[0:64, j * 512:j * 512 + w]
                    if h == 0:
                        mmgroup(bk, np.s_[:, 0:w], [(lh, rh)],
                                [('ve', sv, (lambda e, bk=bk, w=w, j=j: e.tensor_scalar(
                                    out=I_a[:, j * 512:j * 512 + w], in0=bk.t[:, 0:w],
                                    scalar1=0.0, scalar2=iw_s[:, 0:1],
                                    op0=ALU.max, op1=ALU.mult)))])
                    else:
                        mmgroup(bk, np.s_[:, 0:w], [(lh, rh)],
                                [('ve', sv, (lambda e, bk=bk, w=w, h=h: e.tensor_scalar(
                                    out=I_b[:, 0:w], in0=bk.t[:, 0:w], scalar1=0.0,
                                    scalar2=iw_s[:, h:h + 1], op0=ALU.max, op1=ALU.mult)))])
                        EI('ve', sv, (lambda e, w=w, j=j: e.tensor_tensor(
                            out=I_a[:, j * 512:j * 512 + w], in0=I_a[:, j * 512:j * 512 + w],
                            in1=I_b[:, 0:w], op=ALU.add)))
            if T > 0:
                EI('ve', sv, (lambda e, T0=T0: e.tensor_tensor(
                    out=I_a[:, T0 - 64:T0 + 64], in0=I_a[:, T0 - 64:T0 + 64],
                    in1=K128s[:], op=ALU.mult)))
                EI('ve', sv, (lambda e, T0=T0: e.tensor_tensor(
                    out=I_a[:, T0 - 64:T0 + 64], in0=I_a[:, T0 - 64:T0 + 64],
                    in1=NEGMs[:], op=ALU.add)))
                nr, off = 4, 64
            else:
                EI('ve', sv, lambda e: e.tensor_tensor(out=I_a[:, 0:TL], in0=I_a[:, 0:TL],
                                                       in1=K0s[:], op=ALU.mult))
                EI('ve', sv, lambda e: e.tensor_tensor(out=I_a[:, 0:TL], in0=I_a[:, 0:TL],
                                                       in1=SURR0s[:], op=ALU.add))
                nr, off = 12, 0
            gmA = gfA = None
            for r in range(nr):
                cur_t = (I_a, I_b)[r % 2]
                oth_t = (I_b, I_a)[r % 2]
                EI('ve', sv, (lambda e, c=cur_t, Wt=Wt: e.max(out=v8[:], in_=c[:, 0:Wt])))
                W('ve', sv)
                EI('ve', sv, (lambda e, c=cur_t, Wt=Wt, r=r, off=off: e.max_index(
                    out=idx96[:, off + r * 8:off + r * 8 + 8],
                    in_max=v8[:], in_values=c[:, 0:Wt])))
                if r < nr - 1:
                    EI('ve', sv, (lambda e, c=cur_t, o=oth_t, Wt=Wt: e.match_replace(
                        out=o[:, 0:Wt], in_to_replace=v8[:], in_values=c[:, 0:Wt],
                        imm_value=NEG)))
                if r == 0:
                    gmA = emit_gathers(off, 0, 8)
                elif r == 1:
                    gfA = emit_gathers(off, 8, 16)
            EI('ve', sv, lambda e: e.memset(mrun[:], NEG))
            EI('ve', sv, lambda e: e.memset(dnm[:], 0.0))
            EI('ve', sv, lambda e: e.memset(Aacc[:], 0.0))
            for cc in range(4):
                mmgroup((pa0, pa1)[cc % 2], np.s_[:, 0:TL],
                        [(wA["wdq"][:, k, cc * TL:(cc + 1) * TL], xnT[:, k, :])
                         for k in range(8)],
                        [('ve', sv, (lambda e, cc=cc, bk=(pa0, pa1)[cc % 2]: e.tensor_copy(
                            out=cqT[:, cc, :], in_=bk.t[:, 0:TL])))])
            W('te', sv)
            mmgroup(pc0, np.s_[:, 0:512],
                    [(cqT[:, c, :], wA["wuq"][:, c, :]) for c in range(4)],
                    [('ve', sv, lambda e: e.tensor_copy(out=qc[:], in_=pc0.t[:, 0:512]))])
            mmgroup(pc1, np.s_[:, 0:256],
                    [(cqT[:, c, :], wA["wqr"][:, c, :]) for c in range(4)],
                    [('ve', sv, lambda e: e.tensor_copy(out=qr[:], in_=pc1.t[:, 0:256]))])
            Wsp('sc', sv)
            EI('sc', sc, lambda e: e.activation(out=mu[:], in_=qr[:], func=AF.Abs))
            EI('sc', sc, lambda e: e.activation(out=mu[:], in_=mu[:], func=AF.Exp, scale=-1.0))
            EI('sc', sc, lambda e: e.activation(out=mu[:], in_=mu[:], func=AF.Ln, bias=1.0))
            EI('sc', sc, lambda e: e.activation(out=qr[:], in_=qr[:], func=AF.Relu))
            Wsp('ve', sc)
            EI('ve', sv, lambda e: e.tensor_tensor(out=mu[:], in0=mu[:], in1=qr[:], op=ALU.add))
            mu_v = mu[:].rearrange("p (h two i) -> p h two i", h=NH, two=2)
            ct_v = CT_s[:, T * 16:(T + 1) * 16].rearrange("p (o i) -> p o i", o=1).to_broadcast([TL, NH, 16])
            st_v = ST_s[:, T * 16:(T + 1) * 16].rearrange("p (o i) -> p o i", o=1).to_broadcast([TL, NH, 16])
            q1v = qt1[:].rearrange("p (h i) -> p h i", h=NH)
            q2v = qt2[:].rearrange("p (h i) -> p h i", h=NH)
            tmpv = rp[:, 0:NH * 16].rearrange("p (h i) -> p h i", h=NH)
            EI('ve', sv, lambda e, a=mu_v, b=ct_v, o=q1v: e.tensor_tensor(out=o, in0=a[:, :, 0, :], in1=b, op=ALU.mult))
            EI('ve', sv, lambda e, a=mu_v, b=st_v, o=tmpv: e.tensor_tensor(out=o, in0=a[:, :, 1, :], in1=b, op=ALU.mult))
            EI('ve', sv, lambda e, o=q1v, b=tmpv: e.tensor_tensor(out=o, in0=o, in1=b, op=ALU.subtract))
            EI('ve', sv, lambda e, a=mu_v, b=st_v, o=q2v: e.tensor_tensor(out=o, in0=a[:, :, 0, :], in1=b, op=ALU.mult))
            EI('ve', sv, lambda e, a=mu_v, b=ct_v, o=tmpv: e.tensor_tensor(out=o, in0=a[:, :, 1, :], in1=b, op=ALU.mult))
            EI('ve', sv, lambda e, o=q2v, b=tmpv: e.tensor_tensor(out=o, in0=o, in1=b, op=ALU.add))
            if T > 0:
                W('te', sv)
                for c in range(4):
                    transpose_to(qcT[:, c, :], qc[:, c * TL:(c + 1) * TL])
                ct2_v = CT2_s[:, T * 16:(T + 1) * 16].rearrange("p (o i) -> p o i", o=1).to_broadcast([TL, NH, 16])
                st2_v = ST2_s[:, T * 16:(T + 1) * 16].rearrange("p (o i) -> p o i", o=1).to_broadcast([TL, NH, 16])
                s1v = Ssc[:].rearrange("p (h i) -> p h i", h=NH)
                s2v = Ss2[:].rearrange("p (h i) -> p h i", h=NH)
                qb1 = qropeb[:, :, 64:80]
                qb2 = qropeb[:, :, 80:96]
                EI('ve', sv, lambda e, a=mu_v, b=ct2_v, o=s1v: e.tensor_tensor(out=o, in0=a[:, :, 0, :], in1=b, op=ALU.mult))
                EI('ve', sv, lambda e, a=mu_v, b=st2_v, o=s2v: e.tensor_tensor(out=o, in0=a[:, :, 1, :], in1=b, op=ALU.mult))
                EI('ve', sv, lambda e, o=qb1, x=s1v, y=s2v: e.tensor_tensor(out=o, in0=x, in1=y, op=ALU.subtract))
                EI('ve', sv, lambda e, a=mu_v, b=st2_v, o=s1v: e.tensor_tensor(out=o, in0=a[:, :, 0, :], in1=b, op=ALU.mult))
                EI('ve', sv, lambda e, a=mu_v, b=ct2_v, o=s2v: e.tensor_tensor(out=o, in0=a[:, :, 1, :], in1=b, op=ALU.mult))
                EI('ve', sv, lambda e, o=qb2, x=s1v, y=s2v: e.tensor_tensor(out=o, in0=x, in1=y, op=ALU.add))
                W('te', sv)
                for h in range(NH):
                    transpose_to(qropeT[64:96, h, :], qropeb[:, h, :], r0=64, r1=96)
            qc_v = qc[:].rearrange("p (h o dd) -> p h o dd", h=NH, o=1).to_broadcast([TL, NH, GSL, DH])
            gk_v = G[:, :, 0:512].rearrange("p k (h dd) -> p h k dd", h=NH)
            pr_v = prods[:].rearrange("p (h k dd) -> p h k dd", h=NH, k=GSL)
            nu1 = G[:, :, 1024:1040]
            nu2 = G[:, :, 1040:1056]
            k1v = kt1[:].rearrange("p (k i) -> p k i", i=16)
            k2v = kt2[:].rearrange("p (k i) -> p k i", i=16)
            ktv = ktt[:].rearrange("p (k i) -> p k i", i=16)
            q1b = qt1[:].rearrange("p (h o i) -> p h o i", h=NH, o=1).to_broadcast([TL, NH, GSL, 16])
            q2b = qt2[:].rearrange("p (h o i) -> p h o i", h=NH, o=1).to_broadcast([TL, NH, GSL, 16])
            k1b = kt1[:].rearrange("p (o k i) -> p o k i", o=1, i=16).to_broadcast([TL, NH, GSL, 16])
            k2b = kt2[:].rearrange("p (o k i) -> p o k i", o=1, i=16).to_broadcast([TL, NH, GSL, 16])
            rpv = rp[:].rearrange("p (h k i) -> p h k i", h=NH, k=GSL)
            rp2v = rp2[:].rearrange("p (h k i) -> p h k i", h=NH, k=GSL)
            Sv = Ssc[:].rearrange("p (h k) -> p h k", h=NH)
            S2v = Ss2[:].rearrange("p (h k) -> p h k", h=NH)
            ev = ebuf[:].rearrange("p (h k) -> p h k", h=NH)
            mb = mnew[:].rearrange("p (h o) -> p h o", o=1).to_broadcast([TL, NH, GSL])
            ab = alph[:].rearrange("p (h o) -> p h o", o=1).to_broadcast([TL, NH, DH])
            Av = Aacc[:].rearrange("p (h dd) -> p h dd", h=NH)
            eb2 = ebuf[:].rearrange("p (h o k) -> p h o k", h=NH, o=1).to_broadcast([TL, NH, DH, GSL])
            gv_v = G[:, :, 512:1024].rearrange("p k (h dd) -> p h dd k", h=NH)
            pr2 = prods[:].rearrange("p (h dd k) -> p h dd k", h=NH, dd=DH)
            avv = avp[:].rearrange("p (h dd) -> p h dd", h=NH)
            def emit_sweep(k0, gm, gf, pre):
                ckv_ = CK_s[:].rearrange("p (k i) -> p k i", i=16)[:, k0:k0 + GSL, :]
                skv_ = SK_s[:].rearrange("p (k i) -> p k i", i=16)[:, k0:k0 + GSL, :]
                sv_snap = sv.n
                W('gp', g, gf)
                W('gp', sv, sv_snap)
                EI('gp', sg, lambda e: e.memset(spc3[:, 0:8], 0.0))
                EI('gp', sg, lambda e, ckv_=ckv_: e.tensor_tensor(out=k1v, in0=nu1, in1=ckv_, op=ALU.mult))
                EI('gp', sg, lambda e, skv_=skv_: e.tensor_tensor(out=ktv, in0=nu2, in1=skv_, op=ALU.mult))
                EI('gp', sg, lambda e: e.tensor_tensor(out=k1v, in0=k1v, in1=ktv, op=ALU.subtract))
                EI('gp', sg, lambda e, skv_=skv_: e.tensor_tensor(out=k2v, in0=nu1, in1=skv_, op=ALU.mult))
                EI('gp', sg, lambda e, ckv_=ckv_: e.tensor_tensor(out=ktv, in0=nu2, in1=ckv_, op=ALU.mult))
                EI('gp', sg, lambda e: e.tensor_tensor(out=k2v, in0=k2v, in1=ktv, op=ALU.add))
                EI('gp', sg, lambda e: e.tensor_tensor(out=rpv, in0=q1b, in1=k1b, op=ALU.mult))
                EI('gp', sg, lambda e: e.tensor_tensor(out=rp2v, in0=q2b, in1=k2b, op=ALU.mult))
                EI('gp', sg, lambda e: e.tensor_tensor(out=rpv, in0=rpv, in1=rp2v, op=ALU.add))
                sg_snap = sg.n
                for ja, jb, gw in ((0, 8, gm), (8, GSL, gf)):
                    W('ve', g, gw)
                    gkh = G[:, ja:jb, 0:512].rearrange("p k (h dd) -> p h k dd", h=NH)
                    qch = qc[:].rearrange("p (h o dd) -> p h o dd", h=NH, o=1).to_broadcast([TL, NH, jb - ja, DH])
                    EI('ve', sv, (lambda e, prh=pr_v[:, :, ja:jb, :], qch=qch, gkh=gkh:
                                  e.tensor_tensor(out=prh, in0=qch, in1=gkh, op=ALU.mult)))
                    EI('ve', sv, (lambda e, prh=pr_v[:, :, ja:jb, :], svh=Sv[:, :, ja:jb]:
                                  e.tensor_reduce(out=svh, in_=prh, axis=AX.X, op=ALU.add)))
                W('ve', sg, sg_snap)
                EI('ve', sv, lambda e: e.memset(spc[:, 0:8], 0.0))
                EI('ve', sv, lambda e: e.tensor_reduce(out=S2v, in_=rpv, axis=AX.X, op=ALU.add))
                EI('ve', sv, lambda e: e.tensor_tensor(out=Ssc[:], in0=Ssc[:], in1=Ss2[:], op=ALU.add))
                EI('ve', sv, lambda e: e.tensor_reduce(out=rpart[:], in_=Sv, axis=AX.X, op=ALU.max))
                EI('ve', sv, lambda e: e.tensor_tensor(out=mnew[:], in0=mrun[:], in1=rpart[:], op=ALU.max))
                EI('ve', sv, lambda e: e.tensor_tensor(out=alph[:], in0=mrun[:], in1=mnew[:], op=ALU.subtract))
                Wsp('sc', sv)
                EI('sc', sc, lambda e: e.activation(out=alph[:], in_=alph[:], func=AF.Exp, scale=SCALE))
                EI('ve', sv, lambda e: e.tensor_tensor(out=Sv, in0=Sv, in1=mb, op=ALU.subtract))
                Wsp('sc', sv)
                EI('sc', sc, lambda e: e.activation(out=ebuf[:], in_=Ssc[:], func=AF.Exp, scale=SCALE))
                Wsp('ve', sc)
                EI('ve', sv, lambda e: e.tensor_reduce(out=rpart[:], in_=ev, axis=AX.X, op=ALU.add))
                EI('ve', sv, lambda e: e.tensor_tensor(out=dnm[:], in0=dnm[:], in1=alph[:], op=ALU.mult))
                EI('ve', sv, lambda e: e.tensor_tensor(out=dnm[:], in0=dnm[:], in1=rpart[:], op=ALU.add))
                EI('ve', sv, lambda e: e.tensor_copy(out=mrun[:], in_=mnew[:]))
                EI('ve', sv, lambda e: e.tensor_tensor(out=Av, in0=Av, in1=ab, op=ALU.mult))
                EI('ve', sv, lambda e: e.tensor_tensor(out=pr2, in0=eb2, in1=gv_v, op=ALU.mult))
                nxt = None
                if pre is not None:
                    nxt = (emit_gathers(pre, 0, 8), emit_gathers(pre, 8, 16))
                EI('ve', sv, lambda e: e.tensor_reduce(out=avv, in_=pr2, axis=AX.X, op=ALU.add))
                EI('ve', sv, lambda e: e.tensor_tensor(out=Aacc[:], in0=Aacc[:], in1=avp[:], op=ALU.add))
                return nxt

            if T > 0:
                for j in range(2):
                    for cc in range(4):
                        transpose_to(kcTb[:, cc, j * TL:(j + 1) * TL],
                                     kcb[:, j, cc * TL:(cc + 1) * TL])
                for j in range(2):
                    ctk = CT_s[:, (T - 1 + j) * 16:(T + j) * 16]
                    stk = ST_s[:, (T - 1 + j) * 16:(T + j) * 16]
                    m1 = kmub[:, j, 0:16]
                    m2 = kmub[:, j, 16:32]
                    if j == 1:
                        W('ve', stc)
                    EI('ve', sv, lambda e, m1=m1, c=ctk: e.tensor_tensor(out=Ssc[:, 0:16], in0=m1, in1=c, op=ALU.mult))
                    EI('ve', sv, lambda e, m2=m2, s=stk: e.tensor_tensor(out=Ss2[:, 0:16], in0=m2, in1=s, op=ALU.mult))
                    EI('ve', sv, lambda e: e.tensor_tensor(out=rotb[:, 64:80], in0=Ssc[:, 0:16], in1=Ss2[:, 0:16], op=ALU.subtract))
                    EI('ve', sv, lambda e, m1=m1, s=stk: e.tensor_tensor(out=Ssc[:, 0:16], in0=m1, in1=s, op=ALU.mult))
                    EI('ve', sv, lambda e, m2=m2, c=ctk: e.tensor_tensor(out=Ss2[:, 0:16], in0=m2, in1=c, op=ALU.mult))
                    EI('ve', sv, lambda e: e.tensor_tensor(out=rotb[:, 80:96], in0=Ssc[:, 0:16], in1=Ss2[:, 0:16], op=ALU.add))
                    W('te', sv)
                    transpose_to(krTt[64:96, j, :], rotb[:], r0=64, r1=96)
                W('te', sv)
                krf = krTt[64:96, :, :].rearrange("p j t -> p (j t)")
                nxtB = emit_sweep(64, gmA, gfA, 80)
                for h in range(NH):
                    bk = (pc0, pc1)[h // 4]
                    col = (h % 4) * 256
                    hb = 64 * (h % 2)
                    mms = [(qcT[hb:hb + 64, h // 2, :],
                            kcTb[hb:hb + 64, h // 2, :]),
                           (qropeT[64:96, h, :], krf)]
                    ev_ = []
                    if h % 4 == 3:
                        ev_ = [('ve', sv, (lambda e, bk=bk, q=h // 4: e.scalar_tensor_tensor(
                            out=bscore[:, q * 1024:(q + 1) * 1024], in0=bk.t[:, 0:1024],
                            scalar=1.0, in1=BANDM_s[:], op0=ALU.mult, op1=ALU.add)))]
                    mmgroup(bk, np.s_[:, col:col + 256], mms, ev_)
                bs8 = bscore[:].rearrange("p (h u) -> p h u", h=NH)
                mbn256 = mnew[:].rearrange("p (h o) -> p h o", o=1).to_broadcast([TL, NH, 256])
                EI('ve', sv, lambda e, bs8=bs8: e.tensor_reduce(out=rpart[:], in_=bs8, axis=AX.X, op=ALU.max))
                EI('ve', sv, lambda e: e.tensor_tensor(out=mnew[:], in0=mrun[:], in1=rpart[:], op=ALU.max))
                EI('ve', sv, lambda e: e.tensor_tensor(out=alph[:], in0=mrun[:], in1=mnew[:], op=ALU.subtract))
                Wsp('sc', sv)
                EI('sc', sc, lambda e: e.activation(out=alph[:], in_=alph[:], func=AF.Exp, scale=SCALE))
                EI('ve', sv, lambda e, bs8=bs8, mbn256=mbn256: e.tensor_tensor(out=bs8, in0=bs8, in1=mbn256, op=ALU.subtract))
                Wsp('sc', sv)
                EI('sc', sc, lambda e: e.activation(out=bscore[:], in_=bscore[:], func=AF.Exp, scale=SCALE))
                Wsp('ve', sc)
                EI('ve', sv, lambda e, bs8=bs8: e.tensor_reduce(out=rpart[:], in_=bs8, axis=AX.X, op=ALU.add))
                EI('ve', sv, lambda e: e.tensor_tensor(out=dnm[:], in0=dnm[:], in1=alph[:], op=ALU.mult))
                EI('ve', sv, lambda e: e.tensor_tensor(out=dnm[:], in0=dnm[:], in1=rpart[:], op=ALU.add))
                EI('ve', sv, lambda e: e.tensor_copy(out=mrun[:], in_=mnew[:]))
                EI('ve', sv, lambda e: e.tensor_tensor(out=Av, in0=Av, in1=ab, op=ALU.mult))
                EI('ve', sv, lambda e: e.tensor_copy(out=battn[:], in_=bscore[:]))
                W('te', sv)
                pa0.guard()
                for h in range(NH):
                    for c in range(2):
                        transpose_to(batT[:, c, :], battn[:, 256 * h + 128 * c:256 * h + 128 * c + 128])
                    W('te', sv)
                    E('te', (lambda e, h=h: e.matmul(
                        pa0.t[:, 64 * h:64 * h + 64], batT[:, 0, :],
                        VSb[:, 0, 64 * h:64 * h + 64], start=True, stop=False)))
                    EI('te', stc, (lambda e, h=h: e.matmul(
                        pa0.t[:, 64 * h:64 * h + 64], batT[:, 1, :],
                        VSb[:, 1, 64 * h:64 * h + 64], start=False, stop=True)))
                W('ve', stc)
                EI('ve', sv, lambda e: e.tensor_tensor(out=Aacc[:], in0=pa0.t[:, 0:512],
                                                       in1=Aacc[:], op=ALU.add))
                pa0.done(sv)
                emit_sweep(80, nxtB[0], nxtB[1], None)
            else:
                cur = (gmA, gfA)
                for s in range(6):
                    cur = emit_sweep(s * 16, cur[0], cur[1],
                                     (s + 1) * 16 if s < 5 else None)

            EI('ve', sv, lambda e: e.reciprocal(out=dnm[:], in_=dnm[:]))
            db = dnm[:].rearrange("p (h o) -> p h o", o=1).to_broadcast([TL, NH, DH])
            EI('ve', sv, lambda e: e.tensor_tensor(
                out=attnbf[:].rearrange("p (h dd) -> p h dd", h=NH), in0=Av, in1=db, op=ALU.mult))
            W('te', sv)
            for c in range(4):
                transpose_to(attnT[:, c, :], attnbf[:, c * TL:(c + 1) * TL])
            W('te', sv)
            for half in range(2):
                mmgroup(pc0, np.s_[:, half * 512:(half + 1) * 512],
                        [(attnT[:, c, :], wA["wout"][:, c, half * 512:(half + 1) * 512])
                         for c in range(4)], [])
            W('ve', stc)
            EI('ve', sv, lambda e: e.tensor_tensor(out=part_s[:], in0=pc0.t[:, 0:1024],
                                                   in1=x_s[:], op=ALU.add))
            pc0.done(sv)
            W('sy', sv)
            EI('sy', d, (lambda e, b=b, T0=T0: e.dma_start(
                out=part_d[b, T0:T0 + TL, :], in_=part_s[:])), 16)

    def ffn_phase(half):
        f0 = half * (DFF // 2)
        wiT3 = win["wiT"].rearrange("(k p) f -> k p f", p=TL)
        woT3 = win["woT"].rearrange("(k p) dd -> k p dd", p=TL)
        AT3 = win["AT"].rearrange("(k p) c -> k p c", p=TL)
        gwT3 = win["gwT"].rearrange("(k p) c -> k p c", p=TL)
        jobs = []
        for kc in range(8):
            jobs.append((wiT3[kc, :, f0:f0 + 2048], wiT_s[:, kc, :], TL, 2048))
        for kc in range(16):
            jobs.append((woT3[f0 // TL + kc, :, :], woT_s[:, kc, :], TL, 1024))
        for ee in range(NE):
            jobs.append((win["BT"][ee, :, f0:f0 + 2048], BT_s[0:16, ee, :], 16, 2048))
        jobs.append((win["bi"][:, f0:f0 + 2048], bi_s[0:1, 0:2048], 1, 2048))
        if half == 0:
            for kc in range(8):
                jobs.append((AT3[kc], AT_s[:, kc, :], TL, NE * TL))
            for kc in range(8):
                jobs.append((gwT3[kc], gwT_s[:, kc, :], TL, TL))
            jobs.append((win["bo"], bo_s[:], 1, D))
            jobs.append((win["gb"], gbr[:], 1, TL))
        bufs = (stg, stg2)
        ccnt, dcnt = [], []
        for i, (src, dst, p, c) in enumerate(jobs):
            bf = bufs[i % 2]
            if i >= 2:
                W('sy', sv, ccnt[i - 2])
            EI('sy', d, (lambda e, src=src, bf=bf, p=p, c=c: e.dma_start(
                out=bf[0:p, 0:c], in_=src)), 16)
            dcnt.append(d.n)
            W('ve', d, dcnt[i])
            EI('ve', sv, (lambda e, dst=dst, bf=bf, p=p, c=c: e.tensor_copy(
                out=dst, in_=bf[0:p, 0:c])))
            ccnt.append(sv.n)
        W('sy', sv)
        W('te', sv)
        W('sc', sv)
        W('gp', sv)

        for b in range(NB):
            for T in range(NT):
                T0 = T * TL
                EI('sy', d, (lambda e, b=b, T=T: e.dma_start(
                    out=xnT_b[:].rearrange("p c t -> p (c t)"), in_=xnt_d[b, T])), 16)
                EI('sy', d, (lambda e, b=b, T0=T0: e.dma_start(
                    out=partb[:], in_=part_d[b, T0:T0 + TL, :])), 16)
                W('te', d)
                W('ve', d)
                W('sc', sv)
                W('gp', sv)
                ti = b * NT + T
                gts = gates_all[:, ti * NE:(ti + 1) * NE]
                if half == 0:
                    mms = [(xnT_b[:, k, :], gwT_s[:, k, :]) for k in range(8)]
                    mms.append((ones_r[:], gbr[:]))
                    mmgroup(pa0, np.s_[:, 0:TL], mms,
                            [('ve', sv, lambda e: e.tensor_copy(out=probs[:], in_=pa0.t[:, 0:NE]))])
                    EI('ve', sv, lambda e: e.tensor_reduce(out=c1[:], in_=probs[:], axis=AX.X,
                                                           op=ALU.max, negate=True))
                    W('ve', sv)
                    EI('ve', sv, lambda e: e.tensor_scalar(out=probs[:], in0=probs[:], scalar1=c1[:],
                                                           scalar2=None, op0=ALU.add))
                    Wsp('sc', sv)
                    EI('sc', sc, lambda e: e.activation(out=probs[:], in_=probs[:], func=AF.Exp))
                    Wsp('ve', sc)
                    EI('ve', sv, lambda e: e.tensor_reduce(out=c1[:], in_=probs[:], axis=AX.X, op=ALU.add))
                    W('ve', sv)
                    EI('ve', sv, lambda e: e.reciprocal(out=c1[:], in_=c1[:]))
                    W('ve', sv)
                    EI('ve', sv, lambda e: e.tensor_scalar(out=probs[:], in0=probs[:], scalar1=c1[:],
                                                           scalar2=None, op0=ALU.mult))
                    EI('ve', sv, lambda e: e.tensor_reduce(out=c1[:], in_=probs[:], axis=AX.X, op=ALU.max))
                    W('ve', sv)
                    EI('ve', sv, lambda e: e.tensor_scalar(out=dcol[:], in0=probs[:], scalar1=c1[:],
                                                           scalar2=None, op0=ALU.is_equal))
                    EI('ve', sv, lambda e: e.scalar_tensor_tensor(out=dcol[:], in0=dcol[:], scalar=-2.0,
                                                                  in1=probs[:], op0=ALU.mult, op1=ALU.add))
                    EI('ve', sv, lambda e: e.tensor_reduce(out=c2[:], in_=dcol[:], axis=AX.X, op=ALU.max))
                    W('ve', sv)
                    EI('ve', sv, lambda e: e.tensor_tensor(out=c3[:], in0=c1[:], in1=c2[:], op=ALU.add))
                    W('ve', sv)
                    EI('ve', sv, lambda e, gts=gts: e.tensor_scalar(out=gts, in0=probs[:], scalar1=c2[:],
                                                                    scalar2=c3[:], op0=ALU.is_ge, op1=ALU.mult))
                    EI('ve', sv, lambda e, ti=ti: e.tensor_scalar(out=c3_all[:, ti:ti + 1], in0=c3[:], scalar1=2.0,
                                                                  scalar2=None, op0=ALU.mult))
                else:
                    EI('ve', sv, lambda e: e.memset(gpk[:], 0.0))
                    W('ve', sv)
                    EI('ve', sv, lambda e, ti=ti: e.tensor_copy(out=gpk[:, 0:1], in_=c3_all[:, ti:ti + 1]))
                    W('te', sv)
                    ptr0.guard()
                    EI('te', stc, lambda e: e.transpose(ptr0.t[0:16, :], gpk[:], ident[:]))
                    W('ve', stc)
                    EI('ve', sv, lambda e: e.tensor_copy(out=gsr[:], in_=ptr0.t[0:1, 0:TL]))
                    ptr0.done(sv)
                for ee in range(NE):
                    mmgroup(pa1, np.s_[:, 0:TL],
                            [(AT_s[:, k, ee * TL:(ee + 1) * TL], xnT_b[:, k, :])
                             for k in range(8)],
                            [('ve', sv, (lambda e, ee=ee: e.tensor_copy(
                                out=la_s[0:16, ee * TL:(ee + 1) * TL],
                                in_=pa1.t[0:16, 0:TL])))])
                for grp in range(2):
                    n0 = grp * 1024
                    pb = (pc0, pc1)[grp]
                    for nn in range(2):
                        sub = [(xnT_b[:, k, :],
                                wiT_s[:, k, n0 + nn * 512:n0 + nn * 512 + 512])
                               for k in range(8)]
                        sub.append((ones_r[:],
                                    bi_s[0:1, n0 + nn * 512:n0 + nn * 512 + 512]))
                        ev_ = [] if nn == 0 else [
                            ('ve', sv, (lambda e, grp=grp, pb=pb: e.tensor_copy(
                                out=base_s[:, grp * 1024:(grp + 1) * 1024],
                                in_=pb.t[:, 0:1024])))]
                        mmgroup(pb, np.s_[:, nn * 512:(nn + 1) * 512], sub, ev_)
                    pb.done(sv)
                W('gp', sv)
                EI('gp', sg, lambda e: e.memset(Hacc[:], 0.0))
                hcnt = []
                lbufs = (lorab, sadd)
                i = 0
                for ee in range(NE):
                    for grp in range(2):
                        pb = (pc0, pc1)[grp]
                        lb = lbufs[i % 2]
                        if i >= 2:
                            W('ve', sg, hcnt[i - 2])
                        for nn in range(2):
                            n0 = grp * 1024 + nn * 512
                            ev_ = [] if nn == 0 else [
                                ('ve', sv, (lambda e, pb=pb, ee=ee, gts=gts, lb=lb: e.tensor_scalar(
                                    out=lb[:], in0=pb.t[:, 0:1024], scalar1=0.0,
                                    scalar2=gts[:, ee:ee + 1], op0=ALU.max, op1=ALU.mult)))]
                            mmgroup(pb, np.s_[:, nn * 512:(nn + 1) * 512],
                                    [(la_s[0:16, ee * TL:(ee + 1) * TL],
                                      BT_s[0:16, ee, n0:n0 + 512]),
                                     (ident[:], base_s[:, n0:n0 + 512])],
                                    ev_)
                        pb.done(sv)
                        W('gp', sv)
                        EI('gp', sg, (lambda e, grp=grp, lb=lb: e.tensor_tensor(
                            out=Hacc[:, grp * 1024:(grp + 1) * 1024],
                            in0=Hacc[:, grp * 1024:(grp + 1) * 1024],
                            in1=lb[:], op=ALU.add)))
                        hcnt.append(sg.n)
                        i += 1
                W('ve', sg)
                EI('ve', sv, lambda e: e.memset(spc[:, 0:8], 0.0))
                EI('ve', sv, lambda e: e.tensor_copy(out=Hbf[:], in_=Hacc[:]))
                W('te', sv)
                for c in range(16):
                    transpose_to(HT[:, c, :], Hbf[:, c * TL:(c + 1) * TL])
                W('te', sv)
                for nn in range(2):
                    sub = [(HT[:, c, :], woT_s[:, c, nn * 512:(nn + 1) * 512])
                           for c in range(16)]
                    if half == 1:
                        sub.append((gsr[:], bo_s[0:1, nn * 512:(nn + 1) * 512]))
                    mmgroup(pc0, np.s_[:, nn * 512:(nn + 1) * 512], sub, [])
                W('ve', stc)
                EI('ve', sv, lambda e: e.tensor_tensor(out=outb[:], in0=pc0.t[:, 0:1024],
                                                       in1=partb[:], op=ALU.add))
                pc0.done(sv)
                W('sy', sv)
                dst = part_d if half == 0 else out_d
                EI('sy', d, (lambda e, b=b, T0=T0, dst=dst: e.dma_start(
                    out=dst[b, T0:T0 + TL, :], in_=outb[:])), 16)

    for b in range(NB):
        keys_phase(b)
        attn_phase(b)
    ffn_phase(0)
    ffn_phase(1)
    W('sy', d)
    W('sy', g)

    with nc.Block() as block:
        def mk(ek):
            def run(eng):
                for k, fn in prog:
                    if k == ek:
                        fn(eng)
            return run
        block.sync(mk('sy'))
        block.vector(mk('ve'))
        block.scalar(mk('sc'))
        block.tensor(mk('te'))
        block.gpsimd(mk('gp'))

    stack.close()
    return nc


def kernel(**inputs):
    rd = np.asarray(inputs["L0_attn_raw_delta"], dtype=np.float32)
    assert rd.shape == (16,)
    CT, ST, CT2, ST2, CK, SK = _host_consts()
    _MC = _mask_consts()
    x = np.ascontiguousarray(np.asarray(inputs["L0_attn_x"], dtype=np.float32))
    f32 = lambda a: np.ascontiguousarray(np.asarray(a, dtype=np.float32))
    base_map = {
        "wdkv": f32(inputs["L0_attn_W_dkv"]), "wuk": f32(inputs["L0_attn_W_uk"]),
        "wuv": f32(inputs["L0_attn_W_uv"]), "wdq": f32(inputs["L0_attn_W_dq"]),
        "wuq": f32(inputs["L0_attn_W_uq"]), "wqr": f32(inputs["L0_attn_W_qr"]),
        "wkr": f32(inputs["L0_attn_W_kr"]), "wout": f32(inputs["L0_attn_W_out"]),
        "iwq": f32(inputs["L0_attn_idx_wq"]), "iwk": f32(inputs["L0_attn_idx_wk"]),
        "iw": np.tile(np.asarray(inputs["L0_attn_idx_w"], dtype=np.float32)[None, :], (TL, 1)),
        "wiT": f32(np.asarray(inputs["L0_ffn_wi"], dtype=np.float32).T),
        "woT": f32(np.asarray(inputs["L0_ffn_wo"], dtype=np.float32).T),
        "gwT": f32(np.pad(np.asarray(inputs["L0_ffn_gate_w"], dtype=np.float32).T,
                      ((0, 0), (0, TL - NE)))),
        "AT": f32(np.pad(np.transpose(np.asarray(inputs["L0_ffn_lora_A"], dtype=np.float32),
                                  (2, 0, 1)), ((0, 0), (0, 0), (0, TL - RLO))).reshape(D, NE * TL)),
        "BT": f32(np.transpose(np.asarray(inputs["L0_ffn_lora_B"], dtype=np.float32), (0, 2, 1))),
        "bi": f32(np.asarray(inputs["L0_ffn_bi"], dtype=np.float32).reshape(1, DFF)),
        "bo": f32(np.asarray(inputs["L0_ffn_bo"], dtype=np.float32).reshape(1, D)),
        "gb": f32(np.pad(np.asarray(inputs["L0_ffn_gate_b"], dtype=np.float32).reshape(1, NE),
                     ((0, 0), (0, TL - NE)))),
        "CT": CT, "ST": ST, "CT2": CT2, "ST2": ST2, "CK": CK, "SK": SK,
        "K128": _MC[0], "NEGM": _MC[1], "K0": _MC[2], "SURR0": _MC[3],
        "BANDM": _MC[4], "identw": _MC[5],
    }
    nc = build_nc()
    in_maps = []
    for c in range(8):
        m = dict(base_map)
        m["x"] = np.ascontiguousarray(x[2 * c:2 * c + 2])
        in_maps.append(m)
    res = run_bass_kernel_spmd(nc, in_maps, core_ids=list(range(8)))
    out = np.concatenate([np.asarray(r["out"]) for r in res.results], axis=0)
    return out.astype(np.float32)


# version 13 (final)
# speedup vs baseline: 1.1037x; 1.0143x over previous
import contextlib
import numpy as np
import ml_dtypes
import concourse.bass as bass
import concourse.mybir as mybir
from concourse.bass_utils import run_bass_kernel_spmd
from concourse.bass_core_rust import IndirectOffsetOnAxis

F32 = mybir.dt.float32
BF16 = mybir.dt.bfloat16
I32 = mybir.dt.int32
U32 = mybir.dt.uint32
AF = mybir.ActivationFunctionType
ALU = mybir.AluOpType
AX = mybir.AxisListType
bfdt = ml_dtypes.bfloat16

L, D, NB, NT, TL = 2048, 1024, 2, 16, 128
NH, DH, DR = 8, 64, 32
NIH = 4
DFF, NE, RLO = 4096, 8, 16
ROW = 512 + 512 + 32
GSL = 16
NSW = 6
SCALE = (DH + DR) ** -0.5
NEG = -3.0e38
EPS = 1e-5
_PB = 196608


def _host_consts():
    th = 10000.0 ** (-np.arange(16, dtype=np.float64) / 16.0)

    def pack(ang):
        c = np.cos(ang).astype(np.float32)
        s = np.sin(ang).astype(np.float32)
        return (c.reshape(NT, TL, 16).transpose(1, 0, 2).reshape(TL, NT * 16).copy(),
                s.reshape(NT, TL, 16).transpose(1, 0, 2).reshape(TL, NT * 16).copy())
    CT, ST = pack(np.arange(L, dtype=np.float64)[:, None] * th[None, :])
    CT2, ST2 = pack((2.0 * np.arange(L, dtype=np.float64)[:, None] - 63.0) * th[None, :])
    ang_k = np.arange(96, dtype=np.float64)[:, None] * th[None, :]
    CK = np.repeat(np.cos(ang_k).astype(np.float32).reshape(1, 96 * 16), TL, axis=0).copy()
    SK = np.repeat(np.sin(ang_k).astype(np.float32).reshape(1, 96 * 16), TL, axis=0).copy()
    return CT, ST, CT2, ST2, CK, SK



def _mask_consts():
    p = np.arange(TL)[:, None]
    s = np.arange(TL)[None, :]
    K128 = (p - s >= 64).astype(np.float32)
    u = np.arange(TL)[None, :]
    K128 = (u <= p).astype(np.float32)
    NEGM = np.where(u <= p, 0.0, NEG).astype(np.float32)
    d = (p - s).astype(np.float64)
    K0 = (d >= 64).astype(np.float32)
    SURR = np.where((d >= 0) & (d <= 63), 1e30 + d * 1e24,
                    np.where(d < 0, -1e30 + d * 1e24, 0.0)).astype(np.float32)
    u2 = np.arange(256)[None, :]
    BANDM = np.where((u2 >= p + 65) & (u2 <= p + 128), 0.0, NEG).astype(np.float32)
    BANDM4 = np.tile(BANDM, (1, 4)).copy()
    IDENT = np.eye(TL, dtype=np.float32).astype(bfdt)
    return K128, NEGM, K0, SURR, BANDM4, IDENT


def build_nc(dbg=False):
    nc = bass.Bass("TRN2", target_bir_lowering=False)

    x_d = nc.dram_tensor("x", [NB, L, D], F32, kind="ExternalInput").ap()
    out_d = nc.dram_tensor("out", [NB, L, D], F32, kind="ExternalOutput").ap()
    win = {}
    for name, shape in [
        ("wdkv", [D, 256]), ("wuk", [256, 512]), ("wuv", [256, 512]),
        ("wdq", [D, 512]), ("wuq", [512, 512]), ("wqr", [512, 256]),
        ("wkr", [D, DR]), ("wout", [512, D]), ("iwq", [D, 256]), ("iwk", [D, 64]),
        ("iw", [TL, NIH]),
        ("wiT", [D, DFF]), ("woT", [DFF, D]), ("gwT", [D, TL]),
        ("AT", [D, NE * 32]), ("BT", [NE, RLO, DFF]),
        ("bi", [1, DFF]), ("bo", [1, D]), ("gb", [1, TL]),
        ("CT", [TL, NT * 16]), ("ST", [TL, NT * 16]),
        ("CT2", [TL, NT * 16]), ("ST2", [TL, NT * 16]),
        ("CK", [TL, 96 * 16]), ("SK", [TL, 96 * 16]),
        ("K128", [TL, TL]), ("NEGM", [TL, TL]),
        ("K0", [TL, TL]), ("SURR0", [TL, TL]), ("BANDM", [TL, 1024]),
    ]:
        win[name] = nc.dram_tensor(name, shape, F32, kind="ExternalInput").ap()
    ident_d = nc.dram_tensor("identw", [TL, TL], BF16, kind="ExternalInput").ap()
    kvtab_d = nc.dram_tensor("kvtab", [L, ROW], BF16, kind="Internal").ap()
    xnt_d = nc.dram_tensor("xnt", [NB, NT, TL, 8 * TL], BF16, kind="Internal").ap()
    part_d = nc.dram_tensor("part", [NB, L, D], F32, kind="Internal").ap()

    cur = [16448]

    def alloc(name, shape, dt):
        per = int(np.prod(shape[1:])) * mybir.dt.size(dt)
        per = (per + 31) // 32 * 32
        off = cur[0]
        cur[0] = off + per
        assert cur[0] <= _PB - 64, f"sbuf overflow at {name}"
        return nc.alloc_sbuf_tensor_at(name, shape, dt, offset=off)

    ident = alloc("ident", [TL, TL], BF16)
    ones_r = alloc("ones_r", [1, TL], BF16)
    cstc = alloc("cstc", [TL, 8], F32)
    spc = alloc("spc", [TL, 8], F32)
    spc2 = alloc("spc2", [TL, 8], F32)
    spc3 = alloc("spc3", [TL, 8], F32)

    a0 = cur[0]
    CT_s = alloc("CT_s", [TL, NT * 16], F32)
    ST_s = alloc("ST_s", [TL, NT * 16], F32)
    CT2_s = alloc("CT2_s", [TL, NT * 16], F32)
    ST2_s = alloc("ST2_s", [TL, NT * 16], F32)
    CK_s = alloc("CK_s", [TL, 96 * 16], BF16)
    SK_s = alloc("SK_s", [TL, 96 * 16], BF16)
    BANDM_s = alloc("BANDM_s", [TL, 1024], F32)
    iw_s = alloc("iw_s", [TL, NIH], F32)
    K128s = alloc("K128s", [TL, TL], F32)
    NEGMs = alloc("NEGMs", [TL, TL], F32)
    K0s = alloc("K0s", [TL, TL], F32)
    SURR0s = alloc("SURR0s", [TL, TL], F32)
    wA = {}
    for name, shape in [
        ("wdkv", [TL, 8, 256]), ("wuk", [TL, 2, 512]), ("wuv", [TL, 2, 512]),
        ("wdq", [TL, 8, 512]), ("wuq", [TL, 4, 512]), ("wqr", [TL, 4, 256]),
        ("wkr", [TL, 8, DR]), ("wout", [TL, 4, D]), ("iwq", [TL, 8, 256]),
        ("iwk", [TL, 8, 64]),
    ]:
        wA[name] = alloc("A_" + name, shape, BF16)
    kiT = alloc("kiT", [64, L], BF16)
    G = alloc("G", [TL, GSL, ROW], BF16)
    prods = alloc("prods", [TL, NH * GSL * DH], BF16)
    I_a = alloc("I_a", [TL, L], F32)
    I_b = alloc("I_b", [TL, L], F32)
    bscore = I_b
    kcb = alloc("kcb", [TL, 2, 512], BF16)
    VSb = alloc("VSb", [TL, 2, 512], BF16)
    kmub = alloc("kmub", [TL, 2, DR], BF16)
    rotb = alloc("rotb", [TL, TL], BF16)
    kcTb = alloc("kcTb", [TL, 4, 256], BF16)
    krTt = alloc("krTt", [96, 2, TL], BF16)
    qcT = alloc("qcT", [TL, 4, TL], BF16)
    qropeT = alloc("qropeT", [96, NH, TL], BF16)
    qropeb = alloc("qropeb", [TL, NH, TL], BF16)
    battn = alloc("battn", [TL, L], BF16)
    batT = alloc("batT", [TL, 2, TL], BF16)
    x_s = alloc("x_s", [TL, D], F32)
    xn_s = I_a[:, 0:D]
    xnbf = alloc("xnbf", [TL, D], BF16)
    xnT = alloc("xnT", [TL, 8, TL], BF16)
    tab = alloc("tab", [TL, ROW], BF16)
    ckvT = alloc("ckvT", [TL, 2, TL], BF16)
    cqT = alloc("cqT", [TL, 4, TL], BF16)
    qiT = alloc("qiT", [64, NIH, TL], BF16)
    qc = alloc("qc", [TL, 512], BF16)
    qr = alloc("qr", [TL, 256], F32)
    mu = alloc("mu", [TL, 256], F32)
    qt1 = alloc("qt1", [TL, 128], BF16)
    qt2 = alloc("qt2", [TL, 128], BF16)
    kt1 = alloc("kt1", [TL, GSL * 16], BF16)
    kt2 = alloc("kt2", [TL, GSL * 16], BF16)
    ktt = alloc("ktt", [TL, GSL * 16], BF16)
    rp = battn
    rp2 = alloc("rp2", [TL, NH * GSL * 16], BF16)
    Ssc = alloc("Ssc", [TL, NH * GSL], F32)
    Ss2 = alloc("Ss2", [TL, NH * GSL], F32)
    ebuf = alloc("ebuf", [TL, NH * GSL], BF16)
    mrun = alloc("mrun", [TL, NH], F32)
    mnew = alloc("mnew", [TL, NH], F32)
    alph = alloc("alph", [TL, NH], F32)
    dnm = alloc("dnm", [TL, NH], F32)
    rpart = alloc("rpart", [TL, NH], F32)
    Aacc = alloc("Aacc", [TL, 512], F32)
    avp = alloc("avp", [TL, 512], F32)
    attnbf = alloc("attnbf", [TL, 512], BF16)
    attnT = alloc("attnT", [TL, 4, TL], BF16)
    part_s = alloc("part_s", [TL, D], F32)
    idx96 = alloc("idx96", [TL, 96], U32)
    v8 = alloc("v8", [TL, 8], F32)
    col1 = alloc("col1", [TL, 1], F32)
    col2 = alloc("col2", [TL, 1], F32)
    w1t = alloc("w1t", [TL, 512], F32)
    assert cur[0] <= _PB - 64

    cur[0] = a0
    wiT_s = alloc("wiT_s", [TL, 8, DFF // 2], BF16)
    woT_s = alloc("woT_s", [TL, 16, D], BF16)
    BT_s = alloc("BT_s", [48, NE // 2, DFF // 2], BF16)
    AT_s = alloc("AT_s", [TL, 8, NE * 32], BF16)
    gwT_s = alloc("gwT_s", [TL, 8, TL], BF16)
    bi_s = alloc("bi_s", [1, DFF // 2], BF16)
    bo_s = alloc("bo_s", [1, D], BF16)
    gbr = alloc("gbr", [1, TL], BF16)
    Hacc = alloc("Hacc", [TL, DFF // 2], F32)
    stg = Hacc
    base_s = alloc("base_s", [TL, DFF // 2], BF16)
    lorab = alloc("lorab", [TL, 1024], BF16)
    sadd = alloc("sadd", [TL, 1024], BF16)
    Hbf = alloc("Hbf", [TL, DFF // 2], BF16)
    HT = alloc("HT", [TL, 16, TL], BF16)
    xnT_b = alloc("xnT_b", [TL, 8, TL], BF16)
    partb = alloc("partb", [TL, D], F32)
    outb = alloc("outb", [TL, D], F32)
    la_s = alloc("la_s", [48, NE // 2, TL], BF16)
    probs = alloc("probs", [TL, NE], F32)
    gates = alloc("gates", [TL, NE], F32)
    dcol = alloc("dcol", [TL, NE], F32)
    c1 = alloc("c1", [TL, 1], F32)
    c2 = alloc("c2", [TL, 1], F32)
    c3 = alloc("c3", [TL, 1], F32)
    gpk = alloc("gpk", [TL, 16], BF16)
    gsr = alloc("gsr", [1, TL], BF16)
    gates_all = alloc("gates_all", [TL, NB * NT * NE], F32)
    c3_all = alloc("c3_all", [TL, NB * NT], F32)
    stg2 = alloc("stg2", [TL, DFF // 2], F32)
    assert cur[0] <= _PB - 64

    prog = []

    class Ctr:
        __slots__ = ("sem", "n")

        def __init__(self, sem):
            self.sem, self.n = sem, 0

    def E(ek, fn):
        prog.append((ek, fn))

    def EI(ek, ctr, fn, k=1):
        selfw = ek in ("ve", "sc", "gp")
        n_after = ctr.n + k

        def go(eng, fn=fn, s=ctr.sem, k=k, selfw=selfw, n_after=n_after):
            fn(eng).then_inc(s, k)
            if selfw:
                eng.wait_ge(s, n_after)
        prog.append((ek, go))
        ctr.n += k

    def W(ek, ctr, val=None):
        v = ctr.n if val is None else val
        prog.append((ek, lambda eng, s=ctr.sem, v=v: eng.wait_ge(s, v)))

    def Wsp(ek, ctr):
        W(ek, ctr)
        if ek == 've':
            EI('ve', sv, lambda e: e.memset(spc[:, 0:8], 0.0))
        elif ek == 'sc':
            EI('sc', sc, lambda e: e.memzero(spc2[:, 0:8]))

    stack = contextlib.ExitStack()
    d = Ctr(stack.enter_context(nc.semaphore("d")))
    g = Ctr(stack.enter_context(nc.semaphore("g")))
    sv = Ctr(stack.enter_context(nc.semaphore("sv")))
    sc = Ctr(stack.enter_context(nc.semaphore("sc")))
    stc = Ctr(stack.enter_context(nc.semaphore("stc")))
    sg = Ctr(stack.enter_context(nc.semaphore("sg")))

    class Bank:
        def __init__(self, name, shape, dt):
            self.t = nc.alloc_psum_tensor(name, shape, dt)
            self.rel = None

        def guard(self):
            if self.rel is not None:
                W('te', self.rel[0], self.rel[1])

        def done(self, ctr):
            self.rel = (ctr, ctr.n)

    pa0 = Bank("pa0", [TL, 512], F32)
    pa1 = Bank("pa1", [TL, 512], F32)
    pc0 = Bank("pc0", [TL, 1024], F32)
    pc1 = Bank("pc1", [TL, 1024], F32)
    ptr0 = Bank("ptr0", [TL, TL], BF16)
    ptr1 = Bank("ptr1", [TL, TL], BF16)

    def mmgroup(bank, sl, mms, evacs):
        bank.guard()
        n = len(mms)
        for i, (lh, rh) in enumerate(mms):
            fn = (lambda e, lh=lh, rh=rh, s=(i == 0), p=(i == n - 1), b=bank, sl=sl:
                  e.matmul(b.t[sl], lh, rh, start=s, stop=p))
            if i == n - 1:
                EI('te', stc, fn)
            else:
                E('te', fn)
        last = None
        for ek, ctr, fn in evacs:
            W(ek, stc)
            EI(ek, ctr, fn)
            last = ctr
        if last is not None:
            bank.done(last)

    tr_i = [0]

    def transpose_to(dst, src, r0=0, r1=TL):
        bank = (ptr0, ptr1)[tr_i[0] % 2]
        tr_i[0] += 1
        bank.guard()
        EI('te', stc, lambda e, b=bank, s=src: e.transpose(b.t[:], s, ident[:]))
        W('ve', stc)
        EI('ve', sv, lambda e, b=bank, dd=dst, r0=r0, r1=r1:
           e.tensor_copy(out=dd, in_=b.t[r0:r1, :]))
        bank.done(sv)

    EI('ve', sv, lambda e: e.memset(ones_r[:], 1.0))
    _consts = [EPS, 1.0 / D, -1.0, 1e24, 1e30]
    for ci, cval in enumerate(_consts):
        EI('ve', sv, lambda e, ci=ci, cval=cval: e.memset(cstc[:, ci:ci + 1], cval))
        nc.const_aps.register(F32, cval, cstc[:, ci:ci + 1])
    EI('sy', d, lambda e: e.dma_start(out=ident[:], in_=ident_d), 16)
    for nm, t in [("CT", CT_s), ("ST", ST_s), ("CT2", CT2_s), ("ST2", ST2_s),
                  ("iw", iw_s), ("BANDM", BANDM_s),
                  ("K128", K128s), ("NEGM", NEGMs), ("K0", K0s), ("SURR0", SURR0s)]:
        EI('sy', d, lambda e, a=win[nm], t=t: e.dma_start(out=t[:], in_=a), 16)
    for nm, t in [("CK", CK_s), ("SK", SK_s)]:
        EI('sy', d, lambda e, a=win[nm]: e.dma_start(out=I_a[:, 0:1536], in_=a), 16)
        W('ve', d)
        EI('ve', sv, lambda e, t=t: e.tensor_copy(out=t[:], in_=I_a[:, 0:1536]))
        W('sy', sv)

    for nm in ["wdkv", "wuk", "wuv", "wdq", "wuq", "wqr", "wkr", "wout", "iwq", "iwk"]:
        K, C = wA[nm].shape[1], wA[nm].shape[2]
        src3 = win[nm].rearrange("(k p) c -> k p c", p=TL)
        for k in range(K):
            EI('sy', d, lambda e, s=src3, k=k, C=C:
               e.dma_start(out=I_a[:, 0:C], in_=s[k]), 16)
            W('ve', d)
            EI('ve', sv, lambda e, nm=nm, k=k, C=C:
               e.tensor_copy(out=wA[nm][:, k, :], in_=I_a[:, 0:C]))
            W('sy', sv)

    def emit_ln():
        EI('ve', sv, lambda e: e.tensor_reduce(out=col1[:], in_=x_s[:], axis=AX.X, op=ALU.add))
        W('ve', sv)
        EI('ve', sv, lambda e: e.tensor_scalar(out=col1[:], in0=col1[:], scalar1=1.0 / D,
                                               scalar2=None, op0=ALU.mult))
        W('ve', sv)
        EI('ve', sv, lambda e: e.tensor_scalar(out=xn_s[:], in0=x_s[:], scalar1=col1[:],
                                               scalar2=None, op0=ALU.subtract))
        Wsp('sc', sv)
        EI('sc', sc, lambda e: e.activation(out=I_b[:, 0:D], in_=xn_s[:], func=AF.Square))
        Wsp('ve', sc)
        EI('ve', sv, lambda e: e.tensor_reduce(out=col2[:], in_=I_b[:, 0:D], axis=AX.X, op=ALU.add))
        Wsp('sc', sv)
        EI('sc', sc, lambda e: e.activation(out=col2[:], in_=col2[:], func=AF.Sqrt,
                                            scale=1.0 / D, bias=EPS))
        Wsp('ve', sc)
        EI('ve', sv, lambda e: e.reciprocal(out=col2[:], in_=col2[:]))
        W('ve', sv)
        EI('ve', sv, lambda e: e.tensor_scalar(out=xn_s[:], in0=xn_s[:], scalar1=col2[:],
                                               scalar2=None, op0=ALU.mult))
        EI('ve', sv, lambda e: e.tensor_copy(out=xnbf[:], in_=xn_s[:]))
        W('te', sv)
        for c in range(8):
            transpose_to(xnT[:, c, :], xnbf[:, c * TL:(c + 1) * TL])

    def keys_phase(b):
        for T in range(NT):
            T0 = T * TL
            W('sy', g)
            EI('sy', d, lambda e, b=b, T0=T0: e.dma_start(
                out=x_s[:], in_=x_d[b, T0:T0 + TL, :]), 16)
            W('ve', d)
            W('sc', sv)
            emit_ln()
            W('sy', sv)
            EI('sy', d, lambda e, b=b, T=T: e.dma_start(
                out=xnt_d[b, T], in_=xnT[:].rearrange("p c t -> p (c t)")), 16)
            for cc in range(2):
                mmgroup((pa0, pa1)[cc], np.s_[:, 0:TL],
                        [(wA["wdkv"][:, k, cc * TL:(cc + 1) * TL], xnT[:, k, :])
                         for k in range(8)],
                        [('ve', sv, (lambda e, cc=cc, bk=(pa0, pa1)[cc]: e.tensor_copy(
                            out=ckvT[:, cc, :], in_=bk.t[:, 0:TL])))])
            W('te', sv)
            mmgroup(pc0, np.s_[:, 0:512],
                    [(ckvT[:, c, :], wA["wuk"][:, c, :]) for c in range(2)],
                    [('ve', sv, lambda e: e.tensor_copy(out=tab[:, 0:512],
                                                        in_=pc0.t[:, 0:512]))])
            mmgroup(pc1, np.s_[:, 0:512],
                    [(ckvT[:, c, :], wA["wuv"][:, c, :]) for c in range(2)],
                    [('ve', sv, lambda e: e.tensor_copy(out=tab[:, 512:1024],
                                                        in_=pc1.t[:, 0:512]))])
            mmgroup(pa0, np.s_[:, 0:DR],
                    [(xnT[:, k, :], wA["wkr"][:, k, :]) for k in range(8)],
                    [('ve', sv, lambda e: e.tensor_copy(out=w1t[:, 0:DR],
                                                        in_=pa0.t[:, 0:DR]))])
            Wsp('sc', sv)
            EI('sc', sc, lambda e: e.activation(out=w1t[:, 64:64 + DR], in_=w1t[:, 0:DR],
                                                func=AF.Abs))
            EI('sc', sc, lambda e: e.activation(out=w1t[:, 64:64 + DR], in_=w1t[:, 64:64 + DR],
                                                func=AF.Exp, scale=-1.0))
            EI('sc', sc, lambda e: e.activation(out=w1t[:, 64:64 + DR], in_=w1t[:, 64:64 + DR],
                                                func=AF.Ln, bias=1.0))
            EI('sc', sc, lambda e: e.activation(out=w1t[:, 128:128 + DR], in_=w1t[:, 0:DR],
                                                func=AF.Relu))
            Wsp('ve', sc)
            EI('ve', sv, lambda e: e.tensor_tensor(out=tab[:, 1024:1056],
                                                   in0=w1t[:, 64:64 + DR],
                                                   in1=w1t[:, 128:128 + DR], op=ALU.add))
            mmgroup(pa1, np.s_[0:64, 0:TL],
                    [(wA["iwk"][:, k, :], xnT[:, k, :]) for k in range(8)],
                    [('ve', sv, (lambda e, T0=T0: e.tensor_copy(
                        out=kiT[0:64, T0:T0 + TL], in_=pa1.t[0:64, 0:TL])))])
            W('sy', sv)
            EI('sy', d, lambda e, T0=T0: e.dma_start(
                out=kvtab_d[T0:T0 + TL, :], in_=tab[:]), 16)

    def emit_gathers(k0, j0=0, j1=GSL):
        W('gp', sv)
        W('gp', d)
        for j in range(j0, j1):
            EI('gp', g, (lambda e, j=j, k0=k0: e.indirect_dma_start(
                out=G[:, j, :], out_offset=None, in_=kvtab_d,
                in_offset=IndirectOffsetOnAxis(idx96[:, k0 + j:k0 + j + 1], 0))), 16)
        return g.n

    def attn_phase(b):
        for T in range(NT):
            T0 = T * TL
            EI('sy', d, lambda e, b=b, T=T: e.dma_start(
                out=xnT[:].rearrange("p c t -> p (c t)"), in_=xnt_d[b, T]), 16)
            EI('sy', d, lambda e, b=b, T0=T0: e.dma_start(
                out=x_s[:], in_=x_d[b, T0:T0 + TL, :]), 16)
            if T > 0:
                W('sy', stc)
                EI('sy', d, (lambda e, T0=T0: e.dma_start(
                    out=kcb[:],
                    in_=kvtab_d[T0 - TL:T0 + TL, 0:512].rearrange(
                        "(j p) c -> p j c", p=TL))), 16)
                EI('sy', d, (lambda e, T0=T0: e.dma_start(
                    out=VSb[:],
                    in_=kvtab_d[T0 - TL:T0 + TL, 512:1024].rearrange(
                        "(j p) c -> p j c", p=TL))), 16)
                EI('sy', d, (lambda e, T0=T0: e.dma_start(
                    out=kmub[:],
                    in_=kvtab_d[T0 - TL:T0 + TL, 1024:1056].rearrange(
                        "(j p) c -> p j c", p=TL))), 16)
            W('te', d)
            W('ve', d)
            W('ve', sg)
            W('sc', sv)
            W('gp', sv)
            for h in range(NIH):
                mmgroup((pa0, pa1)[h % 2], np.s_[0:64, 0:TL],
                        [(wA["iwq"][:, k, h * 64:(h + 1) * 64], xnT[:, k, :])
                         for k in range(8)],
                        [('ve', sv, (lambda e, h=h, bk=(pa0, pa1)[h % 2]: e.tensor_copy(
                            out=qiT[0:64, h, :], in_=bk.t[0:64, 0:TL])))])
            W('te', sv)
            Wt = T0 + 64 if T > 0 else TL
            nch = (Wt + 511) // 512
            for j in range(nch):
                w = min(512, Wt - j * 512)
                for h in range(NIH):
                    bk = (pa0, pa1)[(j * NIH + h) % 2]
                    lh = qiT[0:64, h, :]
                    rh = kiT[0:64, j * 512:j * 512 + w]
                    if h == 0:
                        mmgroup(bk, np.s_[:, 0:w], [(lh, rh)],
                                [('ve', sv, (lambda e, bk=bk, w=w, j=j: e.tensor_scalar(
                                    out=I_a[:, j * 512:j * 512 + w], in0=bk.t[:, 0:w],
                                    scalar1=0.0, scalar2=iw_s[:, 0:1],
                                    op0=ALU.max, op1=ALU.mult)))])
                    else:
                        mmgroup(bk, np.s_[:, 0:w], [(lh, rh)],
                                [('ve', sv, (lambda e, bk=bk, w=w, h=h: e.tensor_scalar(
                                    out=I_b[:, 0:w], in0=bk.t[:, 0:w], scalar1=0.0,
                                    scalar2=iw_s[:, h:h + 1], op0=ALU.max, op1=ALU.mult)))])
                        EI('ve', sv, (lambda e, w=w, j=j: e.tensor_tensor(
                            out=I_a[:, j * 512:j * 512 + w], in0=I_a[:, j * 512:j * 512 + w],
                            in1=I_b[:, 0:w], op=ALU.add)))
            if T > 0:
                EI('ve', sv, (lambda e, T0=T0: e.tensor_tensor(
                    out=I_a[:, T0 - 64:T0 + 64], in0=I_a[:, T0 - 64:T0 + 64],
                    in1=K128s[:], op=ALU.mult)))
                EI('ve', sv, (lambda e, T0=T0: e.tensor_tensor(
                    out=I_a[:, T0 - 64:T0 + 64], in0=I_a[:, T0 - 64:T0 + 64],
                    in1=NEGMs[:], op=ALU.add)))
                nr, off = 4, 64
            else:
                EI('ve', sv, lambda e: e.tensor_tensor(out=I_a[:, 0:TL], in0=I_a[:, 0:TL],
                                                       in1=K0s[:], op=ALU.mult))
                EI('ve', sv, lambda e: e.tensor_tensor(out=I_a[:, 0:TL], in0=I_a[:, 0:TL],
                                                       in1=SURR0s[:], op=ALU.add))
                nr, off = 12, 0
            gmA = gfA = None
            for r in range(nr):
                cur_t = (I_a, I_b)[r % 2]
                oth_t = (I_b, I_a)[r % 2]
                EI('ve', sv, (lambda e, c=cur_t, Wt=Wt: e.max(out=v8[:], in_=c[:, 0:Wt])))
                W('ve', sv)
                EI('ve', sv, (lambda e, c=cur_t, Wt=Wt, r=r, off=off: e.max_index(
                    out=idx96[:, off + r * 8:off + r * 8 + 8],
                    in_max=v8[:], in_values=c[:, 0:Wt])))
                if r < nr - 1:
                    EI('ve', sv, (lambda e, c=cur_t, o=oth_t, Wt=Wt: e.match_replace(
                        out=o[:, 0:Wt], in_to_replace=v8[:], in_values=c[:, 0:Wt],
                        imm_value=NEG)))
                if r == 0:
                    gmA = emit_gathers(off, 0, 8)
                elif r == 1:
                    gfA = emit_gathers(off, 8, 16)
            EI('ve', sv, lambda e: e.memset(mrun[:], NEG))
            EI('ve', sv, lambda e: e.memset(dnm[:], 0.0))
            EI('ve', sv, lambda e: e.memset(Aacc[:], 0.0))
            for cc in range(4):
                mmgroup((pa0, pa1)[cc % 2], np.s_[:, 0:TL],
                        [(wA["wdq"][:, k, cc * TL:(cc + 1) * TL], xnT[:, k, :])
                         for k in range(8)],
                        [('ve', sv, (lambda e, cc=cc, bk=(pa0, pa1)[cc % 2]: e.tensor_copy(
                            out=cqT[:, cc, :], in_=bk.t[:, 0:TL])))])
            W('te', sv)
            mmgroup(pc0, np.s_[:, 0:512],
                    [(cqT[:, c, :], wA["wuq"][:, c, :]) for c in range(4)],
                    [('ve', sv, lambda e: e.tensor_copy(out=qc[:], in_=pc0.t[:, 0:512]))])
            mmgroup(pc1, np.s_[:, 0:256],
                    [(cqT[:, c, :], wA["wqr"][:, c, :]) for c in range(4)],
                    [('ve', sv, lambda e: e.tensor_copy(out=qr[:], in_=pc1.t[:, 0:256]))])
            Wsp('sc', sv)
            EI('sc', sc, lambda e: e.activation(out=mu[:], in_=qr[:], func=AF.Abs))
            EI('sc', sc, lambda e: e.activation(out=mu[:], in_=mu[:], func=AF.Exp, scale=-1.0))
            EI('sc', sc, lambda e: e.activation(out=mu[:], in_=mu[:], func=AF.Ln, bias=1.0))
            EI('sc', sc, lambda e: e.activation(out=qr[:], in_=qr[:], func=AF.Relu))
            Wsp('ve', sc)
            EI('ve', sv, lambda e: e.tensor_tensor(out=mu[:], in0=mu[:], in1=qr[:], op=ALU.add))
            mu_v = mu[:].rearrange("p (h two i) -> p h two i", h=NH, two=2)
            ct_v = CT_s[:, T * 16:(T + 1) * 16].rearrange("p (o i) -> p o i", o=1).to_broadcast([TL, NH, 16])
            st_v = ST_s[:, T * 16:(T + 1) * 16].rearrange("p (o i) -> p o i", o=1).to_broadcast([TL, NH, 16])
            q1v = qt1[:].rearrange("p (h i) -> p h i", h=NH)
            q2v = qt2[:].rearrange("p (h i) -> p h i", h=NH)
            tmpv = rp[:, 0:NH * 16].rearrange("p (h i) -> p h i", h=NH)
            EI('ve', sv, lambda e, a=mu_v, b=ct_v, o=q1v: e.tensor_tensor(out=o, in0=a[:, :, 0, :], in1=b, op=ALU.mult))
            EI('ve', sv, lambda e, a=mu_v, b=st_v, o=tmpv: e.tensor_tensor(out=o, in0=a[:, :, 1, :], in1=b, op=ALU.mult))
            EI('ve', sv, lambda e, o=q1v, b=tmpv: e.tensor_tensor(out=o, in0=o, in1=b, op=ALU.subtract))
            EI('ve', sv, lambda e, a=mu_v, b=st_v, o=q2v: e.tensor_tensor(out=o, in0=a[:, :, 0, :], in1=b, op=ALU.mult))
            EI('ve', sv, lambda e, a=mu_v, b=ct_v, o=tmpv: e.tensor_tensor(out=o, in0=a[:, :, 1, :], in1=b, op=ALU.mult))
            EI('ve', sv, lambda e, o=q2v, b=tmpv: e.tensor_tensor(out=o, in0=o, in1=b, op=ALU.add))
            if T > 0:
                W('te', sv)
                for c in range(4):
                    transpose_to(qcT[:, c, :], qc[:, c * TL:(c + 1) * TL])
                ct2_v = CT2_s[:, T * 16:(T + 1) * 16].rearrange("p (o i) -> p o i", o=1).to_broadcast([TL, NH, 16])
                st2_v = ST2_s[:, T * 16:(T + 1) * 16].rearrange("p (o i) -> p o i", o=1).to_broadcast([TL, NH, 16])
                s1v = Ssc[:].rearrange("p (h i) -> p h i", h=NH)
                s2v = Ss2[:].rearrange("p (h i) -> p h i", h=NH)
                qb1 = qropeb[:, :, 64:80]
                qb2 = qropeb[:, :, 80:96]
                EI('ve', sv, lambda e, a=mu_v, b=ct2_v, o=s1v: e.tensor_tensor(out=o, in0=a[:, :, 0, :], in1=b, op=ALU.mult))
                EI('ve', sv, lambda e, a=mu_v, b=st2_v, o=s2v: e.tensor_tensor(out=o, in0=a[:, :, 1, :], in1=b, op=ALU.mult))
                EI('ve', sv, lambda e, o=qb1, x=s1v, y=s2v: e.tensor_tensor(out=o, in0=x, in1=y, op=ALU.subtract))
                EI('ve', sv, lambda e, a=mu_v, b=st2_v, o=s1v: e.tensor_tensor(out=o, in0=a[:, :, 0, :], in1=b, op=ALU.mult))
                EI('ve', sv, lambda e, a=mu_v, b=ct2_v, o=s2v: e.tensor_tensor(out=o, in0=a[:, :, 1, :], in1=b, op=ALU.mult))
                EI('ve', sv, lambda e, o=qb2, x=s1v, y=s2v: e.tensor_tensor(out=o, in0=x, in1=y, op=ALU.add))
                W('te', sv)
                for h in range(NH):
                    transpose_to(qropeT[64:96, h, :], qropeb[:, h, :], r0=64, r1=96)
            qc_v = qc[:].rearrange("p (h o dd) -> p h o dd", h=NH, o=1).to_broadcast([TL, NH, GSL, DH])
            gk_v = G[:, :, 0:512].rearrange("p k (h dd) -> p h k dd", h=NH)
            pr_v = prods[:].rearrange("p (h k dd) -> p h k dd", h=NH, k=GSL)
            nu1 = G[:, :, 1024:1040]
            nu2 = G[:, :, 1040:1056]
            k1v = kt1[:].rearrange("p (k i) -> p k i", i=16)
            k2v = kt2[:].rearrange("p (k i) -> p k i", i=16)
            ktv = ktt[:].rearrange("p (k i) -> p k i", i=16)
            q1b = qt1[:].rearrange("p (h o i) -> p h o i", h=NH, o=1).to_broadcast([TL, NH, GSL, 16])
            q2b = qt2[:].rearrange("p (h o i) -> p h o i", h=NH, o=1).to_broadcast([TL, NH, GSL, 16])
            k1b = kt1[:].rearrange("p (o k i) -> p o k i", o=1, i=16).to_broadcast([TL, NH, GSL, 16])
            k2b = kt2[:].rearrange("p (o k i) -> p o k i", o=1, i=16).to_broadcast([TL, NH, GSL, 16])
            rpv = rp[:].rearrange("p (h k i) -> p h k i", h=NH, k=GSL)
            rp2v = rp2[:].rearrange("p (h k i) -> p h k i", h=NH, k=GSL)
            Sv = Ssc[:].rearrange("p (h k) -> p h k", h=NH)
            S2v = Ss2[:].rearrange("p (h k) -> p h k", h=NH)
            ev = ebuf[:].rearrange("p (h k) -> p h k", h=NH)
            mb = mnew[:].rearrange("p (h o) -> p h o", o=1).to_broadcast([TL, NH, GSL])
            ab = alph[:].rearrange("p (h o) -> p h o", o=1).to_broadcast([TL, NH, DH])
            Av = Aacc[:].rearrange("p (h dd) -> p h dd", h=NH)
            eb2 = ebuf[:].rearrange("p (h o k) -> p h o k", h=NH, o=1).to_broadcast([TL, NH, DH, GSL])
            gv_v = G[:, :, 512:1024].rearrange("p k (h dd) -> p h dd k", h=NH)
            pr2 = prods[:].rearrange("p (h dd k) -> p h dd k", h=NH, dd=DH)
            avv = avp[:].rearrange("p (h dd) -> p h dd", h=NH)
            def emit_sweep(k0, gm, gf, pre):
                ckv_ = CK_s[:].rearrange("p (k i) -> p k i", i=16)[:, k0:k0 + GSL, :]
                skv_ = SK_s[:].rearrange("p (k i) -> p k i", i=16)[:, k0:k0 + GSL, :]
                sv_snap = sv.n
                W('gp', g, gf)
                W('gp', sv, sv_snap)
                EI('gp', sg, lambda e: e.memset(spc3[:, 0:8], 0.0))
                EI('gp', sg, lambda e, ckv_=ckv_: e.tensor_tensor(out=k1v, in0=nu1, in1=ckv_, op=ALU.mult))
                EI('gp', sg, lambda e, skv_=skv_: e.tensor_tensor(out=ktv, in0=nu2, in1=skv_, op=ALU.mult))
                EI('gp', sg, lambda e: e.tensor_tensor(out=k1v, in0=k1v, in1=ktv, op=ALU.subtract))
                EI('gp', sg, lambda e, skv_=skv_: e.tensor_tensor(out=k2v, in0=nu1, in1=skv_, op=ALU.mult))
                EI('gp', sg, lambda e, ckv_=ckv_: e.tensor_tensor(out=ktv, in0=nu2, in1=ckv_, op=ALU.mult))
                EI('gp', sg, lambda e: e.tensor_tensor(out=k2v, in0=k2v, in1=ktv, op=ALU.add))
                EI('gp', sg, lambda e: e.tensor_tensor(out=rpv, in0=q1b, in1=k1b, op=ALU.mult))
                EI('gp', sg, lambda e: e.tensor_tensor(out=rp2v, in0=q2b, in1=k2b, op=ALU.mult))
                EI('gp', sg, lambda e: e.tensor_tensor(out=rpv, in0=rpv, in1=rp2v, op=ALU.add))
                sg_snap = sg.n
                for ja, jb, gw in ((0, 8, gm), (8, GSL, gf)):
                    W('ve', g, gw)
                    gkh = G[:, ja:jb, 0:512].rearrange("p k (h dd) -> p h k dd", h=NH)
                    qch = qc[:].rearrange("p (h o dd) -> p h o dd", h=NH, o=1).to_broadcast([TL, NH, jb - ja, DH])
                    EI('ve', sv, (lambda e, prh=pr_v[:, :, ja:jb, :], qch=qch, gkh=gkh:
                                  e.tensor_tensor(out=prh, in0=qch, in1=gkh, op=ALU.mult)))
                    EI('ve', sv, (lambda e, prh=pr_v[:, :, ja:jb, :], svh=Sv[:, :, ja:jb]:
                                  e.tensor_reduce(out=svh, in_=prh, axis=AX.X, op=ALU.add)))
                W('ve', sg, sg_snap)
                EI('ve', sv, lambda e: e.memset(spc[:, 0:8], 0.0))
                EI('ve', sv, lambda e: e.tensor_reduce(out=S2v, in_=rpv, axis=AX.X, op=ALU.add))
                EI('ve', sv, lambda e: e.tensor_tensor(out=Ssc[:], in0=Ssc[:], in1=Ss2[:], op=ALU.add))
                EI('ve', sv, lambda e: e.tensor_reduce(out=rpart[:], in_=Sv, axis=AX.X, op=ALU.max))
                EI('ve', sv, lambda e: e.tensor_tensor(out=mnew[:], in0=mrun[:], in1=rpart[:], op=ALU.max))
                EI('ve', sv, lambda e: e.tensor_tensor(out=alph[:], in0=mrun[:], in1=mnew[:], op=ALU.subtract))
                Wsp('sc', sv)
                EI('sc', sc, lambda e: e.activation(out=alph[:], in_=alph[:], func=AF.Exp, scale=SCALE))
                EI('ve', sv, lambda e: e.tensor_tensor(out=Sv, in0=Sv, in1=mb, op=ALU.subtract))
                Wsp('sc', sv)
                EI('sc', sc, lambda e: e.activation(out=ebuf[:], in_=Ssc[:], func=AF.Exp, scale=SCALE))
                Wsp('ve', sc)
                EI('ve', sv, lambda e: e.tensor_reduce(out=rpart[:], in_=ev, axis=AX.X, op=ALU.add))
                EI('ve', sv, lambda e: e.tensor_tensor(out=dnm[:], in0=dnm[:], in1=alph[:], op=ALU.mult))
                EI('ve', sv, lambda e: e.tensor_tensor(out=dnm[:], in0=dnm[:], in1=rpart[:], op=ALU.add))
                EI('ve', sv, lambda e: e.tensor_copy(out=mrun[:], in_=mnew[:]))
                EI('ve', sv, lambda e: e.tensor_tensor(out=Av, in0=Av, in1=ab, op=ALU.mult))
                EI('ve', sv, lambda e: e.tensor_tensor(out=pr2, in0=eb2, in1=gv_v, op=ALU.mult))
                nxt = None
                if pre is not None:
                    nxt = (emit_gathers(pre, 0, 8), emit_gathers(pre, 8, 16))
                EI('ve', sv, lambda e: e.tensor_reduce(out=avv, in_=pr2, axis=AX.X, op=ALU.add))
                EI('ve', sv, lambda e: e.tensor_tensor(out=Aacc[:], in0=Aacc[:], in1=avp[:], op=ALU.add))
                return nxt

            if T > 0:
                for j in range(2):
                    for cc in range(4):
                        transpose_to(kcTb[:, cc, j * TL:(j + 1) * TL],
                                     kcb[:, j, cc * TL:(cc + 1) * TL])
                for j in range(2):
                    ctk = CT_s[:, (T - 1 + j) * 16:(T + j) * 16]
                    stk = ST_s[:, (T - 1 + j) * 16:(T + j) * 16]
                    m1 = kmub[:, j, 0:16]
                    m2 = kmub[:, j, 16:32]
                    if j == 1:
                        W('ve', stc)
                    EI('ve', sv, lambda e, m1=m1, c=ctk: e.tensor_tensor(out=Ssc[:, 0:16], in0=m1, in1=c, op=ALU.mult))
                    EI('ve', sv, lambda e, m2=m2, s=stk: e.tensor_tensor(out=Ss2[:, 0:16], in0=m2, in1=s, op=ALU.mult))
                    EI('ve', sv, lambda e: e.tensor_tensor(out=rotb[:, 64:80], in0=Ssc[:, 0:16], in1=Ss2[:, 0:16], op=ALU.subtract))
                    EI('ve', sv, lambda e, m1=m1, s=stk: e.tensor_tensor(out=Ssc[:, 0:16], in0=m1, in1=s, op=ALU.mult))
                    EI('ve', sv, lambda e, m2=m2, c=ctk: e.tensor_tensor(out=Ss2[:, 0:16], in0=m2, in1=c, op=ALU.mult))
                    EI('ve', sv, lambda e: e.tensor_tensor(out=rotb[:, 80:96], in0=Ssc[:, 0:16], in1=Ss2[:, 0:16], op=ALU.add))
                    W('te', sv)
                    transpose_to(krTt[64:96, j, :], rotb[:], r0=64, r1=96)
                W('te', sv)
                krf = krTt[64:96, :, :].rearrange("p j t -> p (j t)")
                nxtB = emit_sweep(64, gmA, gfA, 80)
                for h in range(NH):
                    bk = (pc0, pc1)[h // 4]
                    col = (h % 4) * 256
                    hb = 64 * (h % 2)
                    mms = [(qcT[hb:hb + 64, h // 2, :],
                            kcTb[hb:hb + 64, h // 2, :]),
                           (qropeT[64:96, h, :], krf)]
                    ev_ = []
                    if h % 4 == 3:
                        ev_ = [('ve', sv, (lambda e, bk=bk, q=h // 4: e.scalar_tensor_tensor(
                            out=bscore[:, q * 1024:(q + 1) * 1024], in0=bk.t[:, 0:1024],
                            scalar=1.0, in1=BANDM_s[:], op0=ALU.mult, op1=ALU.add)))]
                    mmgroup(bk, np.s_[:, col:col + 256], mms, ev_)
                bs8 = bscore[:].rearrange("p (h u) -> p h u", h=NH)
                mbn256 = mnew[:].rearrange("p (h o) -> p h o", o=1).to_broadcast([TL, NH, 256])
                EI('ve', sv, lambda e, bs8=bs8: e.tensor_reduce(out=rpart[:], in_=bs8, axis=AX.X, op=ALU.max))
                EI('ve', sv, lambda e: e.tensor_tensor(out=mnew[:], in0=mrun[:], in1=rpart[:], op=ALU.max))
                EI('ve', sv, lambda e: e.tensor_tensor(out=alph[:], in0=mrun[:], in1=mnew[:], op=ALU.subtract))
                Wsp('sc', sv)
                EI('sc', sc, lambda e: e.activation(out=alph[:], in_=alph[:], func=AF.Exp, scale=SCALE))
                EI('ve', sv, lambda e, bs8=bs8, mbn256=mbn256: e.tensor_tensor(out=bs8, in0=bs8, in1=mbn256, op=ALU.subtract))
                Wsp('sc', sv)
                EI('sc', sc, lambda e: e.activation(out=bscore[:], in_=bscore[:], func=AF.Exp, scale=SCALE))
                Wsp('ve', sc)
                EI('ve', sv, lambda e, bs8=bs8: e.tensor_reduce(out=rpart[:], in_=bs8, axis=AX.X, op=ALU.add))
                EI('ve', sv, lambda e: e.tensor_tensor(out=dnm[:], in0=dnm[:], in1=alph[:], op=ALU.mult))
                EI('ve', sv, lambda e: e.tensor_tensor(out=dnm[:], in0=dnm[:], in1=rpart[:], op=ALU.add))
                EI('ve', sv, lambda e: e.tensor_copy(out=mrun[:], in_=mnew[:]))
                EI('ve', sv, lambda e: e.tensor_tensor(out=Av, in0=Av, in1=ab, op=ALU.mult))
                EI('ve', sv, lambda e: e.tensor_copy(out=battn[:], in_=bscore[:]))
                W('te', sv)
                pa0.guard()
                for h in range(NH):
                    for c in range(2):
                        transpose_to(batT[:, c, :], battn[:, 256 * h + 128 * c:256 * h + 128 * c + 128])
                    W('te', sv)
                    E('te', (lambda e, h=h: e.matmul(
                        pa0.t[:, 64 * h:64 * h + 64], batT[:, 0, :],
                        VSb[:, 0, 64 * h:64 * h + 64], start=True, stop=False)))
                    EI('te', stc, (lambda e, h=h: e.matmul(
                        pa0.t[:, 64 * h:64 * h + 64], batT[:, 1, :],
                        VSb[:, 1, 64 * h:64 * h + 64], start=False, stop=True)))
                W('ve', stc)
                EI('ve', sv, lambda e: e.tensor_tensor(out=Aacc[:], in0=pa0.t[:, 0:512],
                                                       in1=Aacc[:], op=ALU.add))
                pa0.done(sv)
                emit_sweep(80, nxtB[0], nxtB[1], None)
            else:
                cur = (gmA, gfA)
                for s in range(6):
                    cur = emit_sweep(s * 16, cur[0], cur[1],
                                     (s + 1) * 16 if s < 5 else None)

            EI('ve', sv, lambda e: e.reciprocal(out=dnm[:], in_=dnm[:]))
            db = dnm[:].rearrange("p (h o) -> p h o", o=1).to_broadcast([TL, NH, DH])
            EI('ve', sv, lambda e: e.tensor_tensor(
                out=attnbf[:].rearrange("p (h dd) -> p h dd", h=NH), in0=Av, in1=db, op=ALU.mult))
            W('te', sv)
            for c in range(4):
                transpose_to(attnT[:, c, :], attnbf[:, c * TL:(c + 1) * TL])
            W('te', sv)
            for half in range(2):
                mmgroup(pc0, np.s_[:, half * 512:(half + 1) * 512],
                        [(attnT[:, c, :], wA["wout"][:, c, half * 512:(half + 1) * 512])
                         for c in range(4)], [])
            W('ve', stc)
            EI('ve', sv, lambda e: e.tensor_tensor(out=part_s[:], in0=pc0.t[:, 0:1024],
                                                   in1=x_s[:], op=ALU.add))
            pc0.done(sv)
            W('sy', sv)
            EI('sy', d, (lambda e, b=b, T0=T0: e.dma_start(
                out=part_d[b, T0:T0 + TL, :], in_=part_s[:])), 16)

    def ffn_phase(half):
        f0 = half * (DFF // 2)
        wiT3 = win["wiT"].rearrange("(k p) f -> k p f", p=TL)
        woT3 = win["woT"].rearrange("(k p) dd -> k p dd", p=TL)
        AT3 = win["AT"].rearrange("(k p) c -> k p c", p=TL)
        gwT3 = win["gwT"].rearrange("(k p) c -> k p c", p=TL)
        jobs = []
        for kc in range(8):
            jobs.append((wiT3[kc, :, f0:f0 + 2048], wiT_s[:, kc, :], TL, 2048))
        for kc in range(16):
            jobs.append((woT3[f0 // TL + kc, :, :], woT_s[:, kc, :], TL, 1024))
        for ee in range(NE):
            eb = 32 * (ee % 2)
            jobs.append((win["BT"][ee, :, f0:f0 + 2048],
                         BT_s[eb:eb + 16, ee // 2, :], 16, 2048, eb))
        jobs.append((win["bi"][:, f0:f0 + 2048], bi_s[0:1, 0:2048], 1, 2048))
        if half == 0:
            for kc in range(8):
                jobs.append((AT3[kc], AT_s[:, kc, :], TL, NE * 32))
            for kc in range(8):
                jobs.append((gwT3[kc], gwT_s[:, kc, :], TL, TL))
            jobs.append((win["bo"], bo_s[:], 1, D))
            jobs.append((win["gb"], gbr[:], 1, TL))
        bufs = (stg, stg2)
        ccnt, dcnt = [], []
        for i, job in enumerate(jobs):
            src, dst, p, c = job[:4]
            r0 = job[4] if len(job) > 4 else 0
            bf = bufs[i % 2]
            if i >= 2:
                W('sy', sv, ccnt[i - 2])
            EI('sy', d, (lambda e, src=src, bf=bf, p=p, c=c, r0=r0: e.dma_start(
                out=bf[r0:r0 + p, 0:c], in_=src)), 16)
            dcnt.append(d.n)
            W('ve', d, dcnt[i])
            EI('ve', sv, (lambda e, dst=dst, bf=bf, p=p, c=c, r0=r0: e.tensor_copy(
                out=dst, in_=bf[r0:r0 + p, 0:c])))
            ccnt.append(sv.n)
        W('sy', sv)
        W('te', sv)
        W('sc', sv)
        W('gp', sv)

        for b in range(NB):
            for T in range(NT):
                T0 = T * TL
                EI('sy', d, (lambda e, b=b, T=T: e.dma_start(
                    out=xnT_b[:].rearrange("p c t -> p (c t)"), in_=xnt_d[b, T])), 16)
                EI('sy', d, (lambda e, b=b, T0=T0: e.dma_start(
                    out=partb[:], in_=part_d[b, T0:T0 + TL, :])), 16)
                W('te', d)
                W('ve', d)
                W('sc', sv)
                W('gp', sv)
                ti = b * NT + T
                gts = gates_all[:, ti * NE:(ti + 1) * NE]
                if half == 0:
                    mms = [(xnT_b[:, k, :], gwT_s[:, k, :]) for k in range(8)]
                    mms.append((ones_r[:], gbr[:]))
                    mmgroup(pa0, np.s_[:, 0:TL], mms,
                            [('ve', sv, lambda e: e.tensor_copy(out=probs[:], in_=pa0.t[:, 0:NE]))])
                    EI('ve', sv, lambda e: e.tensor_reduce(out=c1[:], in_=probs[:], axis=AX.X,
                                                           op=ALU.max, negate=True))
                    W('ve', sv)
                    EI('ve', sv, lambda e: e.tensor_scalar(out=probs[:], in0=probs[:], scalar1=c1[:],
                                                           scalar2=None, op0=ALU.add))
                    Wsp('sc', sv)
                    EI('sc', sc, lambda e: e.activation(out=probs[:], in_=probs[:], func=AF.Exp))
                    Wsp('ve', sc)
                    EI('ve', sv, lambda e: e.tensor_reduce(out=c1[:], in_=probs[:], axis=AX.X, op=ALU.add))
                    W('ve', sv)
                    EI('ve', sv, lambda e: e.reciprocal(out=c1[:], in_=c1[:]))
                    W('ve', sv)
                    EI('ve', sv, lambda e: e.tensor_scalar(out=probs[:], in0=probs[:], scalar1=c1[:],
                                                           scalar2=None, op0=ALU.mult))
                    EI('ve', sv, lambda e: e.tensor_reduce(out=c1[:], in_=probs[:], axis=AX.X, op=ALU.max))
                    W('ve', sv)
                    EI('ve', sv, lambda e: e.tensor_scalar(out=dcol[:], in0=probs[:], scalar1=c1[:],
                                                           scalar2=None, op0=ALU.is_equal))
                    EI('ve', sv, lambda e: e.scalar_tensor_tensor(out=dcol[:], in0=dcol[:], scalar=-2.0,
                                                                  in1=probs[:], op0=ALU.mult, op1=ALU.add))
                    EI('ve', sv, lambda e: e.tensor_reduce(out=c2[:], in_=dcol[:], axis=AX.X, op=ALU.max))
                    W('ve', sv)
                    EI('ve', sv, lambda e: e.tensor_tensor(out=c3[:], in0=c1[:], in1=c2[:], op=ALU.add))
                    W('ve', sv)
                    EI('ve', sv, lambda e, gts=gts: e.tensor_scalar(out=gts, in0=probs[:], scalar1=c2[:],
                                                                    scalar2=c3[:], op0=ALU.is_ge, op1=ALU.mult))
                    EI('ve', sv, lambda e, ti=ti: e.tensor_scalar(out=c3_all[:, ti:ti + 1], in0=c3[:], scalar1=2.0,
                                                                  scalar2=None, op0=ALU.mult))
                else:
                    EI('ve', sv, lambda e: e.memset(gpk[:], 0.0))
                    W('ve', sv)
                    EI('ve', sv, lambda e, ti=ti: e.tensor_copy(out=gpk[:, 0:1], in_=c3_all[:, ti:ti + 1]))
                    W('te', sv)
                    ptr0.guard()
                    EI('te', stc, lambda e: e.transpose(ptr0.t[0:16, :], gpk[:], ident[:]))
                    W('ve', stc)
                    EI('ve', sv, lambda e: e.tensor_copy(out=gsr[:], in_=ptr0.t[0:1, 0:TL]))
                    ptr0.done(sv)
                for gpair in range(NE // 2):
                    mmgroup(pa1, np.s_[0:64, 0:TL],
                            [(AT_s[:, k, gpair * 64:(gpair + 1) * 64], xnT_b[:, k, :])
                             for k in range(8)],
                            [('ve', sv, (lambda e, gpair=gpair: e.tensor_copy(
                                out=la_s[0:16, gpair, :], in_=pa1.t[0:16, 0:TL]))),
                             ('ve', sv, (lambda e, gpair=gpair: e.tensor_copy(
                                out=la_s[32:48, gpair, :], in_=pa1.t[32:48, 0:TL])))])
                for grp in range(2):
                    n0 = grp * 1024
                    pb = (pc0, pc1)[grp]
                    for nn in range(2):
                        sub = [(xnT_b[:, k, :],
                                wiT_s[:, k, n0 + nn * 512:n0 + nn * 512 + 512])
                               for k in range(8)]
                        sub.append((ones_r[:],
                                    bi_s[0:1, n0 + nn * 512:n0 + nn * 512 + 512]))
                        ev_ = [] if nn == 0 else [
                            ('ve', sv, (lambda e, grp=grp, pb=pb: e.tensor_copy(
                                out=base_s[:, grp * 1024:(grp + 1) * 1024],
                                in_=pb.t[:, 0:1024])))]
                        mmgroup(pb, np.s_[:, nn * 512:(nn + 1) * 512], sub, ev_)
                    pb.done(sv)
                W('gp', sv)
                EI('gp', sg, lambda e: e.memset(Hacc[:], 0.0))
                hcnt = []
                lbufs = (lorab, sadd)
                i = 0
                for ee in range(NE):
                    for grp in range(2):
                        pb = (pc0, pc1)[grp]
                        lb = lbufs[i % 2]
                        if i >= 2:
                            W('ve', sg, hcnt[i - 2])
                        for nn in range(2):
                            n0 = grp * 1024 + nn * 512
                            ev_ = [] if nn == 0 else [
                                ('ve', sv, (lambda e, pb=pb, ee=ee, gts=gts, lb=lb: e.tensor_scalar(
                                    out=lb[:], in0=pb.t[:, 0:1024], scalar1=0.0,
                                    scalar2=gts[:, ee:ee + 1], op0=ALU.max, op1=ALU.mult)))]
                            eb = 32 * (ee % 2)
                            mmgroup(pb, np.s_[:, nn * 512:(nn + 1) * 512],
                                    [(la_s[eb:eb + 16, ee // 2, :],
                                      BT_s[eb:eb + 16, ee // 2, n0:n0 + 512]),
                                     (ident[:], base_s[:, n0:n0 + 512])],
                                    ev_)
                        pb.done(sv)
                        W('gp', sv)
                        EI('gp', sg, (lambda e, grp=grp, lb=lb: e.tensor_tensor(
                            out=Hacc[:, grp * 1024:(grp + 1) * 1024],
                            in0=Hacc[:, grp * 1024:(grp + 1) * 1024],
                            in1=lb[:], op=ALU.add)))
                        hcnt.append(sg.n)
                        i += 1
                W('ve', sg)
                EI('ve', sv, lambda e: e.memset(spc[:, 0:8], 0.0))
                EI('ve', sv, lambda e: e.tensor_copy(out=Hbf[:], in_=Hacc[:]))
                W('te', sv)
                for c in range(16):
                    transpose_to(HT[:, c, :], Hbf[:, c * TL:(c + 1) * TL])
                W('te', sv)
                for nn in range(2):
                    sub = [(HT[:, c, :], woT_s[:, c, nn * 512:(nn + 1) * 512])
                           for c in range(16)]
                    if half == 1:
                        sub.append((gsr[:], bo_s[0:1, nn * 512:(nn + 1) * 512]))
                    mmgroup(pc0, np.s_[:, nn * 512:(nn + 1) * 512], sub, [])
                W('ve', stc)
                EI('ve', sv, lambda e: e.tensor_tensor(out=outb[:], in0=pc0.t[:, 0:1024],
                                                       in1=partb[:], op=ALU.add))
                pc0.done(sv)
                W('sy', sv)
                dst = part_d if half == 0 else out_d
                EI('sy', d, (lambda e, b=b, T0=T0, dst=dst: e.dma_start(
                    out=dst[b, T0:T0 + TL, :], in_=outb[:])), 16)

    for b in range(NB):
        keys_phase(b)
        attn_phase(b)
    ffn_phase(0)
    ffn_phase(1)
    W('sy', d)
    W('sy', g)

    with nc.Block() as block:
        def mk(ek):
            def run(eng):
                for k, fn in prog:
                    if k == ek:
                        fn(eng)
            return run
        block.sync(mk('sy'))
        block.vector(mk('ve'))
        block.scalar(mk('sc'))
        block.tensor(mk('te'))
        block.gpsimd(mk('gp'))

    stack.close()
    return nc


def kernel(**inputs):
    rd = np.asarray(inputs["L0_attn_raw_delta"], dtype=np.float32)
    assert rd.shape == (16,)
    CT, ST, CT2, ST2, CK, SK = _host_consts()
    _MC = _mask_consts()
    x = np.ascontiguousarray(np.asarray(inputs["L0_attn_x"], dtype=np.float32))
    f32 = lambda a: np.ascontiguousarray(np.asarray(a, dtype=np.float32))
    base_map = {
        "wdkv": f32(inputs["L0_attn_W_dkv"]), "wuk": f32(inputs["L0_attn_W_uk"]),
        "wuv": f32(inputs["L0_attn_W_uv"]), "wdq": f32(inputs["L0_attn_W_dq"]),
        "wuq": f32(inputs["L0_attn_W_uq"]), "wqr": f32(inputs["L0_attn_W_qr"]),
        "wkr": f32(inputs["L0_attn_W_kr"]), "wout": f32(inputs["L0_attn_W_out"]),
        "iwq": f32(inputs["L0_attn_idx_wq"]), "iwk": f32(inputs["L0_attn_idx_wk"]),
        "iw": np.tile(np.asarray(inputs["L0_attn_idx_w"], dtype=np.float32)[None, :], (TL, 1)),
        "wiT": f32(np.asarray(inputs["L0_ffn_wi"], dtype=np.float32).T),
        "woT": f32(np.asarray(inputs["L0_ffn_wo"], dtype=np.float32).T),
        "gwT": f32(np.pad(np.asarray(inputs["L0_ffn_gate_w"], dtype=np.float32).T,
                      ((0, 0), (0, TL - NE)))),
        "AT": f32(np.pad(np.transpose(np.asarray(inputs["L0_ffn_lora_A"], dtype=np.float32),
                                  (2, 0, 1)), ((0, 0), (0, 0), (0, 32 - RLO))).reshape(D, NE * 32)),
        "BT": f32(np.transpose(np.asarray(inputs["L0_ffn_lora_B"], dtype=np.float32), (0, 2, 1))),
        "bi": f32(np.asarray(inputs["L0_ffn_bi"], dtype=np.float32).reshape(1, DFF)),
        "bo": f32(np.asarray(inputs["L0_ffn_bo"], dtype=np.float32).reshape(1, D)),
        "gb": f32(np.pad(np.asarray(inputs["L0_ffn_gate_b"], dtype=np.float32).reshape(1, NE),
                     ((0, 0), (0, TL - NE)))),
        "CT": CT, "ST": ST, "CT2": CT2, "ST2": ST2, "CK": CK, "SK": SK,
        "K128": _MC[0], "NEGM": _MC[1], "K0": _MC[2], "SURR0": _MC[3],
        "BANDM": _MC[4], "identw": _MC[5],
    }
    nc = build_nc()
    in_maps = []
    for c in range(8):
        m = dict(base_map)
        m["x"] = np.ascontiguousarray(x[2 * c:2 * c + 2])
        in_maps.append(m)
    res = run_bass_kernel_spmd(nc, in_maps, core_ids=list(range(8)))
    out = np.concatenate([np.asarray(r["out"]) for r in res.results], axis=0)
    return out.astype(np.float32)
